# version 27
# baseline (speedup 1.0000x reference)
import contextlib
import numpy as np
import ml_dtypes
import concourse.bass as bass
import concourse.mybir as mybir
from concourse.bass_utils import run_bass_kernel_spmd
from concourse.alu_op_type import AluOpType as A
from concourse.aot_env import allow_ici_all_to_all

F32 = mybir.dt.float32
BF16 = mybir.dt.bfloat16
FP8 = mybir.dt.float8e4
AF = mybir.ActivationFunctionType
AX = mybir.AxisListType

B, T, D, NH = 8, 2048, 1024, 64
NCH = 8
TH = T // 2
JW = 6144 // B

_DBG = False

_ENGINES = ("sync", "gpsimd", "vector", "scalar", "tensor")


class Sch:

    def __init__(self):
        self.steps = {e: [] for e in _ENGINES}
        self.cnt = {}

    def add(self, eng, emit, waits=(), sem=None, inc=1):
        if sem is None:
            sem = {"sync": "dI", "gpsimd": "sG", "vector": "sV",
                   "scalar": "sS", "tensor": "sT"}[eng]
        v = self.cnt.get(sem, 0) + inc
        self.cnt[sem] = v
        self.steps[eng].append((emit, tuple(waits), sem, inc))
        return (sem, v)

    def dma(self, eng, out, in_, waits=(), sem=None):
        if sem is None:
            self._dmai = getattr(self, "_dmai", 0) + 1
            sem = "dma%d" % self._dmai
        return self.add(eng, lambda e, o=out, i=in_: e.dma_start(out=o, in_=i),
                        waits=waits, sem=sem, inc=16)

    def emit(self, block, sems):
        own = {"vector": "sV", "scalar": "sS", "gpsimd": "sG"}
        for eng in _ENGINES:
            if not self.steps[eng]:
                continue

            def section(e, eng=eng):
                seen = {}
                mysem = own.get(eng)
                cnt = 0
                for emitfn, waits, semkey, inc in self.steps[eng]:
                    if mysem is not None and semkey == mysem and cnt > 0:
                        waits = tuple(waits) + ((mysem, cnt),)
                    for sk, val in waits:
                        if seen.get(sk, 0) < val:
                            e.wait_ge(sems[sk], val)
                            seen[sk] = val
                    inst = emitfn(e)
                    if inst is not None:
                        inst.then_inc(sems[semkey], inc)
                        if semkey == mysem:
                            cnt += inc
            getattr(block, eng)(section)


def build_program(dbg=False):
    nc = bass.Bass("TRN2", target_bir_lowering=False, detect_race_conditions=False)

    def din(name, shape, dt=F32):
        return nc.dram_tensor(name, list(shape), dt, kind="ExternalInput")

    t_xT = din("xT", (D, T))
    t_wallT = din("wallT", (128, 2048))
    t_bvec2 = din("bvec2", (128, 2))
    t_wpT = din("wpT", (NH, D))
    t_bprow = din("bprow", (1, D))
    t_wfT = din("wfT", (D, D))
    t_vcat = din("vcat", (128, 24))
    t_cmat = din("cmat", (B, D))
    t_wmod = din("wmod_sl", (D, JW))
    t_eye = din("eye", (128, 128))
    t_eyebf = din("eyebf", (128, 128), BF16)
    t_onesbf = din("onesbf", (1, T), BF16)
    t_out = nc.dram_tensor("outT", [D, T], F32, kind="ExternalOutput")

    b_md_i = nc.dram_tensor("b_md_i", [B, JW], F32)
    b_md_o = nc.dram_tensor("b_md_o", [B, JW], F32)
    b_am_i = nc.dram_tensor("b_am_i", [1, 1], F32)
    b_row = nc.dram_tensor("b_row", [1, 2048], F32)
    b_row2 = nc.dram_tensor("b_row2", [1, 2048], F32)
    b_sc = nc.dram_tensor("b_sc", [1, 8], F32)
    b_am_o = nc.dram_tensor("b_am_o", [B, 1], F32)

    dbg_outs = {}
    if dbg:
        for nm, shp, dt in [("d_modT", [128, 48], F32), ("d_statT", [128, 96], F32),
                            ("d_coefT", [128, 48], BF16), ("d_gt0", [128, T], F32),
                            ("d_gt1", [128, T], F32),
                            ("d_x1", [128, T], F32), ("d_h2", [128, T], F32),
                            ("d_scal", [1, 16], F32), ("d_kv", [128, 8], F32),
                            ("d_sg", [128, 16], F32)]:
            dbg_outs[nm] = nc.dram_tensor(nm, shp, dt, kind="ExternalOutput")
        for nm, shp, dt in [("d_q8u", [128, 2048], mybir.dt.uint8),
                            ("d_wq8u", [128, 1024], mybir.dt.uint8)]:
            dbg_outs[nm] = nc.dram_tensor(nm, shp, dt, kind="ExternalOutput")

    es = contextlib.ExitStack()
    sbt = lambda name, shape, dt=F32: es.enter_context(nc.sbuf_tensor(name, list(shape), dt))
    pst = lambda name, shape, dt=F32: es.enter_context(nc.psum_tensor(name, list(shape), dt))

    r1 = sbt("r1", (128, 16384))
    r2 = sbt("r2", (128, 16384))
    r2b = sbt("r2b", (128, 2048))
    saA = es.enter_context(nc.sbuf_tensor("saA", [64, 6144], F32))
    ar2 = sbt("ar2", (128, 2048))

    wallT = [r2[:, 2048 + c * 256: 2048 + (c + 1) * 256] for c in range(NCH)]
    wl = [r2[:, 4096 + c * 256: 4096 + (c + 1) * 256] for c in range(NCH)]
    w3bf = [r2[:, 6144 + c * 128: 6144 + (c + 1) * 128].bitcast(BF16) for c in range(NCH)]
    gt0 = r2[:, 7168:9216]
    sqbfP = [r2[:, 9472:10496].bitcast(BF16), r2[:, 10496:11520].bitcast(BF16)]
    cmat_v = r2[0:B, 12800:13824]
    silu_v = r2[0:B, 14336:15360]
    silu_ch = lambda c: r2[0:B, 14336 + c * 128:14336 + (c + 1) * 128]
    wpT_v = r2[0:NH, 11776:12800]
    modsb8 = r2[0:B, 15360:15360 + JW]
    xbfP = [r2[:, 13312:14336].bitcast(BF16), r2[:, 14336:15360].bitcast(BF16)]
    x1 = [r2[:, c * 2048:(c + 1) * 2048] for c in range(NCH)]

    xch = [r1[:, c * 2048:(c + 1) * 2048] for c in range(NCH)]
    q8 = [r1[:, c * 2048: c * 2048 + 512].bitcast(FP8) for c in range(NCH)]

    gt1 = r2b[:, 0:2048]
    sqbf2P = [r2b[:, 0:1024].bitcast(BF16), r2b[:, 1024:2048].bitcast(BF16)]
    xbf2P = [ar2[:, 0:1024].bitcast(BF16), ar2[:, 1024:2048].bitcast(BF16)]

    sl_li, sl_lf, sl_z, sl_o, sl_m, sl_mp = (saA[0:64, k * 1024:(k + 1) * 1024]
                                             for k in range(6))
    statrowsA = saA[0:4, 0:2048]
    statrowsB = saA[0:2, 2048:4096]
    statrows2a = saA[0:1, 0:2048]
    statrows2b = saA[0:1, 2048:4096]
    arow = saA[0:1, 4096:6144]
    r2row = saA[0:1, 4096:6144]

    h65 = sbt("h65", (65, T), BF16)
    lhsT65 = sbt("lhsT65", (65, D), BF16)
    rhs9 = sbt("rhs9", (4, T), BF16)
    lhsT9 = sbt("lhsT9", (4, 256), BF16)
    k8 = sbt("k8", (8, 128), BF16)
    kvbf = sbt("kvbf", (128, 8), BF16)
    sT32 = sbt("sT32", (128, 64))
    wstatA = sbt("wstatA", (128, 32), BF16)
    wstatB = sbt("wstatB", (128, 16), BF16)
    wq8 = sbt("wq8", (128, 8192), FP8)
    e1P = sbt("e1P", (128, 2048), BF16)
    wfS = sbt("wfS", (128, 1024))
    onecbf = sbt("onecbf", (128, 1), BF16)

    vcat = sbt("vcat_s", (128, 24))
    bvec2 = sbt("bvec2_s", (128, 2))
    modT = sbt("modT", (128, 48))
    modsb48 = sbt("modsb48", (48, 128))
    deriv = sbt("deriv", (128, 64))
    Vs = sbt("Vs", (128, 40))
    statT = sbt("statT", (128, 96))
    statT2 = sbt("statT2", (128, 32))
    coefT = sbt("coefT", (128, 48), BF16)
    coefT2 = sbt("coefT2", (128, 16))
    q8dir = sbt("q8dir", (128, 1024))
    coefTa = sbt("coefTa", (128, 16))
    rm = sbt("rm", (128, 256))
    sg_ = sbt("sg_", (128, 16))
    kvf = sbt("kvf", (128, 8))
    kvr = sbt("kvr", (128, 24))
    sc5 = sbt("sc5", (5, 1))
    amx8 = sbt("amx8", (128, 8))
    amxc = sbt("amxc", (128, 1))
    prow = sbt("prow", (1, 128))
    s11 = sbt("s11", (1, 16))
    ag8 = sbt("ag8", (8, 1))
    ones_r = sbt("ones_r", (1, 128))
    ones_c = sbt("ones_c", (128, 1))
    eye = sbt("eye_s", (128, 128))
    eyebf = sbt("eyebf_s", (128, 128), BF16)

    pb0 = pst("pb0", (128, 2048))
    pb1 = pst("pb1", (128, 2048))
    pb1bf = pb1[:, 0:1024].bitcast(BF16)
    pb1bf2 = pb1[:, 1024:2048].bitcast(BF16)

    S = Sch()
    rmv = lambda k: rm[:, k * 16:(k + 1) * 16]
    MCAR, CCAR, MLAST, B2S, WAMA, WAMB, WACC = 208, 209, 210, 211, 213, 214, 215
    SGC = {"sca": 0, "sc2": 1, "spsa": 2, "ssan": 3, "ssa2": 4, "sP1": 5, "sP12": 6,
           "xs2": 7, "deq": 8, "sw2": 9}
    sgc = lambda k: sg_[:, SGC[k]:SGC[k] + 1]

    d_cmat = S.dma("sync", cmat_v, t_cmat.ap()[:, :])
    d_eye = S.dma("sync", eye[:, :], t_eye.ap()[:, :])
    d_wpT = S.dma("sync", wpT_v, t_wpT.ap()[:, :])
    d_bprow = S.dma("sync", saA[0:1, 4096:5120], t_bprow.ap()[:, :])
    wmpark = [r1[:, 10240 + c * 768: 10240 + (c + 1) * 768] for c in range(NCH)]
    d_wmpark = [S.dma("scalar", wmpark[c], t_wmod.ap()[c * 128:(c + 1) * 128, :])
                for c in range(NCH)]
    d_eyebf = S.dma("sync", eyebf[:, :], t_eyebf.ap()[:, :])
    d_vcat = S.dma("sync", vcat[:, :], t_vcat.ap()[:, :])
    d_bvec = S.dma("sync", bvec2[:, :], t_bvec2.ap()[:, :])
    d_x = [S.dma("scalar", xch[c], t_xT.ap()[c * 128:(c + 1) * 128, :]) for c in range(5)]
    d_wall1 = S.dma("scalar", r2[:, 2048:4096], t_wallT.ap()[:, :])
    d_wall = [d_wall1] * NCH

    t_ones = S.add("vector", lambda e: e.memset(ones_r[:, :], 1.0))
    t_ones2 = S.add("vector", lambda e: e.memset(ones_c[:, :], 1.0))
    t_ones3 = S.add("vector", lambda e: e.memset(onecbf[:, :], 1.0))
    t_z11 = S.add("vector", lambda e: e.memset(s11[:, :], 0.0))
    t_zsg = S.add("vector", lambda e: e.memset(sg_[:, :], 0.0))
    t_silu = S.add("scalar", lambda e: e.activation(out=silu_v, in_=cmat_v, func=AF.Silu),
                   waits=[d_cmat])
    tok_tr = [S.add("tensor", lambda e, c=c: e.transpose(
        pb0[:, 1024 + c * 8:1024 + (c + 1) * 8], silu_ch(c), eye[0:B, 0:B]),
        waits=[t_silu, d_eye]) for c in range(NCH)]
    t_sT = S.add("vector", lambda e: e.tensor_copy(out=sT32[:, :], in_=pb0[:, 1024:1088]),
                 waits=tok_tr)
    mmod = {}
    tok_mod = []
    for c in range(NCH):
        mmod[c] = [S.add("tensor", lambda e, c=c, n0=n0, n1=n1: e.matmul(
            pb0[0:B, n0:n1], sT32[:, c * 8:(c + 1) * 8], wmpark[c][:, n0:n1],
            start=(c == 0), stop=(c == NCH - 1)),
            waits=[d_wmpark[c], t_sT]) for (n0, n1) in ((0, 512), (512, JW))]
        tok_mod = tok_mod + mmod[c]
    d_x.append(S.dma("scalar", xch[5], t_xT.ap()[5 * 128:6 * 128, :],
                     waits=mmod[0] + mmod[1] + mmod[2]))
    d_x.append(S.dma("scalar", xch[6], t_xT.ap()[6 * 128:7 * 128, :],
                     waits=mmod[2] + mmod[3] + mmod[4] + mmod[5]))
    d_x.append(S.dma("scalar", xch[7], t_xT.ap()[7 * 128:8 * 128, :],
                     waits=mmod[5] + mmod[6] + mmod[7]))
    t_modev = S.add("vector", lambda e: e.tensor_copy(out=modsb8, in_=pb0[0:B, 0:JW]),
                    waits=tok_mod)
    t_l65a = S.add("vector", lambda e: e.tensor_copy(out=lhsT65[0:NH, :], in_=wpT_v),
                   waits=[d_wpT])
    t_l65b = S.add("vector", lambda e: e.tensor_copy(out=lhsT65[64:65, :],
                                                     in_=saA[0:1, 4096:5120]), waits=[d_bprow])
    d_mdi = S.dma("sync", b_md_i.ap()[:, :], modsb8, waits=[t_modev])

    def _cc_mod(e):
        with allow_ici_all_to_all():
            return e.collective_compute(
                "AllToAll", A.bypass, replica_groups=[list(range(B))],
                ins=[b_md_i.ap().opt()], outs=[b_md_o.ap().opt()])
    cc_mod = S.add("gpsimd", _cc_mod, waits=[d_mdi], sem="cc", inc=1)
    d_mdo = S.dma("sync", modsb48[:, :],
                  b_md_o.ap().rearrange("r (s e) -> (r s) e", e=128), waits=[cc_mod])
    t_modtr = S.add("tensor", lambda e: e.transpose(
        pb1[:, 0:48], modsb48[:, :], eye[0:48, 0:48]), waits=[d_mdo, d_eye])
    t_modT = S.add("vector", lambda e: e.tensor_copy(out=modT[:, :], in_=pb1[:, 0:48]),
                   waits=[t_modtr])

    t_p1 = S.add("vector", lambda e: e.tensor_scalar(
        out=deriv[:, 0:8], in0=modT[:, 8:16], scalar1=1.0, scalar2=None, op0=A.add),
        waits=[t_modT])
    t_p2 = S.add("vector", lambda e: e.tensor_scalar(
        out=deriv[:, 8:16], in0=modT[:, 32:40], scalar1=1.0, scalar2=None, op0=A.add),
        waits=[t_modT])

    def _gab(dst, src):
        t2 = S.add("vector", lambda e: e.tensor_scalar(
            out=deriv[:, dst:dst + 8], in0=modT[:, src:src + 8], scalar1=0.0, scalar2=2.0,
            op0=A.is_ge, op1=A.mult), waits=[t_modT])
        t3 = S.add("vector", lambda e: e.tensor_scalar(
            out=deriv[:, dst:dst + 8], in0=deriv[:, dst:dst + 8], scalar1=-1.0, scalar2=None,
            op0=A.add), waits=[t2])
        t4 = S.add("vector", lambda e: e.tensor_tensor(
            out=deriv[:, 56:64], in0=deriv[:, dst:dst + 8], in1=modT[:, src:src + 8],
            op=A.mult), waits=[t3])
        t5 = S.add("vector", lambda e: e.tensor_scalar(
            out=deriv[:, 56:64], in0=deriv[:, 56:64], scalar1=0.25, scalar2=None,
            op0=A.max), waits=[t4])
        return S.add("vector", lambda e: e.tensor_tensor(
            out=deriv[:, dst:dst + 8], in0=deriv[:, dst:dst + 8], in1=deriv[:, 56:64],
            op=A.mult), waits=[t5])
    t_ga = _gab(16, 16)
    t_gb = _gab(24, 40)
    t_psa = S.add("vector", lambda e: e.tensor_tensor(
        out=deriv[:, 32:40], in0=deriv[:, 0:8], in1=modT[:, 0:8], op=A.mult),
        waits=[t_p1, t_ga])
    t_c2a = S.add("vector", lambda e: e.tensor_scalar(
        out=deriv[:, 40:48], in0=modT[:, 8:16], scalar1=2.0, scalar2=None, op0=A.add),
        waits=[t_modT])
    t_c2 = S.add("vector", lambda e: e.tensor_tensor(
        out=deriv[:, 40:48], in0=deriv[:, 40:48], in1=modT[:, 8:16], op=A.mult),
        waits=[t_c2a])
    t_sa2 = S.add("vector", lambda e: e.tensor_tensor(
        out=deriv[:, 48:56], in0=modT[:, 0:8], in1=modT[:, 0:8], op=A.mult),
        waits=[t_modT])

    wsA_j = wstatA.ap().rearrange("p (c j) -> p j c", j=4)
    wsB_j = wstatB.ap().rearrange("p (c j) -> p j c", j=2)
    t_ws = [S.add("vector", lambda e: e.memset(wstatA[:, :], 1.0)),
            S.add("vector", lambda e: e.memset(wstatB[:, :], 1.0))]
    t_ws.append(S.add("vector", lambda e: e.tensor_copy(out=wsA_j[:, 1, :], in_=modT[:, 8:16]),
                      waits=[t_modT]))
    t_ws.append(S.add("vector", lambda e: e.tensor_copy(out=wsA_j[:, 2, :], in_=deriv[:, 40:48]),
                      waits=[t_c2]))
    t_ws.append(S.add("vector", lambda e: e.tensor_copy(out=wsA_j[:, 3, :], in_=deriv[:, 32:40]),
                      waits=[t_psa]))
    t_ws.append(S.add("vector", lambda e: e.tensor_copy(out=wsB_j[:, 1, :], in_=deriv[:, 40:48]),
                      waits=[t_c2]))
    vs_j = Vs.ap().rearrange("p (c j) -> p j c", j=5)
    t_vs = [S.add("vector", lambda e, j=j, srcv=srcv: e.tensor_copy(out=vs_j[:, j, :], in_=srcv),
                  waits=[wtok])
            for j, (srcv, wtok) in enumerate([(modT[:, 8:16], t_modT), (deriv[:, 40:48], t_c2),
                                              (deriv[:, 32:40], t_psa), (modT[:, 0:8], t_modT),
                                              (deriv[:, 48:56], t_sa2)])]

    xbufs = [xbfP[0], xbfP[1],
             r2[:, 7168:8192].bitcast(BF16), r2[:, 8192:9216].bitcast(BF16),
             r2b[:, 0:1024].bitcast(BF16), r2b[:, 1024:2048].bitcast(BF16),
             r2[:, 0:1024].bitcast(BF16), r2[:, 1024:2048].bitcast(BF16)]
    sqbufs = [sqbfP[0], sqbfP[1],
              ar2[:, 0:1024].bitcast(BF16), ar2[:, 1024:2048].bitcast(BF16),
              e1P[:, :]]
    t_xbf, t_sq = {}, {}
    prevB5 = {}
    for c in range(NCH):
        t_xbf[c] = S.add("vector", lambda e, c=c: e.tensor_copy(
            out=xbufs[c], in_=xch[c]), waits=[d_x[c]] + (tok_tr if c < 2 else []))
    for c in range(5):
        t_sq[c] = S.add("scalar", lambda e, c=c: e.activation(
            out=sqbufs[c % 5], in_=xch[c], func=AF.Square), waits=[d_x[c]])

    t_wl = [S.add("vector", lambda e, c=c: e.tensor_scalar(
        out=wl[c], in0=wallT[c], scalar1=vcat[:, c:c + 1], scalar2=None, op0=A.mult),
        waits=[d_wall[c], d_vcat]) for c in range(NCH)]
    t_w3 = [S.add("vector", lambda e, c=c: e.tensor_scalar(
        out=w3bf[c], in0=wl[c], scalar1=deriv[:, c:c + 1], scalar2=None, op0=A.mult),
        waits=[t_wl[c], t_p1]) for c in range(NCH)]

    kvr_j = kvr.ap().rearrange("p (c j) -> p j c", j=3)
    t_kvr = [S.add("vector", lambda e: e.memset(kvr[:, :], 1.0))]
    t_kvr.append(S.add("vector", lambda e: e.tensor_copy(out=kvr_j[:, 0, :], in_=modT[:, 0:8]),
                       waits=[t_modT]))
    t_kvr.append(S.add("vector", lambda e: e.tensor_copy(out=kvr_j[:, 1, :], in_=deriv[:, 0:8]),
                       waits=[t_p1]))

    tok_A, tok_B = [], []
    for c in range(NCH):
        if c >= 5:
            t_sq[c] = S.add("scalar", lambda e, c=c: e.activation(
                out=sqbufs[c % 5], in_=xch[c], func=AF.Square),
                waits=[d_x[c]] + prevB5[c - 5])
        mmA = [S.add("tensor", lambda e, c=c, n=n: e.matmul(
            pb0[0:4, n * 512:(n + 1) * 512], wstatA[:, c * 4:(c + 1) * 4],
            xbufs[c][:, n * 512:(n + 1) * 512], start=(c == 0), stop=(c == NCH - 1)),
            waits=[t_xbf[c]] + t_ws) for n in range(4)]
        mmB = [S.add("tensor", lambda e, c=c, n=n: e.matmul(
            pb1[0:2, n * 512:(n + 1) * 512], wstatB[:, c * 2:(c + 1) * 2],
            sqbufs[c % 5][:, n * 512:(n + 1) * 512], start=(c == 0), stop=(c == NCH - 1)),
            waits=[t_sq[c]] + t_ws) for n in range(4)]
        prevB5[c] = mmB
        tok_A += mmA
        tok_B += mmB
    t_srA = S.add("vector", lambda e: e.tensor_copy(out=statrowsA, in_=pb0[0:4, :]),
                  waits=tok_A)
    t_srB = S.add("vector", lambda e: e.tensor_copy(out=statrowsB, in_=pb1[0:2, :]),
                  waits=tok_B)
    tok_kv = []
    for mt in range(2):
        for c in range(NCH):
            tok_kv.append(S.add("tensor", lambda e, c=c, mt=mt: e.matmul(
                pb0[:, mt * 4:mt * 4 + 3], wl[c][:, mt * 128:(mt + 1) * 128],
                kvr[:, c * 3:(c + 1) * 3], start=(c == 0), stop=(c == NCH - 1)),
                waits=[t_wl[c], t_srA] + t_kvr))
        for c in range(NCH):
            tok_kv.append(S.add("tensor", lambda e, c=c, mt=mt: e.matmul(
                pb0[:, mt * 4 + 3:mt * 4 + 4], wallT[c][:, mt * 128:(mt + 1) * 128],
                vcat[:, 8 + c:9 + c], start=(c == 0), stop=(c == NCH - 1)),
                waits=[d_wall[c], d_vcat, t_srA]))
    t_kvf = S.add("vector", lambda e: e.tensor_copy(out=kvf[:, :], in_=pb0[:, 0:8]),
                  waits=tok_kv)
    t_kc = S.add("vector", lambda e: e.tensor_tensor(
        out=kvf[:, 3:4], in0=kvf[:, 3:4], in1=bvec2[:, 0:1], op=A.add),
        waits=[t_kvf, d_bvec])
    t_kc2 = S.add("vector", lambda e: e.tensor_tensor(
        out=kvf[:, 7:8], in0=kvf[:, 7:8], in1=bvec2[:, 1:2], op=A.add),
        waits=[t_kvf, d_bvec])
    t_kvb = S.add("vector", lambda e: e.tensor_copy(
        out=kvbf.ap().rearrange("p (v b) -> p v b", b=2), in_=kvf.ap().rearrange("p (b v) -> p v b", b=2)),
        waits=[t_kc, t_kc2])
    t_kvt = S.add("tensor", lambda e: e.transpose(pb1bf[0:8, 0:128], kvbf[:, :], eyebf[:, :]),
                  waits=[t_kvb, d_eyebf, t_srB])
    t_k8 = S.add("vector", lambda e: e.tensor_copy(out=k8[:, :], in_=pb1bf[0:8, 0:128]),
                 waits=[t_kvt])
    d_l9 = S.dma("sync", lhsT9.ap().rearrange("p (b c) -> p b c", b=2), k8[:, :],
                 waits=[t_k8])
    tok_ks = [S.add("tensor", lambda e, c=c: e.matmul(
        pb0[0:5, 8:9], Vs[:, c * 5:(c + 1) * 5], ones_c[:, :],
        start=(c == 0), stop=(c == NCH - 1)), waits=t_vs + [t_ones2, t_srA] + tok_kv[-1:])
        for c in range(NCH)]
    t_sc5 = S.add("vector", lambda e: e.tensor_copy(out=sc5[:, :], in_=pb0[0:5, 8:9]),
                  waits=tok_ks)
    d_sc1 = S.dma("sync", b_sc.ap()[0:1, 0:5], sc5[:, :], waits=[t_sc5])
    t_sg = S.dma("sync", sg_[:, 0:5], b_sc.ap()[0:1, 0:5].partition_broadcast(128),
                 waits=[d_sc1, t_zsg])
    t_sP1 = S.add("vector", lambda e: e.tensor_scalar(
        out=sgc("sP1"), in0=sgc("sca"), scalar1=1024.0, scalar2=None, op0=A.add),
        waits=[t_sg])
    t_sP12 = S.add("vector", lambda e: e.tensor_scalar(
        out=sgc("sP12"), in0=sgc("sc2"), scalar1=1024.0, scalar2=None, op0=A.add),
        waits=[t_sg])
    t_ssan = S.add("vector", lambda e: e.tensor_scalar(
        out=sgc("ssan"), in0=sgc("ssan"), scalar1=1.0 / 1024.0, scalar2=None, op0=A.mult),
        waits=[t_sg])

    tok_st = []
    for tt in range(16):
        tok_st.append(S.add("tensor", lambda e, tt=tt: e.transpose(
            pb1[:, 512 + tt * 6:512 + tt * 6 + 4],
            statrowsA[:, tt * 128:(tt + 1) * 128], eye[0:4, 0:4]),
            waits=[t_srA, t_srB, t_sc5]))
        tok_st.append(S.add("tensor", lambda e, tt=tt: e.transpose(
            pb1[:, 512 + tt * 6 + 4:512 + tt * 6 + 6],
            statrowsB[:, tt * 128:(tt + 1) * 128], eye[0:2, 0:2]),
            waits=[t_srA, t_srB, t_sc5]))
    t_stT = S.add("vector", lambda e: e.tensor_copy(out=statT[:, :], in_=pb1[:, 512:608]),
                  waits=tok_st)


    st_ap = statT.ap().rearrange("p (tt j) -> p j tt", j=6)
    V_S0, V_Sca, V_Sc2, V_Spsa, V_Q0, V_Qc2 = (st_ap[:, j, :] for j in range(6))

    rmt = []

    def vop(emit, waits=()):
        tok = S.add("vector", emit, waits=list(waits) + [t_stT])
        rmt.append(tok)
        return tok
    vop(lambda e: e.tensor_scalar(out=rmv(0), in0=V_S0, scalar1=1.0 / 1024, scalar2=None, op0=A.mult))
    vop(lambda e: e.tensor_tensor(out=rmv(2), in0=rmv(0), in1=rmv(0), op=A.mult))
    vop(lambda e: e.scalar_tensor_tensor(out=rmv(3), in0=V_Q0, scalar=1.0 / 1024,
                                         in1=rmv(2), op0=A.mult, op1=A.subtract))
    vop(lambda e: e.tensor_scalar(out=rmv(3), in0=rmv(3), scalar1=1e-5, scalar2=None, op0=A.add))
    vop(lambda e: e.reciprocal(out=rmv(5), in_=rmv(3)))
    t_rx = S.add("scalar", lambda e: e.activation(out=rmv(6), in_=rmv(5), func=AF.Sqrt),
                 waits=rmt[-1:])
    vop(lambda e: e.tensor_tensor(out=rmv(7), in0=V_S0, in1=V_Sca, op=A.add))
    vop(lambda e: e.scalar_tensor_tensor(out=rmv(8), in0=rmv(0), scalar=sgc("sP1"),
                                         in1=rmv(7), op0=A.mult, op1=A.subtract),
        waits=[t_sP1])
    vop(lambda e: e.tensor_tensor(out=rmv(9), in0=rmv(8), in1=rmv(6), op=A.mult),
        waits=[t_rx])
    vop(lambda e: e.tensor_scalar(out=rmv(7), in0=rmv(9), scalar1=-1.0 / 1024,
                                  scalar2=sgc("ssan"), op0=A.mult, op1=A.add),
        waits=[t_ssan])
    vop(lambda e: e.tensor_tensor(out=rmv(8), in0=V_S0, in1=V_Sc2, op=A.add))
    vop(lambda e: e.tensor_tensor(out=rmv(9), in0=V_Q0, in1=V_Qc2, op=A.add))
    vop(lambda e: e.tensor_scalar(out=rmv(10), in0=rmv(2), scalar1=sgc("sP12"),
                                  scalar2=None, op0=A.mult), waits=[t_sP12])
    vop(lambda e: e.tensor_tensor(out=rmv(11), in0=rmv(0), in1=rmv(8), op=A.mult))
    vop(lambda e: e.scalar_tensor_tensor(out=rmv(9), in0=rmv(11), scalar=-2.0,
                                         in1=rmv(9), op0=A.mult, op1=A.add))
    vop(lambda e: e.tensor_tensor(out=rmv(9), in0=rmv(9), in1=rmv(10), op=A.add))
    vop(lambda e: e.tensor_tensor(out=rmv(9), in0=rmv(9), in1=rmv(5), op=A.mult))
    vop(lambda e: e.scalar_tensor_tensor(out=rmv(10), in0=rmv(0), scalar=sgc("spsa"),
                                         in1=V_Spsa, op0=A.mult, op1=A.subtract),
        waits=[t_sg])
    vop(lambda e: e.tensor_tensor(out=rmv(10), in0=rmv(10), in1=rmv(6), op=A.mult),
        waits=[t_rx])
    vop(lambda e: e.scalar_tensor_tensor(out=rmv(9), in0=rmv(10), scalar=-2.0,
                                         in1=rmv(9), op0=A.mult, op1=A.add))
    vop(lambda e: e.tensor_scalar(out=rmv(9), in0=rmv(9), scalar1=sgc("ssa2"),
                                  scalar2=1.0 / 1024, op0=A.add, op1=A.mult), waits=[t_sg])
    vop(lambda e: e.tensor_tensor(out=rmv(10), in0=rmv(7), in1=rmv(7), op=A.mult))
    vop(lambda e: e.tensor_tensor(out=rmv(9), in0=rmv(9), in1=rmv(10), op=A.subtract))
    vop(lambda e: e.tensor_scalar(out=rmv(9), in0=rmv(9), scalar1=1e-5, scalar2=None, op0=A.add))
    vop(lambda e: e.reciprocal(out=rmv(11), in_=rmv(9)))
    t_rh = S.add("scalar", lambda e: e.activation(out=rmv(10), in_=rmv(11), func=AF.Sqrt),
                 waits=rmt[-1:])
    vop(lambda e: e.tensor_tensor(out=rmv(11), in0=rmv(6), in1=rmv(10), op=A.mult),
        waits=[t_rh])
    vop(lambda e: e.tensor_tensor(out=rmv(12), in0=rmv(11), in1=rmv(0), op=A.mult))
    vop(lambda e: e.tensor_tensor(out=rmv(4), in0=rmv(10), in1=rmv(7), op=A.mult))
    co_ap = coefT.ap().rearrange("p (tt j) -> p j tt", j=3)
    t_co = [S.add("vector", lambda e, j=j, slab=slab, sg=sg: e.tensor_scalar(
        out=co_ap[:, j, :], in0=rmv(slab), scalar1=sg, scalar2=None, op0=A.mult),
        waits=rmt[-6:] + [t_rh]) for j, slab, sg in [(0, 10, 1.0), (1, 12, -1.0), (2, 4, -1.0)]]
    t_coa = S.add("vector", lambda e: e.tensor_copy(
        out=coefTa[:, :], in_=rmv(11)), waits=rmt[-6:] + [t_rh])
    tok_ct = [S.add("tensor", lambda e, tt=tt: e.transpose(
        pb1bf[0:3, tt * 128:(tt + 1) * 128], coefT[:, tt * 3:(tt + 1) * 3], eyebf[:, :]),
        waits=t_co + [t_k8, t_stT]) for tt in range(16)]
    t_r9a = S.add("vector", lambda e: e.tensor_copy(out=rhs9[0:3, :], in_=pb1bf[0:3, 0:2048]),
                  waits=tok_ct)
    t_r9b = S.dma("sync", rhs9[3:4, :], t_onesbf.ap()[:, :])
    tok_cta = [S.add("tensor", lambda e, tt=tt: e.transpose(
        pb1[0:1, tt * 128:(tt + 1) * 128], coefTa[:, tt:tt + 1], eye[:, :]),
        waits=[t_coa, t_r9a]) for tt in range(16)]
    t_arow = S.add("vector", lambda e: e.tensor_copy(out=arow, in_=pb1[0:1, 0:2048]),
                   waits=tok_cta + [t_l65b])
    d_ar1 = S.dma("sync", b_row.ap()[:, :], arow, waits=[t_arow])
    t_asb = S.dma("sync", ar2[:, :], b_row.ap().partition_broadcast(128),
                  waits=[d_ar1] + tok_B[-1:])


    tok_G = []
    first_w = [t_kvf, t_sc5, t_srA, t_srB, t_stT, t_k8, t_r9a, t_arow]
    for c in range(NCH + 1):
        for mt in range(2):
            pg = pb0 if mt == 0 else pb1
            for n in range(4):
                if c < NCH:
                    tok_G.append(S.add("tensor", lambda e, c=c, mt=mt, n=n, pg=pg: e.matmul(
                        pg[:, n * 512:(n + 1) * 512], w3bf[c][:, mt * 128:(mt + 1) * 128],
                        xbufs[c][:, n * 512:(n + 1) * 512],
                        start=(c == 0), stop=False),
                        waits=[t_xbf[c], t_w3[c]] + (first_w if c == 0 else [])))
                else:
                    tok_G.append(S.add("tensor", lambda e, mt=mt, n=n, pg=pg: e.matmul(
                        pg[:, n * 512:(n + 1) * 512], lhsT9[:, mt * 128:(mt + 1) * 128],
                        rhs9[:, n * 512:(n + 1) * 512], start=False, stop=True),
                        waits=[d_l9, t_r9b]))
    t_gt0 = [S.add("vector", lambda e, n=n: e.tensor_tensor(
        out=gt0[:, n * 512:(n + 1) * 512], in0=pb0[:, n * 512:(n + 1) * 512],
        in1=ar2[:, n * 512:(n + 1) * 512], op=A.mult), waits=tok_G + [t_asb])
        for n in range(4)]
    t_gt1 = [S.add("vector", lambda e, n=n: e.tensor_tensor(
        out=gt1[:, n * 512:(n + 1) * 512], in0=pb1[:, n * 512:(n + 1) * 512],
        in1=ar2[:, n * 512:(n + 1) * 512], op=A.mult), waits=tok_G + [t_asb])
        for n in range(4)]
    t_clip = S.add("vector", lambda e: e.tensor_scalar(
        out=gt0[:, :], in0=gt0[:, :], scalar1=20.0, scalar2=-20.0, op0=A.min, op1=A.max),
        waits=t_gt0)

    wq8f32 = wq8.bitcast(F32)
    P1R = [wq8f32[:, 0:1024], wq8f32[:, 1024:2048], wfS[:, 0:1024]]
    e1Pf32 = e1P.bitcast(F32)
    P2R = [e1Pf32[:, 0:1024], wfS[:, 0:1024]]
    wred_prev = []
    wacc_tok = [None]
    d_wf1 = []

    def emit_wf1_dma(c):
        w = [] if c < 3 else [wred_prev[c - 3]]
        d_wf1.append(S.dma("scalar", P1R[c % 3], t_wfT.ap()[c * 128:(c + 1) * 128, :],
                           waits=w))

    def emit_wred(c):
        tk = S.add("vector", lambda e, c=c: e.tensor_reduce(
            out=rm[:, 208 + (c % 4):208 + (c % 4) + 1], in_=P1R[c % 3], axis=AX.X,
            op=A.max, apply_absolute_value=True), waits=[d_wf1[c]])
        if c == 0:
            wa = S.add("vector", lambda e: e.tensor_copy(
                out=rm[:, WACC:WACC + 1], in_=rm[:, 208:209]), waits=[tk])
        else:
            wa = S.add("vector", lambda e, c=c: e.tensor_tensor(
                out=rm[:, WACC:WACC + 1], in0=rm[:, WACC:WACC + 1],
                in1=rm[:, 208 + (c % 4):208 + (c % 4) + 1], op=A.max),
                waits=[tk, wacc_tok[0]])
        wacc_tok[0] = wa
        wred_prev.append(tk)
    for c in range(3):
        emit_wf1_dma(c)

    t_h1 = S.add("vector", lambda e: e.memset(h65[64:65, :], 1.0))
    s_lf = saA[0:64, 0:2048]
    s_i = saA[0:64, 2048:4096]
    s_m = saA[0:64, 4096:6144]
    li_v = gt0[0:64, :]
    z_v = gt1[0:64, :]
    o_v = ar2[0:64, :]
    d_lf = S.dma("sync", s_lf, gt0[64:128, :], waits=[t_clip])
    d_oo = S.dma("scalar", o_v, gt1[64:128, :], waits=t_gt1)
    t_zt = S.add("scalar", lambda e: e.activation(out=z_v, in_=z_v, func=AF.Tanh),
                 waits=t_gt1)
    t_ot = S.add("scalar", lambda e: e.activation(out=o_v, in_=o_v, func=AF.Tanh,
                                                  scale=0.5), waits=[d_oo])
    t_ms = S.add("vector", lambda e: e.tensor_tensor_scan(
        out=s_m, data0=s_lf, data1=li_v, initial=0.0, op0=A.add, op1=A.max),
        waits=[d_lf, t_clip])
    t_i1 = S.add("vector", lambda e: e.tensor_tensor(
        out=s_i, in0=li_v, in1=s_m, op=A.subtract), waits=[t_ms])
    t_d1 = S.add("vector", lambda e: e.tensor_tensor(
        out=s_lf, in0=s_lf, in1=s_m, op=A.subtract), waits=[t_ms])
    t_f2 = S.add("vector", lambda e: e.tensor_tensor(
        out=s_lf[:, 1:T], in0=s_lf[:, 1:T], in1=s_m[:, 0:T - 1], op=A.add),
        waits=[t_d1])
    t_oa = S.add("vector", lambda e: e.tensor_scalar(
        out=o_v, in0=o_v, scalar1=0.5, scalar2=0.5, op0=A.mult, op1=A.add),
        waits=[t_ot])
    t_ex = S.add("scalar", lambda e: e.activation(
        out=saA[0:64, 0:4096], in_=saA[0:64, 0:4096], func=AF.Exp),
        waits=[t_f2, t_i1])
    t_u = S.add("vector", lambda e: e.tensor_tensor(
        out=s_i, in0=s_i, in1=z_v, op=A.mult), waits=[t_ex, t_zt])
    t_cs = S.add("vector", lambda e: e.tensor_tensor_scan(
        out=s_m, data0=s_lf, data1=s_i, initial=0.0, op0=A.mult, op1=A.add),
        waits=[t_u])
    t_tc = S.add("scalar", lambda e: e.activation(out=s_m, in_=s_m, func=AF.Tanh),
                 waits=[t_cs])
    t_h65 = [S.add("vector", lambda e: e.tensor_tensor(
        out=h65[0:64, :], in0=o_v, in1=s_m, op=A.mult), waits=[t_tc, t_oa])]

    tok_res1 = []
    prev_ev = {}
    for m in range(NCH):
        pg = pb0 if m % 2 == 0 else pb1
        psum_free = [prev_ev[m - 2]] if m >= 2 else (t_gt1 + [t_clip])
        mms = [S.add("tensor", lambda e, m=m, n=n, pg=pg: e.matmul(
            pg[:, n * 512:(n + 1) * 512], lhsT65[:, m * 128:(m + 1) * 128],
            h65[:, n * 512:(n + 1) * 512], start=True, stop=True),
            waits=t_h65 + [t_h1, t_l65a, t_l65b] + psum_free) for n in range(4)]
        region_dead = [t_kvt] if m < 2 else []
        ev = S.add("vector", lambda e, m=m, pg=pg: e.scalar_tensor_tensor(
            out=x1[m], in0=pg[:, :], scalar=deriv[:, 16 + m:17 + m], in1=xch[m],
            op0=A.mult, op1=A.add), waits=mms + [t_ga] + region_dead)
        prev_ev[m] = ev
        tok_res1.append(ev)


    t_x1bf, t_x1sq = {}, {}
    tok_S1, tok_Q1 = [], []
    prevA, prevB = [], []
    prev2A, prev2B = [], []
    for c in range(NCH):
        t_x1bf[c] = S.add("vector", lambda e, c=c: e.tensor_copy(
            out=xbf2P[c % 2], in_=x1[c]), waits=[tok_res1[c], t_asb] + prev2A)
        t_x1sq[c] = S.add("scalar", lambda e, c=c: e.activation(
            out=sqbf2P[c % 2], in_=x1[c], func=AF.Square),
            waits=[tok_res1[c]] + prev2B)
        mmA = [S.add("tensor", lambda e, c=c, n=n: e.matmul(
            pb0[0:1, n * 512:(n + 1) * 512], onecbf[:, :],
            xbf2P[c % 2][:, n * 512:(n + 1) * 512], start=(c == 0), stop=(c == NCH - 1)),
            waits=[t_x1bf[c], t_ones3] + ([tok_res1[6]] if c == 0 else []))
            for n in range(4)]
        mmB = [S.add("tensor", lambda e, c=c, n=n: e.matmul(
            pb1[0:1, n * 512:(n + 1) * 512], onecbf[:, :],
            sqbf2P[c % 2][:, n * 512:(n + 1) * 512], start=(c == 0), stop=(c == NCH - 1)),
            waits=[t_x1sq[c], t_ones3] + ([tok_res1[7]] if c == 0 else []))
            for n in range(4)]
        prev2A, prev2B = prevA, prevB
        prevA, prevB = mmA, mmB
        tok_S1 += mmA
        tok_Q1 += mmB
    t_sr2a = S.add("vector", lambda e: e.tensor_copy(out=statrows2a, in_=pb0[0:1, :]),
                   waits=tok_S1)
    t_sr2b = S.add("vector", lambda e: e.tensor_copy(out=statrows2b, in_=pb1[0:1, :]),
                   waits=tok_Q1)
    tok_st2 = []
    for tt in range(16):
        tok_st2.append(S.add("tensor", lambda e, tt=tt: e.transpose(
            pb0[:, 512 + tt * 2:512 + tt * 2 + 1],
            statrows2a[:, tt * 128:(tt + 1) * 128], eye[0:1, 0:1]),
            waits=[t_sr2a, t_sr2b]))
        tok_st2.append(S.add("tensor", lambda e, tt=tt: e.transpose(
            pb0[:, 512 + tt * 2 + 1:512 + tt * 2 + 2],
            statrows2b[:, tt * 128:(tt + 1) * 128], eye[0:1, 0:1]),
            waits=[t_sr2a, t_sr2b]))
    t_stT2 = S.add("vector", lambda e: e.tensor_copy(out=statT2[:, :], in_=pb0[:, 512:544]),
                   waits=tok_st2)
    st2_ap = statT2.ap().rearrange("p (tt j) -> p j tt", j=2)
    rm2 = []

    def vop2(emit, waits=()):
        tok = S.add("vector", emit, waits=list(waits) + [t_stT2])
        rm2.append(tok)
        return tok
    vop2(lambda e: e.tensor_scalar(out=rmv(0), in0=st2_ap[:, 0, :], scalar1=1.0 / 1024,
                                   scalar2=None, op0=A.mult))
    vop2(lambda e: e.tensor_scalar(out=rmv(1), in0=st2_ap[:, 1, :], scalar1=1.0 / 1024,
                                   scalar2=None, op0=A.mult))
    vop2(lambda e: e.tensor_tensor(out=rmv(2), in0=rmv(0), in1=rmv(0), op=A.mult))
    vop2(lambda e: e.tensor_tensor(out=rmv(3), in0=rmv(1), in1=rmv(2), op=A.subtract))
    vop2(lambda e: e.tensor_scalar(out=rmv(3), in0=rmv(3), scalar1=1e-5, scalar2=None, op0=A.add))
    vop2(lambda e: e.reciprocal(out=rmv(4), in_=rmv(3)))
    t_r2v = S.add("scalar", lambda e: e.activation(out=rmv(5), in_=rmv(4), func=AF.Sqrt),
                  waits=rm2[-1:])
    t_m2c = vop2(lambda e: e.tensor_tensor(out=rmv(6), in0=rmv(0), in1=rmv(5), op=A.mult),
                 waits=[t_r2v])
    t_c2w = [S.add("vector", lambda e: e.tensor_copy(out=coefT2[:, :], in_=rmv(5)),
                   waits=rm2[-1:] + [t_r2v])]
    tok_c2t = [S.add("tensor", lambda e, tt=tt: e.transpose(
        pb1[0:1, tt * 128:(tt + 1) * 128], coefT2[:, tt:tt + 1], eye[:, :]),
        waits=t_c2w[:1] + [t_sr2b] + tok_Q1) for tt in range(16)]
    t_r2r = S.add("vector", lambda e: e.tensor_copy(out=r2row, in_=pb1[0:1, 0:2048]),
                  waits=tok_c2t)
    d_r2a = S.dma("sync", b_row2.ap()[:, :], r2row, waits=[t_r2r])
    t_r2sb = S.dma("sync", ar2[:, :], b_row2.ap().partition_broadcast(128),
                   waits=[d_r2a, t_asb] + tok_S1)
    m2r2b = r2b[:, 0:2048]
    tok_m2t = [S.add("tensor", lambda e, tt=tt: e.transpose(
        pb1[0:1, tt * 128:(tt + 1) * 128], rm[:, 96 + tt:97 + tt], eye[:, :]),
        waits=[t_m2c, t_r2r]) for tt in range(16)]
    t_m2r = S.add("vector", lambda e: e.tensor_copy(out=statrows2a,
                                                    in_=pb1[0:1, 0:2048]),
                  waits=tok_m2t)
    d_m2a = S.dma("sync", b_row.ap()[:, :], statrows2a, waits=[t_m2r])
    t_m2sb = S.dma("sync", m2r2b, b_row.ap().partition_broadcast(128),
                   waits=[d_m2a] + tok_Q1)

    tok_amx = []
    t_h2 = {}
    t_pre = {}
    for c in range(NCH):
        t_t1 = S.add("vector" if c % 2 == 0 else "gpsimd", lambda e, c=c: e.tensor_tensor(
            out=xch[c], in0=x1[c], in1=ar2[:, :], op=A.mult),
            waits=[t_r2sb, tok_res1[c], t_x1bf[c], t_x1sq[c]])
        t_pre[c] = S.add("gpsimd" if c % 2 == 0 else "vector", lambda e, c=c: e.tensor_tensor(
            out=xch[c], in0=xch[c], in1=m2r2b, op=A.subtract),
            waits=[t_t1, t_m2sb])
        t_h2[c] = S.add("scalar", lambda e, c=c: e.activation(
            out=xch[c], in_=xch[c], func=AF.Identity,
            scale=deriv[:, 8 + c:9 + c], bias=modT[:, 24 + c:25 + c]),
            waits=[t_pre[c], t_p2, t_modT])
        tok_amx.append(S.add("vector", lambda e, c=c: e.tensor_reduce(
            out=amx8[:, c:c + 1], in_=xch[c], axis=AX.X, op=A.max,
            apply_absolute_value=True), waits=[t_h2[c]]))
        emit_wred(c)
        if c + 3 < NCH:
            emit_wf1_dma(c + 3)
    t_wt = S.add("tensor", lambda e: e.transpose(
        pb1[0:1, 1280:1408], rm[:, WACC:WACC + 1], eye[:, :]),
        waits=[wacc_tok[0], t_m2r])
    t_wr = S.add("vector", lambda e: e.tensor_copy(out=prow[:, :], in_=pb1[0:1, 1280:1408]),
                 waits=[t_wt])
    t_wl_ = S.add("vector", lambda e: e.tensor_reduce(
        out=s11[0:1, 6:7], in_=prow[:, :], axis=AX.X, op=A.max), waits=[t_wr])
    t_ws1 = S.add("vector", lambda e: e.tensor_scalar(
        out=s11[0:1, 6:7], in0=s11[0:1, 6:7], scalar1=1e-12, scalar2=None, op0=A.max),
        waits=[t_wl_])
    t_ws2 = S.add("vector", lambda e: e.reciprocal(out=s11[0:1, 7:8], in_=s11[0:1, 6:7]),
                  waits=[t_ws1])
    t_ws3 = S.add("vector", lambda e: e.tensor_scalar(
        out=s11[0:1, 7:8], in0=s11[0:1, 7:8], scalar1=448.0, scalar2=448.0,
        op0=A.mult, op1=A.min), waits=[t_ws2])
    t_ws4 = S.add("vector", lambda e: e.reciprocal(out=s11[0:1, 8:9], in_=s11[0:1, 7:8]),
                  waits=[t_ws3])
    t_ws5 = S.add("vector", lambda e: e.tensor_scalar(
        out=s11[0:1, 9:10], in0=s11[0:1, 7:8], scalar1=0.5, scalar2=None, op0=A.mult),
        waits=[t_ws3])
    t_sw2b_mm = S.add("tensor", lambda e: e.matmul(
        pb1[:, 1408:1409], ones_r[:, :], s11[0:1, 9:10], start=True, stop=True),
        waits=[t_ws5, t_ones, t_wt])
    t_sw2b = S.add("vector", lambda e: e.tensor_copy(out=sgc("sw2"), in_=pb1[:, 1408:1409]),
                   waits=[t_sw2b_mm])
    t_amc = S.add("vector", lambda e: e.tensor_reduce(
        out=amxc[:, :], in_=amx8[:, :], axis=AX.X, op=A.max), waits=tok_amx)
    t_amt = S.add("tensor", lambda e: e.transpose(pb0[0:1, 0:128], amxc[:, :], eye[:, :]),
                  waits=[t_amc, t_sr2a])
    t_amr = S.add("vector", lambda e: e.tensor_copy(out=prow[:, :], in_=pb0[0:1, 0:128]),
                  waits=[t_amt])
    t_aml = S.add("vector", lambda e: e.tensor_reduce(
        out=s11[0:1, 0:1], in_=prow[:, :], axis=AX.X, op=A.max), waits=[t_amr])
    d_ami = S.dma("sync", b_am_i.ap()[:, :], s11[0:1, 0:1], waits=[t_aml, t_z11])
    tok_wq = []
    prev_q = []
    for c in range(NCH):
        bw = ([wred_prev[5]] if c == 1 else
              (prevB5[4] if c == 0 else []))
        d = S.dma("scalar", P2R[c % 2], t_wfT.ap()[c * 128:(c + 1) * 128, :],
                  waits=bw + (prev_q[c - 2:c - 1] if c >= 2 else []))
        tq = S.add("vector", lambda e, c=c: e.tensor_scalar(
            out=wq8[:, c * 1024:(c + 1) * 1024], in0=P2R[c % 2], scalar1=sgc("sw2"),
            scalar2=None, op0=A.mult), waits=[d, t_sw2b])
        prev_q.append(tq)
        tok_wq.append(tq)
    cc_am = S.add("gpsimd", lambda e: e.collective_compute(
        "AllGather", A.bypass, replica_groups=[list(range(B))],
        ins=[b_am_i.ap().opt()], outs=[b_am_o.ap().opt()]),
        waits=[d_ami, cc_mod], sem="cc", inc=1)
    d_amo = S.dma("sync", ag8[:, :], b_am_o.ap()[:, :], waits=[cc_am])
    t_agt = S.add("tensor", lambda e: e.transpose(
        pb0[0:1, 128:136], ag8[:, :], eye[0:8, 0:8]), waits=[d_amo, t_amt])
    t_agr = S.add("vector", lambda e: e.tensor_copy(
        out=prow[0:1, 0:8], in_=pb0[0:1, 128:136]), waits=[t_agt])
    t_amg = S.add("vector", lambda e: e.tensor_reduce(
        out=s11[0:1, 1:2], in_=prow[0:1, 0:8], axis=AX.X, op=A.max), waits=[t_agr])

    t_sc1 = S.add("vector", lambda e: e.tensor_scalar(
        out=s11[0:1, 1:2], in0=s11[0:1, 1:2], scalar1=1e-12, scalar2=None, op0=A.max),
        waits=[t_amg])
    t_sc2_ = S.add("vector", lambda e: e.reciprocal(out=s11[0:1, 2:3], in_=s11[0:1, 1:2]),
                   waits=[t_sc1])
    t_sc3 = S.add("vector", lambda e: e.tensor_scalar(
        out=s11[0:1, 2:3], in0=s11[0:1, 2:3], scalar1=448.0, scalar2=448.0,
        op0=A.mult, op1=A.min), waits=[t_sc2_])
    t_sc4 = S.add("vector", lambda e: e.reciprocal(out=s11[0:1, 3:4], in_=s11[0:1, 2:3]),
                  waits=[t_sc3])
    t_sc5_ = S.add("vector", lambda e: e.tensor_scalar(
        out=s11[0:1, 4:5], in0=s11[0:1, 2:3], scalar1=0.5, scalar2=None, op0=A.mult),
        waits=[t_sc3])
    t_dq1 = S.add("vector", lambda e: e.tensor_tensor(
        out=s11[0:1, 10:11], in0=s11[0:1, 3:4], in1=s11[0:1, 8:9], op=A.mult),
        waits=[t_sc4, t_ws4])
    t_dq2 = S.add("vector", lambda e: e.tensor_scalar(
        out=s11[0:1, 5:6], in0=s11[0:1, 10:11], scalar1=4.0, scalar2=None, op0=A.mult),
        waits=[t_dq1])
    t_xsb_mm = S.add("tensor", lambda e: e.matmul(
        pb0[:, 140:141], ones_r[:, :], s11[0:1, 4:5], start=True, stop=True),
        waits=[t_sc5_, t_agt])
    t_dqb_mm = S.add("tensor", lambda e: e.matmul(
        pb0[:, 141:142], ones_r[:, :], s11[0:1, 5:6], start=True, stop=True),
        waits=[t_dq2, t_agt])
    t_xsb = S.add("vector", lambda e: e.tensor_copy(out=sgc("xs2"), in_=pb0[:, 140:141]),
                  waits=[t_xsb_mm])
    t_dqb = S.add("vector", lambda e: e.tensor_copy(out=sgc("deq"), in_=pb0[:, 141:142]),
                  waits=[t_dqb_mm])

    r2bq = r2b.bitcast(FP8)
    wfSq = wfS.bitcast(FP8)
    q8dirq = q8dir.bitcast(FP8)
    qdst = ([r2bq[:, k * 2048:(k + 1) * 2048] for k in range(4)]
            + [wfSq[:, k * 2048:(k + 1) * 2048] for k in range(2)]
            + [q8dirq[:, k * 2048:(k + 1) * 2048] for k in range(2)])
    tok_q8 = []
    pre_all = [t_pre[c] for c in range(NCH)]
    for c in range(NCH):
        if c < 4:
            extra = pre_all
        elif c < 6:
            extra = [tok_wq[7]]
        else:
            extra = []
        tok_q8.append(S.add("vector", lambda e, c=c: e.tensor_scalar(
            out=qdst[c], in0=xch[c], scalar1=sgc("xs2"),
            scalar2=None, op0=A.mult),
            waits=[t_h2[c], t_xsb, tok_amx[c]] + extra))

    wq8_3d = wq8.ap().rearrange("p (k m) -> p k m", k=NCH)
    q8_3d = r1.ap().bitcast(FP8).rearrange("p (c x) -> p c x", c=NCH)
    r2b_3d = r2b.bitcast(FP8).ap().rearrange("p (c x) -> p c x", c=4)
    wfS_3d = wfS.bitcast(FP8).ap().rearrange("p (c x) -> p c x", c=2)
    q8d_3d = q8dir.bitcast(FP8).ap().rearrange("p (c x) -> p c x", c=2)
    rhs_pairs = [r2b_3d[:, 0:2, :], r2b_3d[:, 2:4, :],
                 wfS_3d[:, 0:2, :], q8d_3d[:, 0:2, :]]
    fb_col = lambda m: vcat[:, 16 + m:17 + m]
    prev_e1 = {}
    prev_x2 = {}
    tok_out = []
    for m in range(NCH):
        pg = pb0 if m % 2 == 0 else pb1
        mm_w0 = ([prev_e1[m - 2]] if m >= 2
                 else [t_xsb, t_dqb, t_agt] + ([t_wt, t_sw2b] if m == 1 else []))
        mms = []
        for n in range(4):
            for kp in range(NCH // 2):
                mms.append(S.add("tensor", lambda e, m=m, n=n, kp=kp, pg=pg: e.matmul(
                    pg[:, n * 512:(n + 1) * 512],
                    wq8_3d[:, 2 * kp:2 * kp + 2, m * 128:(m + 1) * 128],
                    rhs_pairs[kp][:, :, n * 512:(n + 1) * 512],
                    start=(kp == 0), stop=(kp == NCH // 2 - 1),
                    perf_mode=mybir.MatmulPerfMode.DoubleRow),
                    waits=[tok_wq[2 * kp], tok_wq[2 * kp + 1], tok_q8[2 * kp],
                           tok_q8[2 * kp + 1]] + (mm_w0 if (kp == 0 and n == 0) else [])))
        t_e1 = S.add("scalar", lambda e, m=m, pg=pg: e.activation(
            out=e1P[:, :], in_=pg[:, :], func=AF.Identity,
            scale=sgc("deq"), bias=fb_col(m)),
            waits=mms + [t_dqb] + ([prev_x2[m - 1]] if m >= 1 else []))
        prev_e1[m] = t_e1
        t_x2 = S.add("vector", lambda e, m=m: e.scalar_tensor_tensor(
            out=x1[m], in0=e1P[:, :], scalar=deriv[:, 24 + m:25 + m],
            in1=x1[m], op0=A.mult, op1=A.add),
            waits=[t_e1, t_gb, tok_q8[m]])
        prev_x2[m] = t_x2
        tok_out.append(S.dma("sync", t_out.ap()[m * 128:(m + 1) * 128, :], x1[m],
                             waits=[t_x2]))

    if dbg:
        tok_out.append(S.dma("sync", dbg_outs["d_modT"].ap()[:, :], modT[:, :], waits=[t_modT]))
        tok_out.append(S.dma("sync", dbg_outs["d_statT"].ap()[:, :], statT[:, :], waits=[t_stT]))
        tok_out.append(S.dma("sync", dbg_outs["d_coefT"].ap()[:, :], coefT[:, :], waits=t_co))
        tok_out.append(S.dma("sync", dbg_outs["d_kv"].ap()[:, :], kvf[:, :], waits=[t_kc]))
        tok_out.append(S.dma("sync", dbg_outs["d_sg"].ap()[:, :], sg_[:, :], waits=[t_sP1, t_sP12, t_ssan, t_xsb, t_dqb, t_sw2b]))
        tok_out.append(S.dma("sync", dbg_outs["d_q8u"].ap()[:, :], q8[0].bitcast(mybir.dt.uint8),
                             waits=[tok_q8[0]] + [prev_x2[m] for m in range(NCH)]))
        tok_out.append(S.dma("sync", dbg_outs["d_wq8u"].ap()[:, :], wq8[:, 0:1024].bitcast(mybir.dt.uint8),
                             waits=[tok_wq[0]]))
        tok_out.append(S.dma("sync", dbg_outs["d_gt0"].ap()[:, :], gt0, waits=[t_clip]))
        tok_out.append(S.dma("sync", dbg_outs["d_gt1"].ap()[:, :], gt1, waits=t_gt1))
        tok_out.append(S.dma("sync", dbg_outs["d_x1"].ap()[:, :], x1[0], waits=[tok_res1[0]]))
        tok_out.append(S.dma("sync", dbg_outs["d_h2"].ap()[:, :], xch[0], waits=[t_h2[0]]))
        tok_out.append(S.dma("sync", dbg_outs["d_scal"].ap()[:, :], s11[:, :],
                             waits=[t_dq2, t_sc5_, t_ws5]))

    S.add("sync", lambda e: e.nop(), waits=tok_out, sem="dI", inc=1)

    with es:
        sems = {nm: es.enter_context(nc.semaphore(nm))
                for nm in sorted(set(list(S.cnt.keys()) + ["dI"]))}
        block = es.enter_context(nc.Block())
        S.emit(block, sems)
    return nc


def kernel(L0_attn_x, L0_attn_ln_w, L0_attn_ln_b, L0_attn_Wi, L0_attn_bi,
           L0_attn_Wf, L0_attn_bf, L0_attn_Wz, L0_attn_bz, L0_attn_Wo,
           L0_attn_bo, L0_attn_Wp, L0_attn_bp, L0_ffn_weight, L0_ffn_bias,
           L0_c, L0_wmod):
    f = lambda a: np.ascontiguousarray(np.asarray(a, dtype=np.float32))
    x = f(L0_attn_x)
    wall = np.concatenate([f(L0_attn_Wi), f(L0_attn_Wf), f(L0_attn_Wz), f(L0_attn_Wo)], 0)
    wallT = np.ascontiguousarray(
        wall.T.reshape(8, 128, 256).transpose(1, 0, 2).reshape(128, 2048))
    bvec = np.concatenate([f(L0_attn_bi), f(L0_attn_bf), f(L0_attn_bz), f(L0_attn_bo)], 0)
    bvec2 = np.ascontiguousarray(bvec.reshape(2, 128).T)
    wpT = np.ascontiguousarray(f(L0_attn_Wp).T)
    bprow = f(L0_attn_bp).reshape(1, D)
    wfT = np.ascontiguousarray(f(L0_ffn_weight).T)
    vcat = np.ascontiguousarray(np.concatenate(
        [f(L0_attn_ln_w).reshape(8, 128).T, f(L0_attn_ln_b).reshape(8, 128).T,
         f(L0_ffn_bias).reshape(8, 128).T], 1))
    cmat = f(L0_c)
    wmod = f(L0_wmod)
    eye = np.eye(128, dtype=np.float32)
    eyebf = np.eye(128, dtype=ml_dtypes.bfloat16)
    onesbf = np.ones((1, T), dtype=ml_dtypes.bfloat16)

    nc = build_program(dbg=_DBG)
    in_maps = [{
        "xT": np.ascontiguousarray(x[b].T),
        "wallT": wallT, "bvec2": bvec2, "wpT": wpT, "bprow": bprow,
        "wfT": wfT, "vcat": vcat, "cmat": cmat,
        "wmod_sl": np.ascontiguousarray(wmod[:, b * JW:(b + 1) * JW]),
        "eye": eye, "eyebf": eyebf, "onesbf": onesbf,
    } for b in range(B)]
    res = run_bass_kernel_spmd(nc, in_maps, core_ids=list(range(B)))
    out = np.stack([np.asarray(res.results[b]["outT"]).T for b in range(B)], 0)
    if _DBG:
        kernel._dbg = [res.results[b] for b in range(B)]
    return out


# version 28
# speedup vs baseline: 1.0468x; 1.0468x over previous
import contextlib
import numpy as np
import ml_dtypes
import concourse.bass as bass
import concourse.mybir as mybir
from concourse.bass_utils import run_bass_kernel_spmd
from concourse.alu_op_type import AluOpType as A
from concourse.aot_env import allow_ici_all_to_all

F32 = mybir.dt.float32
BF16 = mybir.dt.bfloat16
FP8 = mybir.dt.float8e4
AF = mybir.ActivationFunctionType
AX = mybir.AxisListType

B, T, D, NH = 8, 2048, 1024, 64
NCH = 8
TH = T // 2
JW = 6144 // B

_DBG = False

_ENGINES = ("sync", "gpsimd", "vector", "scalar", "tensor")


class Sch:

    def __init__(self):
        self.steps = {e: [] for e in _ENGINES}
        self.cnt = {}

    def add(self, eng, emit, waits=(), sem=None, inc=1):
        if sem is None:
            sem = {"sync": "dI", "gpsimd": "sG", "vector": "sV",
                   "scalar": "sS", "tensor": "sT"}[eng]
        v = self.cnt.get(sem, 0) + inc
        self.cnt[sem] = v
        self.steps[eng].append((emit, tuple(waits), sem, inc))
        return (sem, v)

    def dma(self, eng, out, in_, waits=(), sem=None):
        if sem is None:
            self._dmai = getattr(self, "_dmai", 0) + 1
            sem = "dma%d" % self._dmai
        return self.add(eng, lambda e, o=out, i=in_: e.dma_start(out=o, in_=i),
                        waits=waits, sem=sem, inc=16)

    def emit(self, block, sems):
        own = {"vector": "sV", "scalar": "sS", "gpsimd": "sG"}
        for eng in _ENGINES:
            if not self.steps[eng]:
                continue

            def section(e, eng=eng):
                seen = {}
                mysem = own.get(eng)
                cnt = 0
                for emitfn, waits, semkey, inc in self.steps[eng]:
                    if mysem is not None and semkey == mysem and cnt > 0:
                        waits = tuple(waits) + ((mysem, cnt),)
                    for sk, val in waits:
                        if seen.get(sk, 0) < val:
                            e.wait_ge(sems[sk], val)
                            seen[sk] = val
                    inst = emitfn(e)
                    if inst is not None:
                        inst.then_inc(sems[semkey], inc)
                        if semkey == mysem:
                            cnt += inc
            getattr(block, eng)(section)


def build_program(dbg=False):
    nc = bass.Bass("TRN2", target_bir_lowering=False, detect_race_conditions=False)

    def din(name, shape, dt=F32):
        return nc.dram_tensor(name, list(shape), dt, kind="ExternalInput")

    t_xT = din("xT", (D, T))
    t_wallT = din("wallT", (128, 2048))
    t_bvec2 = din("bvec2", (128, 2))
    t_wpT = din("wpT", (NH, D))
    t_bprow = din("bprow", (1, D))
    t_wfT = din("wfT", (D, D))
    t_vcat = din("vcat", (128, 24))
    t_cmat = din("cmat", (B, D))
    t_wmod = din("wmod_sl", (D, JW))
    t_eye = din("eye", (128, 128))
    t_eyebf = din("eyebf", (128, 128), BF16)
    t_onesbf = din("onesbf", (1, T), BF16)
    t_out = nc.dram_tensor("outT", [D, T], F32, kind="ExternalOutput")

    b_md_i = nc.dram_tensor("b_md_i", [B, JW], F32)
    b_md_o = nc.dram_tensor("b_md_o", [B, JW], F32)
    b_am_i = nc.dram_tensor("b_am_i", [1, 1], F32)
    b_row = nc.dram_tensor("b_row", [1, 2048], F32)
    b_row2 = nc.dram_tensor("b_row2", [1, 2048], F32)
    b_sc = nc.dram_tensor("b_sc", [1, 8], F32)
    b_am_o = nc.dram_tensor("b_am_o", [1, 1], F32)

    dbg_outs = {}
    if dbg:
        for nm, shp, dt in [("d_modT", [128, 48], F32), ("d_statT", [128, 96], F32),
                            ("d_coefT", [128, 48], BF16), ("d_gt0", [128, T], F32),
                            ("d_gt1", [128, T], F32),
                            ("d_x1", [128, T], F32), ("d_h2", [128, T], F32),
                            ("d_scal", [1, 16], F32), ("d_kv", [128, 8], F32),
                            ("d_sg", [128, 16], F32)]:
            dbg_outs[nm] = nc.dram_tensor(nm, shp, dt, kind="ExternalOutput")
        for nm, shp, dt in [("d_q8u", [128, 2048], mybir.dt.uint8),
                            ("d_wq8u", [128, 1024], mybir.dt.uint8)]:
            dbg_outs[nm] = nc.dram_tensor(nm, shp, dt, kind="ExternalOutput")

    es = contextlib.ExitStack()
    sbt = lambda name, shape, dt=F32: es.enter_context(nc.sbuf_tensor(name, list(shape), dt))
    pst = lambda name, shape, dt=F32: es.enter_context(nc.psum_tensor(name, list(shape), dt))

    r1 = sbt("r1", (128, 16384))
    r2 = sbt("r2", (128, 16384))
    r2b = sbt("r2b", (128, 2048))
    saA = es.enter_context(nc.sbuf_tensor("saA", [64, 6144], F32))
    ar2 = sbt("ar2", (128, 2048))

    wallT = [r2[:, 2048 + c * 256: 2048 + (c + 1) * 256] for c in range(NCH)]
    wl = [r2[:, 4096 + c * 256: 4096 + (c + 1) * 256] for c in range(NCH)]
    w3bf = [r2[:, 6144 + c * 128: 6144 + (c + 1) * 128].bitcast(BF16) for c in range(NCH)]
    gt0 = r2[:, 7168:9216]
    sqbfP = [r2[:, 9472:10496].bitcast(BF16), r2[:, 10496:11520].bitcast(BF16)]
    cmat_v = r2[0:B, 12800:13824]
    silu_v = r2[0:B, 14336:15360]
    silu_ch = lambda c: r2[0:B, 14336 + c * 128:14336 + (c + 1) * 128]
    wpT_v = r2[0:NH, 11776:12800]
    modsb8 = r2[0:B, 15360:15360 + JW]
    xbfP = [r2[:, 13312:14336].bitcast(BF16), r2[:, 14336:15360].bitcast(BF16)]
    x1 = [r2[:, c * 2048:(c + 1) * 2048] for c in range(NCH)]

    xch = [r1[:, c * 2048:(c + 1) * 2048] for c in range(NCH)]
    q8 = [r1[:, c * 2048: c * 2048 + 512].bitcast(FP8) for c in range(NCH)]

    gt1 = r2b[:, 0:2048]
    sqbf2P = [r2b[:, 0:1024].bitcast(BF16), r2b[:, 1024:2048].bitcast(BF16)]
    xbf2P = [ar2[:, 0:1024].bitcast(BF16), ar2[:, 1024:2048].bitcast(BF16)]

    sl_li, sl_lf, sl_z, sl_o, sl_m, sl_mp = (saA[0:64, k * 1024:(k + 1) * 1024]
                                             for k in range(6))
    statrowsA = saA[0:4, 0:2048]
    statrowsB = saA[0:2, 2048:4096]
    statrows2a = saA[0:1, 0:2048]
    statrows2b = saA[0:1, 2048:4096]
    arow = saA[0:1, 4096:6144]
    r2row = saA[0:1, 4096:6144]

    h65 = sbt("h65", (65, T), BF16)
    lhsT65 = sbt("lhsT65", (65, D), BF16)
    rhs9 = sbt("rhs9", (4, T), BF16)
    lhsT9 = sbt("lhsT9", (4, 256), BF16)
    B2rhs = sbt("B2rhs", (2, T), BF16)
    B2lhsT = sbt("B2lhsT", (2, 1024), BF16)
    k8 = sbt("k8", (8, 128), BF16)
    kvbf = sbt("kvbf", (128, 8), BF16)
    sT32 = sbt("sT32", (128, 64))
    wstatA = sbt("wstatA", (128, 32), BF16)
    wstatB = sbt("wstatB", (128, 16), BF16)
    wq8 = sbt("wq8", (128, 8192), FP8)
    e1P = sbt("e1P", (128, 2048), BF16)
    wfS = sbt("wfS", (128, 1024))
    onecbf = sbt("onecbf", (128, 1), BF16)

    vcat = sbt("vcat_s", (128, 24))
    bvec2 = sbt("bvec2_s", (128, 2))
    modT = sbt("modT", (128, 48))
    modsb48 = sbt("modsb48", (48, 128))
    deriv = sbt("deriv", (128, 64))
    Vs = sbt("Vs", (128, 40))
    statT = sbt("statT", (128, 96))
    statT2 = sbt("statT2", (128, 32))
    coefT = sbt("coefT", (128, 48), BF16)
    coefT2 = sbt("coefT2", (128, 16))
    coefT2b = sbt("coefT2b", (128, 16), BF16)
    coefTa = sbt("coefTa", (128, 16))
    rm = sbt("rm", (128, 256))
    sg_ = sbt("sg_", (128, 16))
    kvf = sbt("kvf", (128, 8))
    kvr = sbt("kvr", (128, 24))
    sc5 = sbt("sc5", (5, 1))
    amx8 = sbt("amx8", (128, 8))
    amxc = sbt("amxc", (128, 1))
    prow = sbt("prow", (1, 128))
    s11 = sbt("s11", (1, 16))
    ones_r = sbt("ones_r", (1, 128))
    ones_c = sbt("ones_c", (128, 1))
    eye = sbt("eye_s", (128, 128))
    eyebf = sbt("eyebf_s", (128, 128), BF16)

    pb0 = pst("pb0", (128, 2048))
    pb1 = pst("pb1", (128, 2048))
    pb1bf = pb1[:, 0:1024].bitcast(BF16)
    pb1bf2 = pb1[:, 1024:2048].bitcast(BF16)

    S = Sch()
    rmv = lambda k: rm[:, k * 16:(k + 1) * 16]
    MCAR, CCAR, MLAST, B2S, WAMA, WAMB, WACC = 208, 209, 210, 211, 213, 214, 215
    SGC = {"sca": 0, "sc2": 1, "spsa": 2, "ssan": 3, "ssa2": 4, "sP1": 5, "sP12": 6,
           "xs2": 7, "deq": 8, "sw2": 9}
    sgc = lambda k: sg_[:, SGC[k]:SGC[k] + 1]

    d_cmat = S.dma("sync", cmat_v, t_cmat.ap()[:, :])
    d_eye = S.dma("sync", eye[:, :], t_eye.ap()[:, :])
    d_wpT = S.dma("sync", wpT_v, t_wpT.ap()[:, :])
    d_bprow = S.dma("sync", saA[0:1, 4096:5120], t_bprow.ap()[:, :])
    wmpark = [r1[:, 10240 + c * 768: 10240 + (c + 1) * 768] for c in range(NCH)]
    d_wmpark = [S.dma("scalar", wmpark[c], t_wmod.ap()[c * 128:(c + 1) * 128, :])
                for c in range(NCH)]
    d_eyebf = S.dma("sync", eyebf[:, :], t_eyebf.ap()[:, :])
    d_vcat = S.dma("sync", vcat[:, :], t_vcat.ap()[:, :])
    d_bvec = S.dma("sync", bvec2[:, :], t_bvec2.ap()[:, :])
    d_x = [S.dma("scalar", xch[c], t_xT.ap()[c * 128:(c + 1) * 128, :]) for c in range(5)]
    d_wall1 = S.dma("scalar", r2[:, 2048:4096], t_wallT.ap()[:, :])
    d_wall = [d_wall1] * NCH

    t_ones = S.add("vector", lambda e: e.memset(ones_r[:, :], 1.0))
    t_ones2 = S.add("vector", lambda e: e.memset(ones_c[:, :], 1.0))
    t_ones3 = S.add("vector", lambda e: e.memset(onecbf[:, :], 1.0))
    t_z11 = S.add("vector", lambda e: e.memset(s11[:, :], 0.0))
    t_zsg = S.add("vector", lambda e: e.memset(sg_[:, :], 0.0))
    t_silu = S.add("scalar", lambda e: e.activation(out=silu_v, in_=cmat_v, func=AF.Silu),
                   waits=[d_cmat])
    tok_tr = [S.add("tensor", lambda e, c=c: e.transpose(
        pb0[:, 1024 + c * 8:1024 + (c + 1) * 8], silu_ch(c), eye[0:B, 0:B]),
        waits=[t_silu, d_eye]) for c in range(NCH)]
    t_sT = S.add("vector", lambda e: e.tensor_copy(out=sT32[:, :], in_=pb0[:, 1024:1088]),
                 waits=tok_tr)
    mmod = {}
    tok_mod = []
    for c in range(NCH):
        mmod[c] = [S.add("tensor", lambda e, c=c, n0=n0, n1=n1: e.matmul(
            pb0[0:B, n0:n1], sT32[:, c * 8:(c + 1) * 8], wmpark[c][:, n0:n1],
            start=(c == 0), stop=(c == NCH - 1)),
            waits=[d_wmpark[c], t_sT]) for (n0, n1) in ((0, 512), (512, JW))]
        tok_mod = tok_mod + mmod[c]
    d_x.append(S.dma("scalar", xch[5], t_xT.ap()[5 * 128:6 * 128, :],
                     waits=mmod[0] + mmod[1] + mmod[2]))
    d_x.append(S.dma("scalar", xch[6], t_xT.ap()[6 * 128:7 * 128, :],
                     waits=mmod[2] + mmod[3] + mmod[4] + mmod[5]))
    d_x.append(S.dma("scalar", xch[7], t_xT.ap()[7 * 128:8 * 128, :],
                     waits=mmod[5] + mmod[6] + mmod[7]))
    t_modev = S.add("vector", lambda e: e.tensor_copy(out=modsb8, in_=pb0[0:B, 0:JW]),
                    waits=tok_mod)
    t_l65a = S.add("vector", lambda e: e.tensor_copy(out=lhsT65[0:NH, :], in_=wpT_v),
                   waits=[d_wpT])
    t_l65b = S.add("vector", lambda e: e.tensor_copy(out=lhsT65[64:65, :],
                                                     in_=saA[0:1, 4096:5120]), waits=[d_bprow])
    d_mdi = S.dma("sync", b_md_i.ap()[:, :], modsb8, waits=[t_modev])

    def _cc_mod(e):
        with allow_ici_all_to_all():
            return e.collective_compute(
                "AllToAll", A.bypass, replica_groups=[list(range(B))],
                ins=[b_md_i.ap().opt()], outs=[b_md_o.ap().opt()])
    cc_mod = S.add("gpsimd", _cc_mod, waits=[d_mdi], sem="cc", inc=1)
    d_mdo = S.dma("sync", modsb48[:, :],
                  b_md_o.ap().rearrange("r (s e) -> (r s) e", e=128), waits=[cc_mod])
    t_modtr = S.add("tensor", lambda e: e.transpose(
        pb1[:, 0:48], modsb48[:, :], eye[0:48, 0:48]), waits=[d_mdo, d_eye])
    t_modT = S.add("vector", lambda e: e.tensor_copy(out=modT[:, :], in_=pb1[:, 0:48]),
                   waits=[t_modtr])

    t_p1 = S.add("vector", lambda e: e.tensor_scalar(
        out=deriv[:, 0:8], in0=modT[:, 8:16], scalar1=1.0, scalar2=None, op0=A.add),
        waits=[t_modT])
    t_p2 = S.add("vector", lambda e: e.tensor_scalar(
        out=deriv[:, 8:16], in0=modT[:, 32:40], scalar1=1.0, scalar2=None, op0=A.add),
        waits=[t_modT])

    def _gab(dst, src):
        t2 = S.add("vector", lambda e: e.tensor_scalar(
            out=deriv[:, dst:dst + 8], in0=modT[:, src:src + 8], scalar1=0.0, scalar2=2.0,
            op0=A.is_ge, op1=A.mult), waits=[t_modT])
        t3 = S.add("vector", lambda e: e.tensor_scalar(
            out=deriv[:, dst:dst + 8], in0=deriv[:, dst:dst + 8], scalar1=-1.0, scalar2=None,
            op0=A.add), waits=[t2])
        t4 = S.add("vector", lambda e: e.tensor_tensor(
            out=deriv[:, 56:64], in0=deriv[:, dst:dst + 8], in1=modT[:, src:src + 8],
            op=A.mult), waits=[t3])
        t5 = S.add("vector", lambda e: e.tensor_scalar(
            out=deriv[:, 56:64], in0=deriv[:, 56:64], scalar1=0.25, scalar2=None,
            op0=A.max), waits=[t4])
        return S.add("vector", lambda e: e.tensor_tensor(
            out=deriv[:, dst:dst + 8], in0=deriv[:, dst:dst + 8], in1=deriv[:, 56:64],
            op=A.mult), waits=[t5])
    t_ga = _gab(16, 16)
    t_gb = _gab(24, 40)
    t_psa = S.add("vector", lambda e: e.tensor_tensor(
        out=deriv[:, 32:40], in0=deriv[:, 0:8], in1=modT[:, 0:8], op=A.mult),
        waits=[t_p1, t_ga])
    t_c2a = S.add("vector", lambda e: e.tensor_scalar(
        out=deriv[:, 40:48], in0=modT[:, 8:16], scalar1=2.0, scalar2=None, op0=A.add),
        waits=[t_modT])
    t_c2 = S.add("vector", lambda e: e.tensor_tensor(
        out=deriv[:, 40:48], in0=deriv[:, 40:48], in1=modT[:, 8:16], op=A.mult),
        waits=[t_c2a])
    t_sa2 = S.add("vector", lambda e: e.tensor_tensor(
        out=deriv[:, 48:56], in0=modT[:, 0:8], in1=modT[:, 0:8], op=A.mult),
        waits=[t_modT])

    wsA_j = wstatA.ap().rearrange("p (c j) -> p j c", j=4)
    wsB_j = wstatB.ap().rearrange("p (c j) -> p j c", j=2)
    t_ws = [S.add("vector", lambda e: e.memset(wstatA[:, :], 1.0)),
            S.add("vector", lambda e: e.memset(wstatB[:, :], 1.0))]
    t_ws.append(S.add("vector", lambda e: e.tensor_copy(out=wsA_j[:, 1, :], in_=modT[:, 8:16]),
                      waits=[t_modT]))
    t_ws.append(S.add("vector", lambda e: e.tensor_copy(out=wsA_j[:, 2, :], in_=deriv[:, 40:48]),
                      waits=[t_c2]))
    t_ws.append(S.add("vector", lambda e: e.tensor_copy(out=wsA_j[:, 3, :], in_=deriv[:, 32:40]),
                      waits=[t_psa]))
    t_ws.append(S.add("vector", lambda e: e.tensor_copy(out=wsB_j[:, 1, :], in_=deriv[:, 40:48]),
                      waits=[t_c2]))
    vs_j = Vs.ap().rearrange("p (c j) -> p j c", j=5)
    t_vs = [S.add("vector", lambda e, j=j, srcv=srcv: e.tensor_copy(out=vs_j[:, j, :], in_=srcv),
                  waits=[wtok])
            for j, (srcv, wtok) in enumerate([(modT[:, 8:16], t_modT), (deriv[:, 40:48], t_c2),
                                              (deriv[:, 32:40], t_psa), (modT[:, 0:8], t_modT),
                                              (deriv[:, 48:56], t_sa2)])]

    xbufs = [xbfP[0], xbfP[1],
             r2[:, 7168:8192].bitcast(BF16), r2[:, 8192:9216].bitcast(BF16),
             r2b[:, 0:1024].bitcast(BF16), r2b[:, 1024:2048].bitcast(BF16),
             r2[:, 0:1024].bitcast(BF16), r2[:, 1024:2048].bitcast(BF16)]
    sqbufs = [sqbfP[0], sqbfP[1],
              ar2[:, 0:1024].bitcast(BF16), ar2[:, 1024:2048].bitcast(BF16),
              e1P[:, :]]
    t_xbf, t_sq = {}, {}
    prevB5 = {}
    for c in range(NCH):
        t_xbf[c] = S.add("vector", lambda e, c=c: e.tensor_copy(
            out=xbufs[c], in_=xch[c]), waits=[d_x[c]] + (tok_tr if c < 2 else []))
    for c in range(5):
        t_sq[c] = S.add("scalar", lambda e, c=c: e.activation(
            out=sqbufs[c % 5], in_=xch[c], func=AF.Square), waits=[d_x[c]])

    t_wl = [S.add("vector", lambda e, c=c: e.tensor_scalar(
        out=wl[c], in0=wallT[c], scalar1=vcat[:, c:c + 1], scalar2=None, op0=A.mult),
        waits=[d_wall[c], d_vcat]) for c in range(NCH)]
    t_w3 = [S.add("vector", lambda e, c=c: e.tensor_scalar(
        out=w3bf[c], in0=wl[c], scalar1=deriv[:, c:c + 1], scalar2=None, op0=A.mult),
        waits=[t_wl[c], t_p1]) for c in range(NCH)]

    kvr_j = kvr.ap().rearrange("p (c j) -> p j c", j=3)
    t_kvr = [S.add("vector", lambda e: e.memset(kvr[:, :], 1.0))]
    t_kvr.append(S.add("vector", lambda e: e.tensor_copy(out=kvr_j[:, 0, :], in_=modT[:, 0:8]),
                       waits=[t_modT]))
    t_kvr.append(S.add("vector", lambda e: e.tensor_copy(out=kvr_j[:, 1, :], in_=deriv[:, 0:8]),
                       waits=[t_p1]))

    tok_A, tok_B = [], []
    for c in range(NCH):
        if c >= 5:
            t_sq[c] = S.add("scalar", lambda e, c=c: e.activation(
                out=sqbufs[c % 5], in_=xch[c], func=AF.Square),
                waits=[d_x[c]] + prevB5[c - 5])
        mmA = [S.add("tensor", lambda e, c=c, n=n: e.matmul(
            pb0[0:4, n * 512:(n + 1) * 512], wstatA[:, c * 4:(c + 1) * 4],
            xbufs[c][:, n * 512:(n + 1) * 512], start=(c == 0), stop=(c == NCH - 1)),
            waits=[t_xbf[c]] + t_ws) for n in range(4)]
        mmB = [S.add("tensor", lambda e, c=c, n=n: e.matmul(
            pb1[0:2, n * 512:(n + 1) * 512], wstatB[:, c * 2:(c + 1) * 2],
            sqbufs[c % 5][:, n * 512:(n + 1) * 512], start=(c == 0), stop=(c == NCH - 1)),
            waits=[t_sq[c]] + t_ws) for n in range(4)]
        prevB5[c] = mmB
        tok_A += mmA
        tok_B += mmB
    t_srA = S.add("vector", lambda e: e.tensor_copy(out=statrowsA, in_=pb0[0:4, :]),
                  waits=tok_A)
    t_srB = S.add("vector", lambda e: e.tensor_copy(out=statrowsB, in_=pb1[0:2, :]),
                  waits=tok_B)
    tok_kv = []
    for mt in range(2):
        for c in range(NCH):
            tok_kv.append(S.add("tensor", lambda e, c=c, mt=mt: e.matmul(
                pb0[:, mt * 4:mt * 4 + 3], wl[c][:, mt * 128:(mt + 1) * 128],
                kvr[:, c * 3:(c + 1) * 3], start=(c == 0), stop=(c == NCH - 1)),
                waits=[t_wl[c], t_srA] + t_kvr))
        for c in range(NCH):
            tok_kv.append(S.add("tensor", lambda e, c=c, mt=mt: e.matmul(
                pb0[:, mt * 4 + 3:mt * 4 + 4], wallT[c][:, mt * 128:(mt + 1) * 128],
                vcat[:, 8 + c:9 + c], start=(c == 0), stop=(c == NCH - 1)),
                waits=[d_wall[c], d_vcat, t_srA]))
    t_kvf = S.add("vector", lambda e: e.tensor_copy(out=kvf[:, :], in_=pb0[:, 0:8]),
                  waits=tok_kv)
    t_kc = S.add("vector", lambda e: e.tensor_tensor(
        out=kvf[:, 3:4], in0=kvf[:, 3:4], in1=bvec2[:, 0:1], op=A.add),
        waits=[t_kvf, d_bvec])
    t_kc2 = S.add("vector", lambda e: e.tensor_tensor(
        out=kvf[:, 7:8], in0=kvf[:, 7:8], in1=bvec2[:, 1:2], op=A.add),
        waits=[t_kvf, d_bvec])
    t_kvb = S.add("vector", lambda e: e.tensor_copy(
        out=kvbf.ap().rearrange("p (v b) -> p v b", b=2), in_=kvf.ap().rearrange("p (b v) -> p v b", b=2)),
        waits=[t_kc, t_kc2])
    t_kvt = S.add("tensor", lambda e: e.transpose(pb1bf[0:8, 0:128], kvbf[:, :], eyebf[:, :]),
                  waits=[t_kvb, d_eyebf, t_srB])
    t_k8 = S.add("vector", lambda e: e.tensor_copy(out=k8[:, :], in_=pb1bf[0:8, 0:128]),
                 waits=[t_kvt])
    d_l9 = S.dma("sync", lhsT9.ap().rearrange("p (b c) -> p b c", b=2), k8[:, :],
                 waits=[t_k8])
    tok_ks = [S.add("tensor", lambda e, c=c: e.matmul(
        pb0[0:5, 8:9], Vs[:, c * 5:(c + 1) * 5], ones_c[:, :],
        start=(c == 0), stop=(c == NCH - 1)), waits=t_vs + [t_ones2, t_srA] + tok_kv[-1:])
        for c in range(NCH)]
    t_sc5 = S.add("vector", lambda e: e.tensor_copy(out=sc5[:, :], in_=pb0[0:5, 8:9]),
                  waits=tok_ks)
    d_sc1 = S.dma("sync", b_sc.ap()[0:1, 0:5], sc5[:, :], waits=[t_sc5])
    t_sg = S.dma("sync", sg_[:, 0:5], b_sc.ap()[0:1, 0:5].partition_broadcast(128),
                 waits=[d_sc1, t_zsg])
    t_sP1 = S.add("vector", lambda e: e.tensor_scalar(
        out=sgc("sP1"), in0=sgc("sca"), scalar1=1024.0, scalar2=None, op0=A.add),
        waits=[t_sg])
    t_sP12 = S.add("vector", lambda e: e.tensor_scalar(
        out=sgc("sP12"), in0=sgc("sc2"), scalar1=1024.0, scalar2=None, op0=A.add),
        waits=[t_sg])
    t_ssan = S.add("vector", lambda e: e.tensor_scalar(
        out=sgc("ssan"), in0=sgc("ssan"), scalar1=1.0 / 1024.0, scalar2=None, op0=A.mult),
        waits=[t_sg])

    tok_st = []
    for tt in range(16):
        tok_st.append(S.add("tensor", lambda e, tt=tt: e.transpose(
            pb1[:, 512 + tt * 6:512 + tt * 6 + 4],
            statrowsA[:, tt * 128:(tt + 1) * 128], eye[0:4, 0:4]),
            waits=[t_srA, t_srB, t_sc5]))
        tok_st.append(S.add("tensor", lambda e, tt=tt: e.transpose(
            pb1[:, 512 + tt * 6 + 4:512 + tt * 6 + 6],
            statrowsB[:, tt * 128:(tt + 1) * 128], eye[0:2, 0:2]),
            waits=[t_srA, t_srB, t_sc5]))
    t_stT = S.add("vector", lambda e: e.tensor_copy(out=statT[:, :], in_=pb1[:, 512:608]),
                  waits=tok_st)


    st_ap = statT.ap().rearrange("p (tt j) -> p j tt", j=6)
    V_S0, V_Sca, V_Sc2, V_Spsa, V_Q0, V_Qc2 = (st_ap[:, j, :] for j in range(6))

    rmt = []

    def vop(emit, waits=()):
        tok = S.add("vector", emit, waits=list(waits) + [t_stT])
        rmt.append(tok)
        return tok
    vop(lambda e: e.tensor_scalar(out=rmv(0), in0=V_S0, scalar1=1.0 / 1024, scalar2=None, op0=A.mult))
    vop(lambda e: e.tensor_tensor(out=rmv(2), in0=rmv(0), in1=rmv(0), op=A.mult))
    vop(lambda e: e.scalar_tensor_tensor(out=rmv(3), in0=V_Q0, scalar=1.0 / 1024,
                                         in1=rmv(2), op0=A.mult, op1=A.subtract))
    vop(lambda e: e.tensor_scalar(out=rmv(3), in0=rmv(3), scalar1=1e-5, scalar2=None, op0=A.add))
    vop(lambda e: e.reciprocal(out=rmv(5), in_=rmv(3)))
    t_rx = S.add("scalar", lambda e: e.activation(out=rmv(6), in_=rmv(5), func=AF.Sqrt),
                 waits=rmt[-1:])
    vop(lambda e: e.tensor_tensor(out=rmv(7), in0=V_S0, in1=V_Sca, op=A.add))
    vop(lambda e: e.scalar_tensor_tensor(out=rmv(8), in0=rmv(0), scalar=sgc("sP1"),
                                         in1=rmv(7), op0=A.mult, op1=A.subtract),
        waits=[t_sP1])
    vop(lambda e: e.tensor_tensor(out=rmv(9), in0=rmv(8), in1=rmv(6), op=A.mult),
        waits=[t_rx])
    vop(lambda e: e.tensor_scalar(out=rmv(7), in0=rmv(9), scalar1=-1.0 / 1024,
                                  scalar2=sgc("ssan"), op0=A.mult, op1=A.add),
        waits=[t_ssan])
    vop(lambda e: e.tensor_tensor(out=rmv(8), in0=V_S0, in1=V_Sc2, op=A.add))
    vop(lambda e: e.tensor_tensor(out=rmv(9), in0=V_Q0, in1=V_Qc2, op=A.add))
    vop(lambda e: e.tensor_scalar(out=rmv(10), in0=rmv(2), scalar1=sgc("sP12"),
                                  scalar2=None, op0=A.mult), waits=[t_sP12])
    vop(lambda e: e.tensor_tensor(out=rmv(11), in0=rmv(0), in1=rmv(8), op=A.mult))
    vop(lambda e: e.scalar_tensor_tensor(out=rmv(9), in0=rmv(11), scalar=-2.0,
                                         in1=rmv(9), op0=A.mult, op1=A.add))
    vop(lambda e: e.tensor_tensor(out=rmv(9), in0=rmv(9), in1=rmv(10), op=A.add))
    vop(lambda e: e.tensor_tensor(out=rmv(9), in0=rmv(9), in1=rmv(5), op=A.mult))
    vop(lambda e: e.scalar_tensor_tensor(out=rmv(10), in0=rmv(0), scalar=sgc("spsa"),
                                         in1=V_Spsa, op0=A.mult, op1=A.subtract),
        waits=[t_sg])
    vop(lambda e: e.tensor_tensor(out=rmv(10), in0=rmv(10), in1=rmv(6), op=A.mult),
        waits=[t_rx])
    vop(lambda e: e.scalar_tensor_tensor(out=rmv(9), in0=rmv(10), scalar=-2.0,
                                         in1=rmv(9), op0=A.mult, op1=A.add))
    vop(lambda e: e.tensor_scalar(out=rmv(9), in0=rmv(9), scalar1=sgc("ssa2"),
                                  scalar2=1.0 / 1024, op0=A.add, op1=A.mult), waits=[t_sg])
    vop(lambda e: e.tensor_tensor(out=rmv(10), in0=rmv(7), in1=rmv(7), op=A.mult))
    vop(lambda e: e.tensor_tensor(out=rmv(9), in0=rmv(9), in1=rmv(10), op=A.subtract))
    vop(lambda e: e.tensor_scalar(out=rmv(9), in0=rmv(9), scalar1=1e-5, scalar2=None, op0=A.add))
    vop(lambda e: e.reciprocal(out=rmv(11), in_=rmv(9)))
    t_rh = S.add("scalar", lambda e: e.activation(out=rmv(10), in_=rmv(11), func=AF.Sqrt),
                 waits=rmt[-1:])
    vop(lambda e: e.tensor_tensor(out=rmv(11), in0=rmv(6), in1=rmv(10), op=A.mult),
        waits=[t_rh])
    vop(lambda e: e.tensor_tensor(out=rmv(12), in0=rmv(11), in1=rmv(0), op=A.mult))
    vop(lambda e: e.tensor_tensor(out=rmv(4), in0=rmv(10), in1=rmv(7), op=A.mult))
    co_ap = coefT.ap().rearrange("p (tt j) -> p j tt", j=3)
    t_co = [S.add("vector", lambda e, j=j, slab=slab, sg=sg: e.tensor_scalar(
        out=co_ap[:, j, :], in0=rmv(slab), scalar1=sg, scalar2=None, op0=A.mult),
        waits=rmt[-6:] + [t_rh]) for j, slab, sg in [(0, 10, 1.0), (1, 12, -1.0), (2, 4, -1.0)]]
    t_coa = S.add("vector", lambda e: e.tensor_copy(
        out=coefTa[:, :], in_=rmv(11)), waits=rmt[-6:] + [t_rh])
    tok_ct = [S.add("tensor", lambda e, tt=tt: e.transpose(
        pb1bf[0:3, tt * 128:(tt + 1) * 128], coefT[:, tt * 3:(tt + 1) * 3], eyebf[:, :]),
        waits=t_co + [t_k8, t_stT]) for tt in range(16)]
    t_r9a = S.add("vector", lambda e: e.tensor_copy(out=rhs9[0:3, :], in_=pb1bf[0:3, 0:2048]),
                  waits=tok_ct)
    t_r9b = S.dma("sync", rhs9[3:4, :], t_onesbf.ap()[:, :])
    tok_cta = [S.add("tensor", lambda e, tt=tt: e.transpose(
        pb1[0:1, tt * 128:(tt + 1) * 128], coefTa[:, tt:tt + 1], eye[:, :]),
        waits=[t_coa, t_r9a]) for tt in range(16)]
    t_arow = S.add("vector", lambda e: e.tensor_copy(out=arow, in_=pb1[0:1, 0:2048]),
                   waits=tok_cta + [t_l65b])
    d_ar1 = S.dma("sync", b_row.ap()[:, :], arow, waits=[t_arow])
    t_asb = S.dma("sync", ar2[:, :], b_row.ap().partition_broadcast(128),
                  waits=[d_ar1] + tok_B[-1:])


    tok_G = []
    first_w = [t_kvf, t_sc5, t_srA, t_srB, t_stT, t_k8, t_r9a, t_arow]
    for c in range(NCH + 1):
        for mt in range(2):
            pg = pb0 if mt == 0 else pb1
            for n in range(4):
                if c < NCH:
                    tok_G.append(S.add("tensor", lambda e, c=c, mt=mt, n=n, pg=pg: e.matmul(
                        pg[:, n * 512:(n + 1) * 512], w3bf[c][:, mt * 128:(mt + 1) * 128],
                        xbufs[c][:, n * 512:(n + 1) * 512],
                        start=(c == 0), stop=False),
                        waits=[t_xbf[c], t_w3[c]] + (first_w if c == 0 else [])))
                else:
                    tok_G.append(S.add("tensor", lambda e, mt=mt, n=n, pg=pg: e.matmul(
                        pg[:, n * 512:(n + 1) * 512], lhsT9[:, mt * 128:(mt + 1) * 128],
                        rhs9[:, n * 512:(n + 1) * 512], start=False, stop=True),
                        waits=[d_l9, t_r9b]))
    t_gt0 = [S.add("vector", lambda e, n=n: e.tensor_tensor(
        out=gt0[:, n * 512:(n + 1) * 512], in0=pb0[:, n * 512:(n + 1) * 512],
        in1=ar2[:, n * 512:(n + 1) * 512], op=A.mult), waits=tok_G + [t_asb])
        for n in range(4)]
    t_gt1 = [S.add("vector", lambda e, n=n: e.tensor_tensor(
        out=gt1[:, n * 512:(n + 1) * 512], in0=pb1[:, n * 512:(n + 1) * 512],
        in1=ar2[:, n * 512:(n + 1) * 512], op=A.mult), waits=tok_G + [t_asb])
        for n in range(4)]
    t_clip = S.add("vector", lambda e: e.tensor_scalar(
        out=gt0[:, :], in0=gt0[:, :], scalar1=20.0, scalar2=-20.0, op0=A.min, op1=A.max),
        waits=t_gt0)

    wq8f32 = wq8.bitcast(F32)
    P1R = [wq8f32[:, 0:1024], wq8f32[:, 1024:2048], wfS[:, 0:1024]]
    e1Pf32 = e1P.bitcast(F32)
    P2R = [e1Pf32[:, 0:1024], wfS[:, 0:1024]]
    q8tmp = e1P.bitcast(FP8)[:, 0:2048]
    wred_prev = []
    wacc_tok = [None]
    d_wf1 = []

    def emit_wf1_dma(c):
        w = [] if c < 3 else [wred_prev[c - 3]]
        d_wf1.append(S.dma("scalar", P1R[c % 3], t_wfT.ap()[c * 128:(c + 1) * 128, :],
                           waits=w))

    def emit_wred(c):
        tk = S.add("vector", lambda e, c=c: e.tensor_reduce(
            out=rm[:, 208 + (c % 4):208 + (c % 4) + 1], in_=P1R[c % 3], axis=AX.X,
            op=A.max, apply_absolute_value=True), waits=[d_wf1[c]])
        if c == 0:
            wa = S.add("vector", lambda e: e.tensor_copy(
                out=rm[:, WACC:WACC + 1], in_=rm[:, 208:209]), waits=[tk])
        else:
            wa = S.add("vector", lambda e, c=c: e.tensor_tensor(
                out=rm[:, WACC:WACC + 1], in0=rm[:, WACC:WACC + 1],
                in1=rm[:, 208 + (c % 4):208 + (c % 4) + 1], op=A.max),
                waits=[tk, wacc_tok[0]])
        wacc_tok[0] = wa
        wred_prev.append(tk)
    for c in range(3):
        emit_wf1_dma(c)

    t_h1 = S.add("vector", lambda e: e.memset(h65[64:65, :], 1.0))
    s_lf = saA[0:64, 0:2048]
    s_i = saA[0:64, 2048:4096]
    s_m = saA[0:64, 4096:6144]
    li_v = gt0[0:64, :]
    z_v = gt1[0:64, :]
    o_v = ar2[0:64, :]
    d_lf = S.dma("sync", s_lf, gt0[64:128, :], waits=[t_clip])
    d_oo = S.dma("scalar", o_v, gt1[64:128, :], waits=t_gt1)
    t_zt = S.add("scalar", lambda e: e.activation(out=z_v, in_=z_v, func=AF.Tanh),
                 waits=t_gt1)
    t_ot = S.add("scalar", lambda e: e.activation(out=o_v, in_=o_v, func=AF.Tanh,
                                                  scale=0.5), waits=[d_oo])
    t_ms = S.add("vector", lambda e: e.tensor_tensor_scan(
        out=s_m, data0=s_lf, data1=li_v, initial=0.0, op0=A.add, op1=A.max),
        waits=[d_lf, t_clip])
    t_i1 = S.add("gpsimd", lambda e: e.tensor_tensor(
        out=s_i, in0=li_v, in1=s_m, op=A.subtract), waits=[t_ms])
    t_d1 = S.add("vector", lambda e: e.tensor_tensor(
        out=s_lf, in0=s_lf, in1=s_m, op=A.subtract), waits=[t_ms])
    t_f2 = S.add("vector", lambda e: e.tensor_tensor(
        out=s_lf[:, 1:T], in0=s_lf[:, 1:T], in1=s_m[:, 0:T - 1], op=A.add),
        waits=[t_d1])
    t_oa = S.add("vector", lambda e: e.tensor_scalar(
        out=o_v, in0=o_v, scalar1=0.5, scalar2=0.5, op0=A.mult, op1=A.add),
        waits=[t_ot])
    t_ex = S.add("scalar", lambda e: e.activation(
        out=saA[0:64, 0:4096], in_=saA[0:64, 0:4096], func=AF.Exp),
        waits=[t_f2, t_i1])
    t_u = S.add("vector", lambda e: e.tensor_tensor(
        out=s_i, in0=s_i, in1=z_v, op=A.mult), waits=[t_ex, t_zt])
    t_cs = S.add("vector", lambda e: e.tensor_tensor_scan(
        out=s_m, data0=s_lf, data1=s_i, initial=0.0, op0=A.mult, op1=A.add),
        waits=[t_u])
    t_tc = S.add("scalar", lambda e: e.activation(out=s_m, in_=s_m, func=AF.Tanh),
                 waits=[t_cs])
    t_h65 = [S.add("vector", lambda e: e.tensor_tensor(
        out=h65[0:64, :], in0=o_v, in1=s_m, op=A.mult), waits=[t_tc, t_oa])]

    tok_res1 = []
    prev_ev = {}
    for m in range(NCH):
        pg = pb0 if m % 2 == 0 else pb1
        psum_free = [prev_ev[m - 2]] if m >= 2 else (t_gt1 + [t_clip])
        mms = [S.add("tensor", lambda e, m=m, n=n, pg=pg: e.matmul(
            pg[:, n * 512:(n + 1) * 512], lhsT65[:, m * 128:(m + 1) * 128],
            h65[:, n * 512:(n + 1) * 512], start=True, stop=True),
            waits=t_h65 + [t_h1, t_l65a, t_l65b] + psum_free) for n in range(4)]
        region_dead = [t_kvt] if m < 2 else []
        ev = S.add("vector", lambda e, m=m, pg=pg: e.scalar_tensor_tensor(
            out=x1[m], in0=pg[:, :], scalar=deriv[:, 16 + m:17 + m], in1=xch[m],
            op0=A.mult, op1=A.add), waits=mms + [t_ga] + region_dead)
        prev_ev[m] = ev
        tok_res1.append(ev)


    t_x1bf, t_x1sq = {}, {}
    tok_S1, tok_Q1 = [], []
    prevA, prevB = [], []
    prev2A, prev2B = [], []
    for c in range(NCH):
        t_x1bf[c] = S.add("vector", lambda e, c=c: e.tensor_copy(
            out=xbf2P[c % 2], in_=x1[c]), waits=[tok_res1[c], t_asb] + prev2A)
        t_x1sq[c] = S.add("scalar", lambda e, c=c: e.activation(
            out=sqbf2P[c % 2], in_=x1[c], func=AF.Square),
            waits=[tok_res1[c]] + prev2B)
        mmA = [S.add("tensor", lambda e, c=c, n=n: e.matmul(
            pb0[0:1, n * 512:(n + 1) * 512], onecbf[:, :],
            xbf2P[c % 2][:, n * 512:(n + 1) * 512], start=(c == 0), stop=(c == NCH - 1)),
            waits=[t_x1bf[c], t_ones3] + ([tok_res1[6]] if c == 0 else []))
            for n in range(4)]
        mmB = [S.add("tensor", lambda e, c=c, n=n: e.matmul(
            pb1[0:1, n * 512:(n + 1) * 512], onecbf[:, :],
            sqbf2P[c % 2][:, n * 512:(n + 1) * 512], start=(c == 0), stop=(c == NCH - 1)),
            waits=[t_x1sq[c], t_ones3] + ([tok_res1[7]] if c == 0 else []))
            for n in range(4)]
        prev2A, prev2B = prevA, prevB
        prevA, prevB = mmA, mmB
        tok_S1 += mmA
        tok_Q1 += mmB
    t_sr2a = S.add("vector", lambda e: e.tensor_copy(out=statrows2a, in_=pb0[0:1, :]),
                   waits=tok_S1)
    t_sr2b = S.add("vector", lambda e: e.tensor_copy(out=statrows2b, in_=pb1[0:1, :]),
                   waits=tok_Q1)
    tok_st2 = []
    for tt in range(16):
        tok_st2.append(S.add("tensor", lambda e, tt=tt: e.transpose(
            pb0[:, 512 + tt * 2:512 + tt * 2 + 1],
            statrows2a[:, tt * 128:(tt + 1) * 128], eye[0:1, 0:1]),
            waits=[t_sr2a, t_sr2b]))
        tok_st2.append(S.add("tensor", lambda e, tt=tt: e.transpose(
            pb0[:, 512 + tt * 2 + 1:512 + tt * 2 + 2],
            statrows2b[:, tt * 128:(tt + 1) * 128], eye[0:1, 0:1]),
            waits=[t_sr2a, t_sr2b]))
    t_stT2 = S.add("vector", lambda e: e.tensor_copy(out=statT2[:, :], in_=pb0[:, 512:544]),
                   waits=tok_st2)
    st2_ap = statT2.ap().rearrange("p (tt j) -> p j tt", j=2)
    rm2 = []

    def vop2(emit, waits=()):
        tok = S.add("vector", emit, waits=list(waits) + [t_stT2])
        rm2.append(tok)
        return tok
    vop2(lambda e: e.tensor_scalar(out=rmv(0), in0=st2_ap[:, 0, :], scalar1=1.0 / 1024,
                                   scalar2=None, op0=A.mult))
    vop2(lambda e: e.tensor_scalar(out=rmv(1), in0=st2_ap[:, 1, :], scalar1=1.0 / 1024,
                                   scalar2=None, op0=A.mult))
    vop2(lambda e: e.tensor_tensor(out=rmv(2), in0=rmv(0), in1=rmv(0), op=A.mult))
    vop2(lambda e: e.tensor_tensor(out=rmv(3), in0=rmv(1), in1=rmv(2), op=A.subtract))
    vop2(lambda e: e.tensor_scalar(out=rmv(3), in0=rmv(3), scalar1=1e-5, scalar2=None, op0=A.add))
    vop2(lambda e: e.reciprocal(out=rmv(4), in_=rmv(3)))
    t_r2v = S.add("scalar", lambda e: e.activation(out=rmv(5), in_=rmv(4), func=AF.Sqrt),
                  waits=rm2[-1:])
    vop2(lambda e: e.tensor_tensor(out=rmv(6), in0=rmv(0), in1=rmv(5), op=A.mult),
         waits=[t_r2v])
    t_c2w = [S.add("vector", lambda e: e.tensor_copy(out=coefT2[:, :], in_=rmv(5)),
                   waits=rm2[-1:] + [t_r2v]),
             S.add("vector", lambda e: e.tensor_scalar(
                 out=coefT2b[:, :], in0=rmv(6), scalar1=-1.0, scalar2=None, op0=A.mult),
                 waits=rm2[-1:])]
    tok_c2t = [S.add("tensor", lambda e, tt=tt: e.transpose(
        pb1[0:1, tt * 128:(tt + 1) * 128], coefT2[:, tt:tt + 1], eye[:, :]),
        waits=t_c2w[:1] + [t_sr2b] + tok_Q1) for tt in range(16)]
    t_r2r = S.add("vector", lambda e: e.tensor_copy(out=r2row, in_=pb1[0:1, 0:2048]),
                  waits=tok_c2t)
    d_r2a = S.dma("sync", b_row2.ap()[:, :], r2row, waits=[t_r2r])
    t_r2sb = S.dma("sync", ar2[:, :], b_row2.ap().partition_broadcast(128),
                   waits=[d_r2a, t_asb] + tok_S1)
    tok_c2tb = [S.add("tensor", lambda e, tt=tt: e.transpose(
        pb1bf2[0:1, tt * 128:(tt + 1) * 128], coefT2b[:, tt:tt + 1],
        eyebf[:, :]), waits=t_c2w[1:2] + [t_r2r]) for tt in range(16)]
    t_b2r2 = S.add("vector", lambda e: e.tensor_copy(
        out=B2rhs[0:1, :], in_=pb1bf2[0:1, 0:2048]), waits=tok_c2tb)
    t_b2r1 = S.dma("sync", B2rhs[1:2, :], t_onesbf.ap()[:, :])
    tok_rb = [t_r2sb]
    t_b2l = []
    tok_b2t = []
    mps = rm.ap().rearrange("p n -> p n")[:, 216:232].rearrange("p (c j) -> p j c", j=2)
    t_mp1 = S.add("vector", lambda e: e.tensor_copy(out=mps[:, 0, :], in_=deriv[:, 8:16]),
                  waits=[t_p2])
    t_mp2 = S.add("vector", lambda e: e.tensor_copy(out=mps[:, 1, :], in_=modT[:, 24:32]),
                  waits=[t_modT])
    for c in range(NCH):
        tt_ = S.add("tensor", lambda e, c=c: e.transpose(
            pb1[0:2, 1152:1280], rm[:, 216 + c * 2:218 + c * 2], eye[:, :]),
            waits=[t_mp1, t_mp2, t_b2r2] + tok_rb + (tok_b2t[-1:] if tok_b2t else [t_sr2b] + tok_Q1))
        ev = S.add("vector", lambda e, c=c: e.tensor_copy(
            out=B2lhsT[:, c * 128:(c + 1) * 128], in_=pb1[0:2, 1152:1280]), waits=[tt_])
        tok_b2t.append(ev)
        t_b2l.append(ev)

    tok_amx = []
    t_h2 = {}
    prev_b2 = {}
    for c in range(NCH):
        pg = pb0 if c % 2 == 0 else pb1
        t_t1 = S.add("vector" if c % 2 == 0 else "gpsimd", lambda e, c=c: e.tensor_tensor(
            out=xch[c], in0=x1[c], in1=ar2[:, :], op=A.mult),
            waits=[t_r2sb, tok_res1[c], t_x1bf[c], t_x1sq[c]])
        mmb2 = [S.add("tensor", lambda e, c=c, n=n, pg=pg: e.matmul(
            pg[:, n * 512:(n + 1) * 512], B2lhsT[:, c * 128:(c + 1) * 128],
            B2rhs[:, n * 512:(n + 1) * 512], start=True, stop=True),
            waits=[t_b2l[c], t_b2r1, t_b2r2] +
                  ([prev_b2[c - 2]] if c >= 2 else [t_r2sb] + tok_b2t))
            for n in range(4)]
        t_h2[c] = S.add("vector", lambda e, c=c, pg=pg: e.scalar_tensor_tensor(
            out=xch[c], in0=xch[c], scalar=deriv[:, 8 + c:9 + c], in1=pg[:, :],
            op0=A.mult, op1=A.add), waits=mmb2 + [t_t1, t_p2])
        prev_b2[c] = t_h2[c]
        tok_amx.append(S.add("vector", lambda e, c=c: e.tensor_reduce(
            out=amx8[:, c:c + 1], in_=xch[c], axis=AX.X, op=A.max,
            apply_absolute_value=True), waits=[t_h2[c]]))
        emit_wred(c)
        if c + 3 < NCH:
            emit_wf1_dma(c + 3)
    t_wt = S.add("tensor", lambda e: e.transpose(
        pb1[0:1, 1280:1408], rm[:, WACC:WACC + 1], eye[:, :]),
        waits=[wacc_tok[0], t_h2[7]])
    t_wr = S.add("vector", lambda e: e.tensor_copy(out=prow[:, :], in_=pb1[0:1, 1280:1408]),
                 waits=[t_wt])
    t_wl_ = S.add("vector", lambda e: e.tensor_reduce(
        out=s11[0:1, 6:7], in_=prow[:, :], axis=AX.X, op=A.max), waits=[t_wr])
    t_ws1 = S.add("vector", lambda e: e.tensor_scalar(
        out=s11[0:1, 6:7], in0=s11[0:1, 6:7], scalar1=1e-12, scalar2=None, op0=A.max),
        waits=[t_wl_])
    t_ws2 = S.add("vector", lambda e: e.reciprocal(out=s11[0:1, 7:8], in_=s11[0:1, 6:7]),
                  waits=[t_ws1])
    t_ws3 = S.add("vector", lambda e: e.tensor_scalar(
        out=s11[0:1, 7:8], in0=s11[0:1, 7:8], scalar1=448.0, scalar2=448.0,
        op0=A.mult, op1=A.min), waits=[t_ws2])
    t_ws4 = S.add("vector", lambda e: e.reciprocal(out=s11[0:1, 8:9], in_=s11[0:1, 7:8]),
                  waits=[t_ws3])
    t_ws5 = S.add("vector", lambda e: e.tensor_scalar(
        out=s11[0:1, 9:10], in0=s11[0:1, 7:8], scalar1=0.5, scalar2=None, op0=A.mult),
        waits=[t_ws3])
    t_sw2b_mm = S.add("tensor", lambda e: e.matmul(
        pb1[:, 1408:1409], ones_r[:, :], s11[0:1, 9:10], start=True, stop=True),
        waits=[t_ws5, t_ones, t_wt])
    t_sw2b = S.add("vector", lambda e: e.tensor_copy(out=sgc("sw2"), in_=pb1[:, 1408:1409]),
                   waits=[t_sw2b_mm])
    t_amc = S.add("vector", lambda e: e.tensor_reduce(
        out=amxc[:, :], in_=amx8[:, :], axis=AX.X, op=A.max), waits=tok_amx)
    t_amt = S.add("tensor", lambda e: e.transpose(pb0[0:1, 0:128], amxc[:, :], eye[:, :]),
                  waits=[t_amc] + [prev_b2[c] for c in range(NCH)])
    t_amr = S.add("vector", lambda e: e.tensor_copy(out=prow[:, :], in_=pb0[0:1, 0:128]),
                  waits=[t_amt])
    t_aml = S.add("vector", lambda e: e.tensor_reduce(
        out=s11[0:1, 0:1], in_=prow[:, :], axis=AX.X, op=A.max), waits=[t_amr])
    d_ami = S.dma("sync", b_am_i.ap()[:, :], s11[0:1, 0:1], waits=[t_aml, t_z11])
    tok_wq = []
    prev_q = []
    for c in range(NCH):
        bw = ([wred_prev[5]] if c == 1 else
              (prevB5[4] if c == 0 else []))
        d = S.dma("scalar", P2R[c % 2], t_wfT.ap()[c * 128:(c + 1) * 128, :],
                  waits=bw + (prev_q[c - 2:c - 1] if c >= 2 else []))
        tq = S.add("vector", lambda e, c=c: e.tensor_scalar(
            out=wq8[:, c * 1024:(c + 1) * 1024], in0=P2R[c % 2], scalar1=sgc("sw2"),
            scalar2=None, op0=A.mult), waits=[d, t_sw2b])
        prev_q.append(tq)
        tok_wq.append(tq)
    cc_am = S.add("gpsimd", lambda e: e.collective_compute(
        "AllReduce", A.max, replica_groups=[list(range(B))],
        ins=[b_am_i.ap().opt()], outs=[b_am_o.ap().opt()]),
        waits=[d_ami, cc_mod], sem="cc", inc=1)
    t_amg = S.dma("sync", s11[0:1, 1:2], b_am_o.ap()[:, :], waits=[cc_am])

    t_sc1 = S.add("vector", lambda e: e.tensor_scalar(
        out=s11[0:1, 1:2], in0=s11[0:1, 1:2], scalar1=1e-12, scalar2=None, op0=A.max),
        waits=[t_amg])
    t_sc2_ = S.add("vector", lambda e: e.reciprocal(out=s11[0:1, 2:3], in_=s11[0:1, 1:2]),
                   waits=[t_sc1])
    t_sc3 = S.add("vector", lambda e: e.tensor_scalar(
        out=s11[0:1, 2:3], in0=s11[0:1, 2:3], scalar1=448.0, scalar2=448.0,
        op0=A.mult, op1=A.min), waits=[t_sc2_])
    t_sc4 = S.add("vector", lambda e: e.reciprocal(out=s11[0:1, 3:4], in_=s11[0:1, 2:3]),
                  waits=[t_sc3])
    t_sc5_ = S.add("vector", lambda e: e.tensor_scalar(
        out=s11[0:1, 4:5], in0=s11[0:1, 2:3], scalar1=0.5, scalar2=None, op0=A.mult),
        waits=[t_sc3])
    t_dq1 = S.add("vector", lambda e: e.tensor_tensor(
        out=s11[0:1, 10:11], in0=s11[0:1, 3:4], in1=s11[0:1, 8:9], op=A.mult),
        waits=[t_sc4, t_ws4])
    t_dq2 = S.add("vector", lambda e: e.tensor_scalar(
        out=s11[0:1, 5:6], in0=s11[0:1, 10:11], scalar1=4.0, scalar2=None, op0=A.mult),
        waits=[t_dq1])
    t_xsb_mm = S.add("tensor", lambda e: e.matmul(
        pb0[:, 140:141], ones_r[:, :], s11[0:1, 4:5], start=True, stop=True),
        waits=[t_sc5_, t_amr])
    t_dqb_mm = S.add("tensor", lambda e: e.matmul(
        pb0[:, 141:142], ones_r[:, :], s11[0:1, 5:6], start=True, stop=True),
        waits=[t_dq2, t_amr])
    t_xsb = S.add("vector", lambda e: e.tensor_copy(out=sgc("xs2"), in_=pb0[:, 140:141]),
                  waits=[t_xsb_mm])
    t_dqb = S.add("vector", lambda e: e.tensor_copy(out=sgc("deq"), in_=pb0[:, 141:142]),
                  waits=[t_dqb_mm])

    r2bq = r2b.bitcast(FP8)
    wfSq = wfS.bitcast(FP8)
    qdst = [r2bq[:, k * 2048:(k + 1) * 2048] for k in range(4)] +            [wfSq[:, k * 2048:(k + 1) * 2048] for k in range(2)]
    tok_q8 = []
    for c in range(NCH):
        if c < 4:
            extra = tok_Q1[-1:]
        else:
            extra = [tok_wq[7]]
        if c < 6:
            tok_q8.append(S.add("vector", lambda e, c=c: e.tensor_scalar(
                out=qdst[c], in0=xch[c], scalar1=sgc("xs2"),
                scalar2=None, op0=A.mult),
                waits=[t_h2[c], t_xsb, tok_amx[c]] + extra))
        else:
            tq = S.add("vector", lambda e, c=c: e.tensor_scalar(
                out=q8tmp[:, :], in0=xch[c], scalar1=sgc("xs2"),
                scalar2=None, op0=A.mult),
                waits=[t_h2[c], t_xsb, tok_amx[c], tok_wq[6]])
            tok_q8.append(S.add("vector", lambda e, c=c: e.tensor_copy(
                out=q8[c], in_=q8tmp[:, :]), waits=[tq]))

    wq8_3d = wq8.ap().rearrange("p (k m) -> p k m", k=NCH)
    q8_3d = r1.ap().bitcast(FP8).rearrange("p (c x) -> p c x", c=NCH)
    r2b_3d = r2b.bitcast(FP8).ap().rearrange("p (c x) -> p c x", c=4)
    wfS_3d = wfS.bitcast(FP8).ap().rearrange("p (c x) -> p c x", c=2)
    rhs_pairs = [r2b_3d[:, 0:2, :], r2b_3d[:, 2:4, :],
                 wfS_3d[:, 0:2, :], q8_3d[:, 6:8, 0:2048]]
    fb_col = lambda m: vcat[:, 16 + m:17 + m]
    prev_e1 = {}
    prev_x2 = {}
    tok_out = []
    for m in range(NCH):
        pg = pb0 if m % 2 == 0 else pb1
        mm_w0 = ([prev_e1[m - 2]] if m >= 2
                 else [t_xsb, t_dqb] + ([t_wt, t_sw2b] if m == 1 else []))
        mms = []
        for n in range(4):
            for kp in range(NCH // 2):
                mms.append(S.add("tensor", lambda e, m=m, n=n, kp=kp, pg=pg: e.matmul(
                    pg[:, n * 512:(n + 1) * 512],
                    wq8_3d[:, 2 * kp:2 * kp + 2, m * 128:(m + 1) * 128],
                    rhs_pairs[kp][:, :, n * 512:(n + 1) * 512],
                    start=(kp == 0), stop=(kp == NCH // 2 - 1),
                    perf_mode=mybir.MatmulPerfMode.DoubleRow),
                    waits=[tok_wq[2 * kp], tok_wq[2 * kp + 1], tok_q8[2 * kp],
                           tok_q8[2 * kp + 1]] + (mm_w0 if (kp == 0 and n == 0) else [])))
        t_e1 = S.add("scalar", lambda e, m=m, pg=pg: e.activation(
            out=e1P[:, :], in_=pg[:, :], func=AF.Identity,
            scale=sgc("deq"), bias=fb_col(m)),
            waits=mms + [t_dqb] + ([prev_x2[m - 1]] if m >= 1 else []))
        prev_e1[m] = t_e1
        t_x2 = S.add("vector", lambda e, m=m: e.scalar_tensor_tensor(
            out=x1[m], in0=e1P[:, :], scalar=deriv[:, 24 + m:25 + m],
            in1=x1[m], op0=A.mult, op1=A.add),
            waits=[t_e1, t_gb, tok_q8[m]])
        prev_x2[m] = t_x2
        tok_out.append(S.dma("sync", t_out.ap()[m * 128:(m + 1) * 128, :], x1[m],
                             waits=[t_x2]))

    if dbg:
        tok_out.append(S.dma("sync", dbg_outs["d_modT"].ap()[:, :], modT[:, :], waits=[t_modT]))
        tok_out.append(S.dma("sync", dbg_outs["d_statT"].ap()[:, :], statT[:, :], waits=[t_stT]))
        tok_out.append(S.dma("sync", dbg_outs["d_coefT"].ap()[:, :], coefT[:, :], waits=t_co))
        tok_out.append(S.dma("sync", dbg_outs["d_kv"].ap()[:, :], kvf[:, :], waits=[t_kc]))
        tok_out.append(S.dma("sync", dbg_outs["d_sg"].ap()[:, :], sg_[:, :], waits=[t_sP1, t_sP12, t_ssan, t_xsb, t_dqb, t_sw2b]))
        tok_out.append(S.dma("sync", dbg_outs["d_q8u"].ap()[:, :], q8[0].bitcast(mybir.dt.uint8),
                             waits=[tok_q8[0]] + [prev_x2[m] for m in range(NCH)]))
        tok_out.append(S.dma("sync", dbg_outs["d_wq8u"].ap()[:, :], wq8[:, 0:1024].bitcast(mybir.dt.uint8),
                             waits=[tok_wq[0]]))
        tok_out.append(S.dma("sync", dbg_outs["d_gt0"].ap()[:, :], gt0, waits=[t_clip]))
        tok_out.append(S.dma("sync", dbg_outs["d_gt1"].ap()[:, :], gt1, waits=t_gt1))
        tok_out.append(S.dma("sync", dbg_outs["d_x1"].ap()[:, :], x1[0], waits=[tok_res1[0]]))
        tok_out.append(S.dma("sync", dbg_outs["d_h2"].ap()[:, :], xch[0], waits=[t_h2[0]]))
        tok_out.append(S.dma("sync", dbg_outs["d_scal"].ap()[:, :], s11[:, :],
                             waits=[t_dq2, t_sc5_, t_ws5]))

    S.add("sync", lambda e: e.nop(), waits=tok_out, sem="dI", inc=1)

    with es:
        sems = {nm: es.enter_context(nc.semaphore(nm))
                for nm in sorted(set(list(S.cnt.keys()) + ["dI"]))}
        block = es.enter_context(nc.Block())
        S.emit(block, sems)
    return nc


def kernel(L0_attn_x, L0_attn_ln_w, L0_attn_ln_b, L0_attn_Wi, L0_attn_bi,
           L0_attn_Wf, L0_attn_bf, L0_attn_Wz, L0_attn_bz, L0_attn_Wo,
           L0_attn_bo, L0_attn_Wp, L0_attn_bp, L0_ffn_weight, L0_ffn_bias,
           L0_c, L0_wmod):
    f = lambda a: np.ascontiguousarray(np.asarray(a, dtype=np.float32))
    x = f(L0_attn_x)
    wall = np.concatenate([f(L0_attn_Wi), f(L0_attn_Wf), f(L0_attn_Wz), f(L0_attn_Wo)], 0)
    wallT = np.ascontiguousarray(
        wall.T.reshape(8, 128, 256).transpose(1, 0, 2).reshape(128, 2048))
    bvec = np.concatenate([f(L0_attn_bi), f(L0_attn_bf), f(L0_attn_bz), f(L0_attn_bo)], 0)
    bvec2 = np.ascontiguousarray(bvec.reshape(2, 128).T)
    wpT = np.ascontiguousarray(f(L0_attn_Wp).T)
    bprow = f(L0_attn_bp).reshape(1, D)
    wfT = np.ascontiguousarray(f(L0_ffn_weight).T)
    vcat = np.ascontiguousarray(np.concatenate(
        [f(L0_attn_ln_w).reshape(8, 128).T, f(L0_attn_ln_b).reshape(8, 128).T,
         f(L0_ffn_bias).reshape(8, 128).T], 1))
    cmat = f(L0_c)
    wmod = f(L0_wmod)
    eye = np.eye(128, dtype=np.float32)
    eyebf = np.eye(128, dtype=ml_dtypes.bfloat16)
    onesbf = np.ones((1, T), dtype=ml_dtypes.bfloat16)

    nc = build_program(dbg=_DBG)
    in_maps = [{
        "xT": np.ascontiguousarray(x[b].T),
        "wallT": wallT, "bvec2": bvec2, "wpT": wpT, "bprow": bprow,
        "wfT": wfT, "vcat": vcat, "cmat": cmat,
        "wmod_sl": np.ascontiguousarray(wmod[:, b * JW:(b + 1) * JW]),
        "eye": eye, "eyebf": eyebf, "onesbf": onesbf,
    } for b in range(B)]
    res = run_bass_kernel_spmd(nc, in_maps, core_ids=list(range(B)))
    out = np.stack([np.asarray(res.results[b]["outT"]).T for b in range(B)], 0)
    if _DBG:
        kernel._dbg = [res.results[b] for b in range(B)]
    return out


# version 29 (final)
# speedup vs baseline: 1.0526x; 1.0055x over previous
import contextlib
import numpy as np
import ml_dtypes
import concourse.bass as bass
import concourse.mybir as mybir
from concourse.bass_utils import run_bass_kernel_spmd
from concourse.alu_op_type import AluOpType as A
from concourse.aot_env import allow_ici_all_to_all

F32 = mybir.dt.float32
BF16 = mybir.dt.bfloat16
FP8 = mybir.dt.float8e4
AF = mybir.ActivationFunctionType
AX = mybir.AxisListType

B, T, D, NH = 8, 2048, 1024, 64
NCH = 8
TH = T // 2
JW = 6144 // B

_DBG = False

_ENGINES = ("sync", "gpsimd", "vector", "scalar", "tensor")


class Sch:

    def __init__(self):
        self.steps = {e: [] for e in _ENGINES}
        self.cnt = {}

    def add(self, eng, emit, waits=(), sem=None, inc=1):
        if sem is None:
            sem = {"sync": "dI", "gpsimd": "sG", "vector": "sV",
                   "scalar": "sS", "tensor": "sT"}[eng]
        v = self.cnt.get(sem, 0) + inc
        self.cnt[sem] = v
        self.steps[eng].append((emit, tuple(waits), sem, inc))
        return (sem, v)

    def dma(self, eng, out, in_, waits=(), sem=None):
        if sem is None:
            self._dmai = getattr(self, "_dmai", 0) + 1
            sem = "dma%d" % self._dmai
        return self.add(eng, lambda e, o=out, i=in_: e.dma_start(out=o, in_=i),
                        waits=waits, sem=sem, inc=16)

    def emit(self, block, sems):
        own = {"vector": "sV", "scalar": "sS", "gpsimd": "sG"}
        for eng in _ENGINES:
            if not self.steps[eng]:
                continue

            def section(e, eng=eng):
                seen = {}
                mysem = own.get(eng)
                cnt = 0
                for emitfn, waits, semkey, inc in self.steps[eng]:
                    if mysem is not None and semkey == mysem and cnt > 0:
                        waits = tuple(waits) + ((mysem, cnt),)
                    for sk, val in waits:
                        if seen.get(sk, 0) < val:
                            e.wait_ge(sems[sk], val)
                            seen[sk] = val
                    inst = emitfn(e)
                    if inst is not None:
                        inst.then_inc(sems[semkey], inc)
                        if semkey == mysem:
                            cnt += inc
            getattr(block, eng)(section)


def build_program(dbg=False):
    nc = bass.Bass("TRN2", target_bir_lowering=False, detect_race_conditions=False)

    def din(name, shape, dt=F32):
        return nc.dram_tensor(name, list(shape), dt, kind="ExternalInput")

    t_xT = din("xT", (D, T))
    t_wallT = din("wallT", (128, 2048))
    t_bvec2 = din("bvec2", (128, 2))
    t_wpT = din("wpT", (NH, D))
    t_bprow = din("bprow", (1, D))
    t_wfT = din("wfT", (D, D))
    t_vcat = din("vcat", (128, 24))
    t_cmat = din("cmat", (B, D))
    t_wmod = din("wmod_sl", (D, JW))
    t_eye = din("eye", (128, 128))
    t_eyebf = din("eyebf", (128, 128), BF16)
    t_onesbf = din("onesbf", (1, T), BF16)
    t_out = nc.dram_tensor("outT", [D, T], F32, kind="ExternalOutput")

    b_md_i = nc.dram_tensor("b_md_i", [B, JW], F32)
    b_md_o = nc.dram_tensor("b_md_o", [B, JW], F32)
    b_am_i = nc.dram_tensor("b_am_i", [1, 1], F32)
    b_row = nc.dram_tensor("b_row", [1, 2048], F32)
    b_row2 = nc.dram_tensor("b_row2", [1, 2048], F32)
    b_sc = nc.dram_tensor("b_sc", [1, 8], F32)
    b_am_o = nc.dram_tensor("b_am_o", [1, 1], F32)

    dbg_outs = {}
    if dbg:
        for nm, shp, dt in [("d_modT", [128, 48], F32), ("d_statT", [128, 96], F32),
                            ("d_coefT", [128, 48], BF16), ("d_gt0", [128, T], F32),
                            ("d_gt1", [128, T], F32),
                            ("d_x1", [128, T], F32), ("d_h2", [128, T], F32),
                            ("d_scal", [1, 16], F32), ("d_kv", [128, 8], F32),
                            ("d_sg", [128, 16], F32)]:
            dbg_outs[nm] = nc.dram_tensor(nm, shp, dt, kind="ExternalOutput")
        for nm, shp, dt in [("d_q8u", [128, 2048], mybir.dt.uint8),
                            ("d_wq8u", [128, 1024], mybir.dt.uint8)]:
            dbg_outs[nm] = nc.dram_tensor(nm, shp, dt, kind="ExternalOutput")

    es = contextlib.ExitStack()
    sbt = lambda name, shape, dt=F32: es.enter_context(nc.sbuf_tensor(name, list(shape), dt))
    pst = lambda name, shape, dt=F32: es.enter_context(nc.psum_tensor(name, list(shape), dt))

    r1 = sbt("r1", (128, 16384))
    r2 = sbt("r2", (128, 16384))
    r2b = sbt("r2b", (128, 2048))
    saA = es.enter_context(nc.sbuf_tensor("saA", [64, 6144], F32))
    ar2 = sbt("ar2", (128, 2048))

    wallT = [r2[:, 2048 + c * 256: 2048 + (c + 1) * 256] for c in range(NCH)]
    wl = [r2[:, 4096 + c * 256: 4096 + (c + 1) * 256] for c in range(NCH)]
    w3bf = [r2[:, 6144 + c * 128: 6144 + (c + 1) * 128].bitcast(BF16) for c in range(NCH)]
    gt0 = r2[:, 7168:9216]
    sqbfP = [r2[:, 9472:10496].bitcast(BF16), r2[:, 10496:11520].bitcast(BF16)]
    cmat_v = r2[0:B, 12800:13824]
    silu_v = r2[0:B, 14336:15360]
    silu_ch = lambda c: r2[0:B, 14336 + c * 128:14336 + (c + 1) * 128]
    wpT_v = r2[0:NH, 11776:12800]
    modsb8 = r2[0:B, 15360:15360 + JW]
    xbfP = [r2[:, 13312:14336].bitcast(BF16), r2[:, 14336:15360].bitcast(BF16)]
    x1 = [r2[:, c * 2048:(c + 1) * 2048] for c in range(NCH)]

    xch = [r1[:, c * 2048:(c + 1) * 2048] for c in range(NCH)]
    q8 = [r1[:, c * 2048: c * 2048 + 512].bitcast(FP8) for c in range(NCH)]

    gt1 = r2b[:, 0:2048]
    sqbf2P = [r2b[:, 0:1024].bitcast(BF16), r2b[:, 1024:2048].bitcast(BF16)]
    xbf2P = [ar2[:, 0:1024].bitcast(BF16), ar2[:, 1024:2048].bitcast(BF16)]

    sl_li, sl_lf, sl_z, sl_o, sl_m, sl_mp = (saA[0:64, k * 1024:(k + 1) * 1024]
                                             for k in range(6))
    statrowsA = saA[0:4, 0:2048]
    statrowsB = saA[0:2, 2048:4096]
    statrows2a = saA[0:1, 0:2048]
    statrows2b = saA[0:1, 2048:4096]
    arow = saA[0:1, 4096:6144]
    r2row = saA[0:1, 4096:6144]

    h65 = sbt("h65", (65, T), BF16)
    lhsT65 = sbt("lhsT65", (65, D), BF16)
    rhs9 = sbt("rhs9", (4, T), BF16)
    lhsT9 = sbt("lhsT9", (4, 256), BF16)
    B2rhs = sbt("B2rhs", (2, T), BF16)
    B2lhsT = sbt("B2lhsT", (2, 1024), BF16)
    k8 = sbt("k8", (8, 128), BF16)
    kvbf = sbt("kvbf", (128, 8), BF16)
    sT32 = sbt("sT32", (128, 64))
    wstatA = sbt("wstatA", (128, 32), BF16)
    wstatB = sbt("wstatB", (128, 16), BF16)
    wq8 = sbt("wq8", (128, 8192), FP8)
    e1P = sbt("e1P", (128, 2048), BF16)
    wfS = sbt("wfS", (128, 1024))
    onecbf = sbt("onecbf", (128, 1), BF16)

    vcat = sbt("vcat_s", (128, 24))
    bvec2 = sbt("bvec2_s", (128, 2))
    modT = sbt("modT", (128, 48))
    modsb48 = sbt("modsb48", (48, 128))
    deriv = sbt("deriv", (128, 64))
    Vs = sbt("Vs", (128, 40))
    statT = sbt("statT", (128, 96))
    statT2 = sbt("statT2", (128, 32))
    coefT = sbt("coefT", (128, 48), BF16)
    coefT2 = sbt("coefT2", (128, 16))
    coefT2b = sbt("coefT2b", (128, 16), BF16)
    coefTa = sbt("coefTa", (128, 16))
    rm = sbt("rm", (128, 256))
    sg_ = sbt("sg_", (128, 16))
    kvf = sbt("kvf", (128, 8))
    kvr = sbt("kvr", (128, 24))
    sc5 = sbt("sc5", (5, 1))
    amx8 = sbt("amx8", (128, 8))
    amxc = sbt("amxc", (128, 1))
    prow = sbt("prow", (1, 128))
    s11 = sbt("s11", (1, 16))
    ones_r = sbt("ones_r", (1, 128))
    ones_c = sbt("ones_c", (128, 1))
    eye = sbt("eye_s", (128, 128))
    eyebf = sbt("eyebf_s", (128, 128), BF16)

    pb0 = pst("pb0", (128, 2048))
    pb1 = pst("pb1", (128, 2048))
    pb1bf = pb1[:, 0:1024].bitcast(BF16)
    pb1bf2 = pb1[:, 1024:2048].bitcast(BF16)

    S = Sch()
    rmv = lambda k: rm[:, k * 16:(k + 1) * 16]
    MCAR, CCAR, MLAST, B2S, WAMA, WAMB, WACC = 208, 209, 210, 211, 213, 214, 215
    SGC = {"sca": 0, "sc2": 1, "spsa": 2, "ssan": 3, "ssa2": 4, "sP1": 5, "sP12": 6,
           "xs2": 7, "deq": 8, "sw2": 9}
    sgc = lambda k: sg_[:, SGC[k]:SGC[k] + 1]

    d_cmat = S.dma("sync", cmat_v, t_cmat.ap()[:, :])
    d_eye = S.dma("sync", eye[:, :], t_eye.ap()[:, :])
    d_wpT = S.dma("sync", wpT_v, t_wpT.ap()[:, :])
    d_bprow = S.dma("sync", saA[0:1, 4096:5120], t_bprow.ap()[:, :])
    wmpark = [r1[:, 10240 + c * 768: 10240 + (c + 1) * 768] for c in range(NCH)]
    d_wmpark = [S.dma("scalar", wmpark[c], t_wmod.ap()[c * 128:(c + 1) * 128, :])
                for c in range(NCH)]
    d_eyebf = S.dma("sync", eyebf[:, :], t_eyebf.ap()[:, :])
    d_vcat = S.dma("sync", vcat[:, :], t_vcat.ap()[:, :])
    d_bvec = S.dma("sync", bvec2[:, :], t_bvec2.ap()[:, :])
    d_x = [S.dma("scalar", xch[c], t_xT.ap()[c * 128:(c + 1) * 128, :]) for c in range(5)]
    d_wall1 = S.dma("scalar", r2[:, 2048:4096], t_wallT.ap()[:, :])
    d_wall = [d_wall1] * NCH

    t_ones = S.add("vector", lambda e: e.memset(ones_r[:, :], 1.0))
    t_ones2 = S.add("vector", lambda e: e.memset(ones_c[:, :], 1.0))
    t_ones3 = S.add("vector", lambda e: e.memset(onecbf[:, :], 1.0))
    t_z11 = S.add("vector", lambda e: e.memset(s11[:, :], 0.0))
    t_zsg = S.add("vector", lambda e: e.memset(sg_[:, :], 0.0))
    t_silu = S.add("scalar", lambda e: e.activation(out=silu_v, in_=cmat_v, func=AF.Silu),
                   waits=[d_cmat])
    tok_tr = [S.add("tensor", lambda e, c=c: e.transpose(
        pb0[:, 1024 + c * 8:1024 + (c + 1) * 8], silu_ch(c), eye[0:B, 0:B]),
        waits=[t_silu, d_eye]) for c in range(NCH)]
    t_sT = S.add("vector", lambda e: e.tensor_copy(out=sT32[:, :], in_=pb0[:, 1024:1088]),
                 waits=tok_tr)
    mmod = {}
    tok_mod = []
    for c in range(NCH):
        mmod[c] = [S.add("tensor", lambda e, c=c, n0=n0, n1=n1: e.matmul(
            pb0[0:B, n0:n1], sT32[:, c * 8:(c + 1) * 8], wmpark[c][:, n0:n1],
            start=(c == 0), stop=(c == NCH - 1)),
            waits=[d_wmpark[c], t_sT]) for (n0, n1) in ((0, 512), (512, JW))]
        tok_mod = tok_mod + mmod[c]
    d_x.append(S.dma("scalar", xch[5], t_xT.ap()[5 * 128:6 * 128, :],
                     waits=mmod[0] + mmod[1] + mmod[2]))
    d_x.append(S.dma("scalar", xch[6], t_xT.ap()[6 * 128:7 * 128, :],
                     waits=mmod[2] + mmod[3] + mmod[4] + mmod[5]))
    d_x.append(S.dma("scalar", xch[7], t_xT.ap()[7 * 128:8 * 128, :],
                     waits=mmod[5] + mmod[6] + mmod[7]))
    t_modev = S.add("vector", lambda e: e.tensor_copy(out=modsb8, in_=pb0[0:B, 0:JW]),
                    waits=tok_mod)
    t_l65a = S.add("vector", lambda e: e.tensor_copy(out=lhsT65[0:NH, :], in_=wpT_v),
                   waits=[d_wpT])
    t_l65b = S.add("vector", lambda e: e.tensor_copy(out=lhsT65[64:65, :],
                                                     in_=saA[0:1, 4096:5120]), waits=[d_bprow])
    d_mdi = S.dma("sync", b_md_i.ap()[:, :], modsb8, waits=[t_modev])

    def _cc_mod(e):
        with allow_ici_all_to_all():
            return e.collective_compute(
                "AllToAll", A.bypass, replica_groups=[list(range(B))],
                ins=[b_md_i.ap().opt()], outs=[b_md_o.ap().opt()])
    cc_mod = S.add("gpsimd", _cc_mod, waits=[d_mdi], sem="cc", inc=1)
    d_mdo = S.dma("sync", modsb48[:, :],
                  b_md_o.ap().rearrange("r (s e) -> (r s) e", e=128), waits=[cc_mod])
    t_modtr = S.add("tensor", lambda e: e.transpose(
        pb1[:, 0:48], modsb48[:, :], eye[0:48, 0:48]), waits=[d_mdo, d_eye])
    t_modT = S.add("vector", lambda e: e.tensor_copy(out=modT[:, :], in_=pb1[:, 0:48]),
                   waits=[t_modtr])

    t_p1 = S.add("vector", lambda e: e.tensor_scalar(
        out=deriv[:, 0:8], in0=modT[:, 8:16], scalar1=1.0, scalar2=None, op0=A.add),
        waits=[t_modT])
    t_p2 = S.add("vector", lambda e: e.tensor_scalar(
        out=deriv[:, 8:16], in0=modT[:, 32:40], scalar1=1.0, scalar2=None, op0=A.add),
        waits=[t_modT])

    def _gab(dst, src):
        t2 = S.add("vector", lambda e: e.tensor_scalar(
            out=deriv[:, dst:dst + 8], in0=modT[:, src:src + 8], scalar1=0.0, scalar2=2.0,
            op0=A.is_ge, op1=A.mult), waits=[t_modT])
        t3 = S.add("vector", lambda e: e.tensor_scalar(
            out=deriv[:, dst:dst + 8], in0=deriv[:, dst:dst + 8], scalar1=-1.0, scalar2=None,
            op0=A.add), waits=[t2])
        t4 = S.add("vector", lambda e: e.tensor_tensor(
            out=deriv[:, 56:64], in0=deriv[:, dst:dst + 8], in1=modT[:, src:src + 8],
            op=A.mult), waits=[t3])
        t5 = S.add("vector", lambda e: e.tensor_scalar(
            out=deriv[:, 56:64], in0=deriv[:, 56:64], scalar1=0.25, scalar2=None,
            op0=A.max), waits=[t4])
        return S.add("vector", lambda e: e.tensor_tensor(
            out=deriv[:, dst:dst + 8], in0=deriv[:, dst:dst + 8], in1=deriv[:, 56:64],
            op=A.mult), waits=[t5])
    t_ga = _gab(16, 16)
    t_gb = _gab(24, 40)
    t_psa = S.add("vector", lambda e: e.tensor_tensor(
        out=deriv[:, 32:40], in0=deriv[:, 0:8], in1=modT[:, 0:8], op=A.mult),
        waits=[t_p1, t_ga])
    t_c2a = S.add("vector", lambda e: e.tensor_scalar(
        out=deriv[:, 40:48], in0=modT[:, 8:16], scalar1=2.0, scalar2=None, op0=A.add),
        waits=[t_modT])
    t_c2 = S.add("vector", lambda e: e.tensor_tensor(
        out=deriv[:, 40:48], in0=deriv[:, 40:48], in1=modT[:, 8:16], op=A.mult),
        waits=[t_c2a])
    t_sa2 = S.add("vector", lambda e: e.tensor_tensor(
        out=deriv[:, 48:56], in0=modT[:, 0:8], in1=modT[:, 0:8], op=A.mult),
        waits=[t_modT])

    wsA_j = wstatA.ap().rearrange("p (c j) -> p j c", j=4)
    wsB_j = wstatB.ap().rearrange("p (c j) -> p j c", j=2)
    t_ws = [S.add("vector", lambda e: e.memset(wstatA[:, :], 1.0)),
            S.add("vector", lambda e: e.memset(wstatB[:, :], 1.0))]
    t_ws.append(S.add("vector", lambda e: e.tensor_copy(out=wsA_j[:, 1, :], in_=modT[:, 8:16]),
                      waits=[t_modT]))
    t_ws.append(S.add("vector", lambda e: e.tensor_copy(out=wsA_j[:, 2, :], in_=deriv[:, 40:48]),
                      waits=[t_c2]))
    t_ws.append(S.add("vector", lambda e: e.tensor_copy(out=wsA_j[:, 3, :], in_=deriv[:, 32:40]),
                      waits=[t_psa]))
    t_ws.append(S.add("vector", lambda e: e.tensor_copy(out=wsB_j[:, 1, :], in_=deriv[:, 40:48]),
                      waits=[t_c2]))
    vs_j = Vs.ap().rearrange("p (c j) -> p j c", j=5)
    t_vs = [S.add("vector", lambda e, j=j, srcv=srcv: e.tensor_copy(out=vs_j[:, j, :], in_=srcv),
                  waits=[wtok])
            for j, (srcv, wtok) in enumerate([(modT[:, 8:16], t_modT), (deriv[:, 40:48], t_c2),
                                              (deriv[:, 32:40], t_psa), (modT[:, 0:8], t_modT),
                                              (deriv[:, 48:56], t_sa2)])]

    xbufs = [xbfP[0], xbfP[1],
             r2[:, 7168:8192].bitcast(BF16), r2[:, 8192:9216].bitcast(BF16),
             r2b[:, 0:1024].bitcast(BF16), r2b[:, 1024:2048].bitcast(BF16),
             r2[:, 0:1024].bitcast(BF16), r2[:, 1024:2048].bitcast(BF16)]
    sqbufs = [sqbfP[0], sqbfP[1],
              ar2[:, 0:1024].bitcast(BF16), ar2[:, 1024:2048].bitcast(BF16),
              e1P[:, :]]
    t_xbf, t_sq = {}, {}
    prevB5 = {}
    for c in range(NCH):
        t_xbf[c] = S.add("vector", lambda e, c=c: e.tensor_copy(
            out=xbufs[c], in_=xch[c]), waits=[d_x[c]] + (tok_tr if c < 2 else []))
    for c in range(5):
        t_sq[c] = S.add("scalar", lambda e, c=c: e.activation(
            out=sqbufs[c % 5], in_=xch[c], func=AF.Square), waits=[d_x[c]])

    t_wl = [S.add("vector", lambda e, c=c: e.tensor_scalar(
        out=wl[c], in0=wallT[c], scalar1=vcat[:, c:c + 1], scalar2=None, op0=A.mult),
        waits=[d_wall[c], d_vcat]) for c in range(NCH)]
    t_w3 = [S.add("vector", lambda e, c=c: e.tensor_scalar(
        out=w3bf[c], in0=wl[c], scalar1=deriv[:, c:c + 1], scalar2=None, op0=A.mult),
        waits=[t_wl[c], t_p1]) for c in range(NCH)]

    kvr_j = kvr.ap().rearrange("p (c j) -> p j c", j=3)
    t_kvr = [S.add("vector", lambda e: e.memset(kvr[:, :], 1.0))]
    t_kvr.append(S.add("vector", lambda e: e.tensor_copy(out=kvr_j[:, 0, :], in_=modT[:, 0:8]),
                       waits=[t_modT]))
    t_kvr.append(S.add("vector", lambda e: e.tensor_copy(out=kvr_j[:, 1, :], in_=deriv[:, 0:8]),
                       waits=[t_p1]))

    tok_A, tok_B = [], []
    for c in range(NCH):
        if c >= 5:
            t_sq[c] = S.add("scalar", lambda e, c=c: e.activation(
                out=sqbufs[c % 5], in_=xch[c], func=AF.Square),
                waits=[d_x[c]] + prevB5[c - 5])
        mmA = [S.add("tensor", lambda e, c=c, n=n: e.matmul(
            pb0[0:4, n * 512:(n + 1) * 512], wstatA[:, c * 4:(c + 1) * 4],
            xbufs[c][:, n * 512:(n + 1) * 512], start=(c == 0), stop=(c == NCH - 1)),
            waits=[t_xbf[c]] + t_ws) for n in range(4)]
        mmB = [S.add("tensor", lambda e, c=c, n=n: e.matmul(
            pb1[0:2, n * 512:(n + 1) * 512], wstatB[:, c * 2:(c + 1) * 2],
            sqbufs[c % 5][:, n * 512:(n + 1) * 512], start=(c == 0), stop=(c == NCH - 1)),
            waits=[t_sq[c]] + t_ws) for n in range(4)]
        prevB5[c] = mmB
        tok_A += mmA
        tok_B += mmB
    t_srA = S.add("vector", lambda e: e.tensor_copy(out=statrowsA, in_=pb0[0:4, :]),
                  waits=tok_A)
    t_srB = S.add("vector", lambda e: e.tensor_copy(out=statrowsB, in_=pb1[0:2, :]),
                  waits=tok_B)
    tok_kv = []
    for mt in range(2):
        for c in range(NCH):
            tok_kv.append(S.add("tensor", lambda e, c=c, mt=mt: e.matmul(
                pb0[:, mt * 4:mt * 4 + 3], wl[c][:, mt * 128:(mt + 1) * 128],
                kvr[:, c * 3:(c + 1) * 3], start=(c == 0), stop=(c == NCH - 1)),
                waits=[t_wl[c], t_srA] + t_kvr))
        for c in range(NCH):
            tok_kv.append(S.add("tensor", lambda e, c=c, mt=mt: e.matmul(
                pb0[:, mt * 4 + 3:mt * 4 + 4], wallT[c][:, mt * 128:(mt + 1) * 128],
                vcat[:, 8 + c:9 + c], start=(c == 0), stop=(c == NCH - 1)),
                waits=[d_wall[c], d_vcat, t_srA]))
    t_kvf = S.add("vector", lambda e: e.tensor_copy(out=kvf[:, :], in_=pb0[:, 0:8]),
                  waits=tok_kv)
    t_kc = S.add("vector", lambda e: e.tensor_tensor(
        out=kvf[:, 3:4], in0=kvf[:, 3:4], in1=bvec2[:, 0:1], op=A.add),
        waits=[t_kvf, d_bvec])
    t_kc2 = S.add("vector", lambda e: e.tensor_tensor(
        out=kvf[:, 7:8], in0=kvf[:, 7:8], in1=bvec2[:, 1:2], op=A.add),
        waits=[t_kvf, d_bvec])
    t_kvb = S.add("vector", lambda e: e.tensor_copy(
        out=kvbf.ap().rearrange("p (v b) -> p v b", b=2), in_=kvf.ap().rearrange("p (b v) -> p v b", b=2)),
        waits=[t_kc, t_kc2])
    t_kvt = S.add("tensor", lambda e: e.transpose(pb1bf[0:8, 0:128], kvbf[:, :], eyebf[:, :]),
                  waits=[t_kvb, d_eyebf, t_srB])
    t_k8 = S.add("vector", lambda e: e.tensor_copy(out=k8[:, :], in_=pb1bf[0:8, 0:128]),
                 waits=[t_kvt])
    d_l9 = S.dma("sync", lhsT9.ap().rearrange("p (b c) -> p b c", b=2), k8[:, :],
                 waits=[t_k8])
    tok_ks = [S.add("tensor", lambda e, c=c: e.matmul(
        pb0[0:5, 8:9], Vs[:, c * 5:(c + 1) * 5], ones_c[:, :],
        start=(c == 0), stop=(c == NCH - 1)), waits=t_vs + [t_ones2, t_srA] + tok_kv[-1:])
        for c in range(NCH)]
    t_sc5 = S.add("vector", lambda e: e.tensor_copy(out=sc5[:, :], in_=pb0[0:5, 8:9]),
                  waits=tok_ks)
    d_sc1 = S.dma("sync", b_sc.ap()[0:1, 0:5], sc5[:, :], waits=[t_sc5])
    t_sg = S.dma("sync", sg_[:, 0:5], b_sc.ap()[0:1, 0:5].partition_broadcast(128),
                 waits=[d_sc1, t_zsg])
    t_sP1 = S.add("vector", lambda e: e.tensor_scalar(
        out=sgc("sP1"), in0=sgc("sca"), scalar1=1024.0, scalar2=None, op0=A.add),
        waits=[t_sg])
    t_sP12 = S.add("vector", lambda e: e.tensor_scalar(
        out=sgc("sP12"), in0=sgc("sc2"), scalar1=1024.0, scalar2=None, op0=A.add),
        waits=[t_sg])
    t_ssan = S.add("vector", lambda e: e.tensor_scalar(
        out=sgc("ssan"), in0=sgc("ssan"), scalar1=1.0 / 1024.0, scalar2=None, op0=A.mult),
        waits=[t_sg])

    tok_st = []
    for tt in range(16):
        tok_st.append(S.add("tensor", lambda e, tt=tt: e.transpose(
            pb1[:, 512 + tt * 6:512 + tt * 6 + 4],
            statrowsA[:, tt * 128:(tt + 1) * 128], eye[0:4, 0:4]),
            waits=[t_srA, t_srB, t_sc5]))
        tok_st.append(S.add("tensor", lambda e, tt=tt: e.transpose(
            pb1[:, 512 + tt * 6 + 4:512 + tt * 6 + 6],
            statrowsB[:, tt * 128:(tt + 1) * 128], eye[0:2, 0:2]),
            waits=[t_srA, t_srB, t_sc5]))
    t_stT = S.add("vector", lambda e: e.tensor_copy(out=statT[:, :], in_=pb1[:, 512:608]),
                  waits=tok_st)


    st_ap = statT.ap().rearrange("p (tt j) -> p j tt", j=6)
    V_S0, V_Sca, V_Sc2, V_Spsa, V_Q0, V_Qc2 = (st_ap[:, j, :] for j in range(6))

    rmt = []

    def vop(emit, waits=()):
        tok = S.add("vector", emit, waits=list(waits) + [t_stT])
        rmt.append(tok)
        return tok
    vop(lambda e: e.tensor_scalar(out=rmv(0), in0=V_S0, scalar1=1.0 / 1024, scalar2=None, op0=A.mult))
    vop(lambda e: e.tensor_tensor(out=rmv(2), in0=rmv(0), in1=rmv(0), op=A.mult))
    vop(lambda e: e.scalar_tensor_tensor(out=rmv(3), in0=V_Q0, scalar=1.0 / 1024,
                                         in1=rmv(2), op0=A.mult, op1=A.subtract))
    vop(lambda e: e.tensor_scalar(out=rmv(3), in0=rmv(3), scalar1=1e-5, scalar2=None, op0=A.add))
    vop(lambda e: e.reciprocal(out=rmv(5), in_=rmv(3)))
    t_rx = S.add("scalar", lambda e: e.activation(out=rmv(6), in_=rmv(5), func=AF.Sqrt),
                 waits=rmt[-1:])
    vop(lambda e: e.tensor_tensor(out=rmv(7), in0=V_S0, in1=V_Sca, op=A.add))
    vop(lambda e: e.scalar_tensor_tensor(out=rmv(8), in0=rmv(0), scalar=sgc("sP1"),
                                         in1=rmv(7), op0=A.mult, op1=A.subtract),
        waits=[t_sP1])
    vop(lambda e: e.tensor_tensor(out=rmv(9), in0=rmv(8), in1=rmv(6), op=A.mult),
        waits=[t_rx])
    vop(lambda e: e.tensor_scalar(out=rmv(7), in0=rmv(9), scalar1=-1.0 / 1024,
                                  scalar2=sgc("ssan"), op0=A.mult, op1=A.add),
        waits=[t_ssan])
    vop(lambda e: e.tensor_tensor(out=rmv(8), in0=V_S0, in1=V_Sc2, op=A.add))
    vop(lambda e: e.tensor_tensor(out=rmv(9), in0=V_Q0, in1=V_Qc2, op=A.add))
    vop(lambda e: e.tensor_scalar(out=rmv(10), in0=rmv(2), scalar1=sgc("sP12"),
                                  scalar2=None, op0=A.mult), waits=[t_sP12])
    vop(lambda e: e.tensor_tensor(out=rmv(11), in0=rmv(0), in1=rmv(8), op=A.mult))
    vop(lambda e: e.scalar_tensor_tensor(out=rmv(9), in0=rmv(11), scalar=-2.0,
                                         in1=rmv(9), op0=A.mult, op1=A.add))
    vop(lambda e: e.tensor_tensor(out=rmv(9), in0=rmv(9), in1=rmv(10), op=A.add))
    vop(lambda e: e.tensor_tensor(out=rmv(9), in0=rmv(9), in1=rmv(5), op=A.mult))
    vop(lambda e: e.scalar_tensor_tensor(out=rmv(10), in0=rmv(0), scalar=sgc("spsa"),
                                         in1=V_Spsa, op0=A.mult, op1=A.subtract),
        waits=[t_sg])
    vop(lambda e: e.tensor_tensor(out=rmv(10), in0=rmv(10), in1=rmv(6), op=A.mult),
        waits=[t_rx])
    vop(lambda e: e.scalar_tensor_tensor(out=rmv(9), in0=rmv(10), scalar=-2.0,
                                         in1=rmv(9), op0=A.mult, op1=A.add))
    vop(lambda e: e.tensor_scalar(out=rmv(9), in0=rmv(9), scalar1=sgc("ssa2"),
                                  scalar2=1.0 / 1024, op0=A.add, op1=A.mult), waits=[t_sg])
    vop(lambda e: e.tensor_tensor(out=rmv(10), in0=rmv(7), in1=rmv(7), op=A.mult))
    vop(lambda e: e.tensor_tensor(out=rmv(9), in0=rmv(9), in1=rmv(10), op=A.subtract))
    vop(lambda e: e.tensor_scalar(out=rmv(9), in0=rmv(9), scalar1=1e-5, scalar2=None, op0=A.add))
    vop(lambda e: e.reciprocal(out=rmv(11), in_=rmv(9)))
    t_rh = S.add("scalar", lambda e: e.activation(out=rmv(10), in_=rmv(11), func=AF.Sqrt),
                 waits=rmt[-1:])
    vop(lambda e: e.tensor_tensor(out=rmv(11), in0=rmv(6), in1=rmv(10), op=A.mult),
        waits=[t_rh])
    vop(lambda e: e.tensor_tensor(out=rmv(12), in0=rmv(11), in1=rmv(0), op=A.mult))
    vop(lambda e: e.tensor_tensor(out=rmv(4), in0=rmv(10), in1=rmv(7), op=A.mult))
    co_ap = coefT.ap().rearrange("p (tt j) -> p j tt", j=3)
    t_co = [S.add("vector", lambda e, j=j, slab=slab, sg=sg: e.tensor_scalar(
        out=co_ap[:, j, :], in0=rmv(slab), scalar1=sg, scalar2=None, op0=A.mult),
        waits=rmt[-6:] + [t_rh]) for j, slab, sg in [(0, 10, 1.0), (1, 12, -1.0), (2, 4, -1.0)]]
    t_coa = S.add("vector", lambda e: e.tensor_copy(
        out=coefTa[:, :], in_=rmv(11)), waits=rmt[-6:] + [t_rh])
    tok_ct = [S.add("tensor", lambda e, tt=tt: e.transpose(
        pb1bf[0:3, tt * 128:(tt + 1) * 128], coefT[:, tt * 3:(tt + 1) * 3], eyebf[:, :]),
        waits=t_co + [t_k8, t_stT]) for tt in range(16)]
    t_r9a = S.add("vector", lambda e: e.tensor_copy(out=rhs9[0:3, :], in_=pb1bf[0:3, 0:2048]),
                  waits=tok_ct)
    t_r9b = S.dma("sync", rhs9[3:4, :], t_onesbf.ap()[:, :])
    tok_cta = [S.add("tensor", lambda e, tt=tt: e.transpose(
        pb1[0:1, tt * 128:(tt + 1) * 128], coefTa[:, tt:tt + 1], eye[:, :]),
        waits=[t_coa, t_r9a]) for tt in range(16)]
    t_arow = S.add("vector", lambda e: e.tensor_copy(out=arow, in_=pb1[0:1, 0:2048]),
                   waits=tok_cta + [t_l65b])
    d_ar1 = S.dma("sync", b_row.ap()[:, :], arow, waits=[t_arow])
    t_asb = S.dma("sync", ar2[:, :], b_row.ap().partition_broadcast(128),
                  waits=[d_ar1] + tok_B[-1:])


    tok_G = []
    first_w = [t_kvf, t_sc5, t_srA, t_srB, t_stT, t_k8, t_r9a, t_arow]
    for c in range(NCH + 1):
        for mt in range(2):
            pg = pb0 if mt == 0 else pb1
            for n in range(4):
                if c < NCH:
                    tok_G.append(S.add("tensor", lambda e, c=c, mt=mt, n=n, pg=pg: e.matmul(
                        pg[:, n * 512:(n + 1) * 512], w3bf[c][:, mt * 128:(mt + 1) * 128],
                        xbufs[c][:, n * 512:(n + 1) * 512],
                        start=(c == 0), stop=False),
                        waits=[t_xbf[c], t_w3[c]] + (first_w if c == 0 else [])))
                else:
                    tok_G.append(S.add("tensor", lambda e, mt=mt, n=n, pg=pg: e.matmul(
                        pg[:, n * 512:(n + 1) * 512], lhsT9[:, mt * 128:(mt + 1) * 128],
                        rhs9[:, n * 512:(n + 1) * 512], start=False, stop=True),
                        waits=[d_l9, t_r9b]))
    t_gt0 = [S.add("vector", lambda e, n=n: e.tensor_tensor(
        out=gt0[:, n * 512:(n + 1) * 512], in0=pb0[:, n * 512:(n + 1) * 512],
        in1=ar2[:, n * 512:(n + 1) * 512], op=A.mult), waits=tok_G + [t_asb])
        for n in range(4)]
    t_gt1 = [S.add("vector", lambda e, n=n: e.tensor_tensor(
        out=gt1[:, n * 512:(n + 1) * 512], in0=pb1[:, n * 512:(n + 1) * 512],
        in1=ar2[:, n * 512:(n + 1) * 512], op=A.mult), waits=tok_G + [t_asb])
        for n in range(4)]
    t_clip = S.add("vector", lambda e: e.tensor_scalar(
        out=gt0[:, :], in0=gt0[:, :], scalar1=20.0, scalar2=-20.0, op0=A.min, op1=A.max),
        waits=t_gt0)

    wq8f32 = wq8.bitcast(F32)
    P1R = [wq8f32[:, 0:1024], wq8f32[:, 1024:2048], wfS[:, 0:1024]]
    e1Pf32 = e1P.bitcast(F32)
    P2R = [e1Pf32[:, 0:1024], wfS[:, 0:1024]]
    q8tmp = e1P.bitcast(FP8)[:, 0:2048]
    wred_prev = []
    wacc_tok = [None]
    d_wf1 = []

    def emit_wf1_dma(c):
        w = [] if c < 3 else [wred_prev[c - 3]]
        d_wf1.append(S.dma("scalar", P1R[c % 3], t_wfT.ap()[c * 128:(c + 1) * 128, :],
                           waits=w))

    def emit_wred(c):
        tk = S.add("vector", lambda e, c=c: e.tensor_reduce(
            out=rm[:, 208 + (c % 4):208 + (c % 4) + 1], in_=P1R[c % 3], axis=AX.X,
            op=A.max, apply_absolute_value=True), waits=[d_wf1[c]])
        if c == 0:
            wa = S.add("vector", lambda e: e.tensor_copy(
                out=rm[:, WACC:WACC + 1], in_=rm[:, 208:209]), waits=[tk])
        else:
            wa = S.add("vector", lambda e, c=c: e.tensor_tensor(
                out=rm[:, WACC:WACC + 1], in0=rm[:, WACC:WACC + 1],
                in1=rm[:, 208 + (c % 4):208 + (c % 4) + 1], op=A.max),
                waits=[tk, wacc_tok[0]])
        wacc_tok[0] = wa
        wred_prev.append(tk)
    for c in range(3):
        emit_wf1_dma(c)

    t_h1 = S.add("vector", lambda e: e.memset(h65[64:65, :], 1.0))
    s_lf = saA[0:64, 0:2048]
    s_i = saA[0:64, 2048:4096]
    s_m = saA[0:64, 4096:6144]
    li_v = gt0[0:64, :]
    z_v = gt1[0:64, :]
    o_v = ar2[0:64, :]
    d_lf = S.dma("sync", s_lf, gt0[64:128, :], waits=[t_clip])
    d_oo = S.dma("scalar", o_v, gt1[64:128, :], waits=t_gt1)
    t_zt = S.add("scalar", lambda e: e.activation(out=z_v, in_=z_v, func=AF.Tanh),
                 waits=t_gt1)
    t_ot = S.add("scalar", lambda e: e.activation(out=o_v, in_=o_v, func=AF.Tanh,
                                                  scale=0.5), waits=[d_oo])
    t_ms = S.add("vector", lambda e: e.tensor_tensor_scan(
        out=s_m, data0=s_lf, data1=li_v, initial=0.0, op0=A.add, op1=A.max),
        waits=[d_lf, t_clip])
    t_i1 = S.add("gpsimd", lambda e: e.tensor_tensor(
        out=s_i, in0=li_v, in1=s_m, op=A.subtract), waits=[t_ms])
    t_d1 = S.add("vector", lambda e: e.tensor_tensor(
        out=s_lf, in0=s_lf, in1=s_m, op=A.subtract), waits=[t_ms])
    t_f2 = S.add("vector", lambda e: e.tensor_tensor(
        out=s_lf[:, 1:T], in0=s_lf[:, 1:T], in1=s_m[:, 0:T - 1], op=A.add),
        waits=[t_d1])
    t_oa = S.add("vector", lambda e: e.tensor_scalar(
        out=o_v, in0=o_v, scalar1=0.5, scalar2=0.5, op0=A.mult, op1=A.add),
        waits=[t_ot])
    t_ex = S.add("scalar", lambda e: e.activation(
        out=saA[0:64, 0:4096], in_=saA[0:64, 0:4096], func=AF.Exp),
        waits=[t_f2, t_i1])
    t_u = S.add("vector", lambda e: e.tensor_tensor(
        out=s_i, in0=s_i, in1=z_v, op=A.mult), waits=[t_ex, t_zt])
    t_cs = S.add("vector", lambda e: e.tensor_tensor_scan(
        out=s_m, data0=s_lf, data1=s_i, initial=0.0, op0=A.mult, op1=A.add),
        waits=[t_u])
    t_tc = S.add("scalar", lambda e: e.activation(out=s_m, in_=s_m, func=AF.Tanh),
                 waits=[t_cs])
    t_h65 = [S.add("vector", lambda e: e.tensor_tensor(
        out=h65[0:64, :], in0=o_v, in1=s_m, op=A.mult), waits=[t_tc, t_oa])]

    tok_res1 = []
    prev_ev = {}
    for m in range(NCH):
        pg = pb0 if m % 2 == 0 else pb1
        psum_free = [prev_ev[m - 2]] if m >= 2 else (t_gt1 + [t_clip])
        mms = [S.add("tensor", lambda e, m=m, n=n, pg=pg: e.matmul(
            pg[:, n * 512:(n + 1) * 512], lhsT65[:, m * 128:(m + 1) * 128],
            h65[:, n * 512:(n + 1) * 512], start=True, stop=True),
            waits=t_h65 + [t_h1, t_l65a, t_l65b] + psum_free) for n in range(4)]
        region_dead = [t_kvt] if m < 2 else []
        ev = S.add("vector", lambda e, m=m, pg=pg: e.scalar_tensor_tensor(
            out=x1[m], in0=pg[:, :], scalar=deriv[:, 16 + m:17 + m], in1=xch[m],
            op0=A.mult, op1=A.add), waits=mms + [t_ga] + region_dead)
        prev_ev[m] = ev
        tok_res1.append(ev)


    t_x1bf, t_x1sq = {}, {}
    tok_S1, tok_Q1 = [], []
    prevA, prevB = [], []
    prev2A, prev2B = [], []
    for c in range(NCH):
        t_x1bf[c] = S.add("vector", lambda e, c=c: e.tensor_copy(
            out=xbf2P[c % 2], in_=x1[c]), waits=[tok_res1[c], t_asb] + prev2A)
        t_x1sq[c] = S.add("scalar", lambda e, c=c: e.activation(
            out=sqbf2P[c % 2], in_=x1[c], func=AF.Square),
            waits=[tok_res1[c]] + prev2B)
        mmA = [S.add("tensor", lambda e, c=c, n=n: e.matmul(
            pb0[0:1, n * 512:(n + 1) * 512], onecbf[:, :],
            xbf2P[c % 2][:, n * 512:(n + 1) * 512], start=(c == 0), stop=(c == NCH - 1)),
            waits=[t_x1bf[c], t_ones3] + ([tok_res1[6]] if c == 0 else []))
            for n in range(4)]
        mmB = [S.add("tensor", lambda e, c=c, n=n: e.matmul(
            pb1[0:1, n * 512:(n + 1) * 512], onecbf[:, :],
            sqbf2P[c % 2][:, n * 512:(n + 1) * 512], start=(c == 0), stop=(c == NCH - 1)),
            waits=[t_x1sq[c], t_ones3] + ([tok_res1[7]] if c == 0 else []))
            for n in range(4)]
        prev2A, prev2B = prevA, prevB
        prevA, prevB = mmA, mmB
        tok_S1 += mmA
        tok_Q1 += mmB
    t_sr2a = S.add("vector", lambda e: e.tensor_copy(out=statrows2a, in_=pb0[0:1, :]),
                   waits=tok_S1)
    t_sr2b = S.add("vector", lambda e: e.tensor_copy(out=statrows2b, in_=pb1[0:1, :]),
                   waits=tok_Q1)
    tok_st2 = []
    for tt in range(16):
        tok_st2.append(S.add("tensor", lambda e, tt=tt: e.transpose(
            pb0[:, 512 + tt * 2:512 + tt * 2 + 1],
            statrows2a[:, tt * 128:(tt + 1) * 128], eye[0:1, 0:1]),
            waits=[t_sr2a, t_sr2b]))
        tok_st2.append(S.add("tensor", lambda e, tt=tt: e.transpose(
            pb0[:, 512 + tt * 2 + 1:512 + tt * 2 + 2],
            statrows2b[:, tt * 128:(tt + 1) * 128], eye[0:1, 0:1]),
            waits=[t_sr2a, t_sr2b]))
    t_stT2 = S.add("vector", lambda e: e.tensor_copy(out=statT2[:, :], in_=pb0[:, 512:544]),
                   waits=tok_st2)
    st2_ap = statT2.ap().rearrange("p (tt j) -> p j tt", j=2)
    rm2 = []

    def vop2(emit, waits=()):
        tok = S.add("vector", emit, waits=list(waits) + [t_stT2])
        rm2.append(tok)
        return tok
    vop2(lambda e: e.tensor_scalar(out=rmv(0), in0=st2_ap[:, 0, :], scalar1=1.0 / 1024,
                                   scalar2=None, op0=A.mult))
    vop2(lambda e: e.tensor_scalar(out=rmv(1), in0=st2_ap[:, 1, :], scalar1=1.0 / 1024,
                                   scalar2=None, op0=A.mult))
    vop2(lambda e: e.tensor_tensor(out=rmv(2), in0=rmv(0), in1=rmv(0), op=A.mult))
    vop2(lambda e: e.tensor_tensor(out=rmv(3), in0=rmv(1), in1=rmv(2), op=A.subtract))
    vop2(lambda e: e.tensor_scalar(out=rmv(3), in0=rmv(3), scalar1=1e-5, scalar2=None, op0=A.add))
    vop2(lambda e: e.reciprocal(out=rmv(4), in_=rmv(3)))
    t_r2v = S.add("scalar", lambda e: e.activation(out=rmv(5), in_=rmv(4), func=AF.Sqrt),
                  waits=rm2[-1:])
    vop2(lambda e: e.tensor_tensor(out=rmv(6), in0=rmv(0), in1=rmv(5), op=A.mult),
         waits=[t_r2v])
    t_c2w = [S.add("vector", lambda e: e.tensor_copy(out=coefT2[:, :], in_=rmv(5)),
                   waits=rm2[-1:] + [t_r2v]),
             S.add("vector", lambda e: e.tensor_scalar(
                 out=coefT2b[:, :], in0=rmv(6), scalar1=-1.0, scalar2=None, op0=A.mult),
                 waits=rm2[-1:])]
    tok_c2t = [S.add("tensor", lambda e, tt=tt: e.transpose(
        pb1[0:1, tt * 128:(tt + 1) * 128], coefT2[:, tt:tt + 1], eye[:, :]),
        waits=t_c2w[:1] + [t_sr2b] + tok_Q1) for tt in range(16)]
    t_r2r = S.add("vector", lambda e: e.tensor_copy(out=r2row, in_=pb1[0:1, 0:2048]),
                  waits=tok_c2t)
    d_r2a = S.dma("sync", b_row2.ap()[:, :], r2row, waits=[t_r2r])
    t_r2sb = S.dma("sync", ar2[:, :], b_row2.ap().partition_broadcast(128),
                   waits=[d_r2a, t_asb] + tok_S1)
    tok_c2tb = [S.add("tensor", lambda e, tt=tt: e.transpose(
        pb1bf2[0:1, tt * 128:(tt + 1) * 128], coefT2b[:, tt:tt + 1],
        eyebf[:, :]), waits=t_c2w[1:2] + [t_r2r]) for tt in range(16)]
    t_b2r2 = S.add("vector", lambda e: e.tensor_copy(
        out=B2rhs[0:1, :], in_=pb1bf2[0:1, 0:2048]), waits=tok_c2tb)
    t_b2r1 = S.dma("sync", B2rhs[1:2, :], t_onesbf.ap()[:, :])
    tok_rb = [t_r2sb]
    t_b2l = []
    tok_b2t = []
    mps = rm.ap().rearrange("p n -> p n")[:, 216:232].rearrange("p (c j) -> p j c", j=2)
    t_mp1 = S.add("vector", lambda e: e.tensor_copy(out=mps[:, 0, :], in_=deriv[:, 8:16]),
                  waits=[t_p2])
    t_mp2 = S.add("vector", lambda e: e.tensor_copy(out=mps[:, 1, :], in_=modT[:, 24:32]),
                  waits=[t_modT])
    for c in range(NCH):
        tt_ = S.add("tensor", lambda e, c=c: e.transpose(
            pb1[0:2, 1152:1280], rm[:, 216 + c * 2:218 + c * 2], eye[:, :]),
            waits=[t_mp1, t_mp2, t_b2r2] + tok_rb + (tok_b2t[-1:] if tok_b2t else [t_sr2b] + tok_Q1))
        ev = S.add("vector", lambda e, c=c: e.tensor_copy(
            out=B2lhsT[:, c * 128:(c + 1) * 128], in_=pb1[0:2, 1152:1280]), waits=[tt_])
        tok_b2t.append(ev)
        t_b2l.append(ev)

    tok_amx = []
    t_h2 = {}
    prev_b2 = {}
    for c in range(NCH):
        pg = pb0 if c % 2 == 0 else pb1
        t_t1 = S.add("vector" if c % 2 == 0 else "gpsimd", lambda e, c=c: e.tensor_tensor(
            out=xch[c], in0=x1[c], in1=ar2[:, :], op=A.mult),
            waits=[t_r2sb, tok_res1[c], t_x1bf[c], t_x1sq[c]])
        mmb2 = [S.add("tensor", lambda e, c=c, n=n, pg=pg: e.matmul(
            pg[:, n * 512:(n + 1) * 512], B2lhsT[:, c * 128:(c + 1) * 128],
            B2rhs[:, n * 512:(n + 1) * 512], start=True, stop=True),
            waits=[t_b2l[c], t_b2r1, t_b2r2] +
                  ([prev_b2[c - 2]] if c >= 2 else [t_r2sb] + tok_b2t))
            for n in range(4)]
        t_h2[c] = S.add("vector", lambda e, c=c, pg=pg: e.scalar_tensor_tensor(
            out=xch[c], in0=xch[c], scalar=deriv[:, 8 + c:9 + c], in1=pg[:, :],
            op0=A.mult, op1=A.add), waits=mmb2 + [t_t1, t_p2])
        prev_b2[c] = t_h2[c]
        tok_amx.append(S.add("vector", lambda e, c=c: e.tensor_reduce(
            out=amx8[:, c:c + 1], in_=xch[c], axis=AX.X, op=A.max,
            apply_absolute_value=True), waits=[t_h2[c]]))
        emit_wred(c)
        if c + 3 < NCH:
            emit_wf1_dma(c + 3)
    t_wl_ = S.add("gpsimd", lambda e: e.tensor_reduce(
        out=s11[0:1, 6:7], in_=rm[:, WACC:WACC + 1], axis=AX.C, op=A.max),
        waits=[wacc_tok[0]])
    t_ws1 = S.add("vector", lambda e: e.tensor_scalar(
        out=s11[0:1, 6:7], in0=s11[0:1, 6:7], scalar1=1e-12, scalar2=None, op0=A.max),
        waits=[t_wl_])
    t_ws2 = S.add("vector", lambda e: e.reciprocal(out=s11[0:1, 7:8], in_=s11[0:1, 6:7]),
                  waits=[t_ws1])
    t_ws3 = S.add("vector", lambda e: e.tensor_scalar(
        out=s11[0:1, 7:8], in0=s11[0:1, 7:8], scalar1=448.0, scalar2=448.0,
        op0=A.mult, op1=A.min), waits=[t_ws2])
    t_ws4 = S.add("vector", lambda e: e.reciprocal(out=s11[0:1, 8:9], in_=s11[0:1, 7:8]),
                  waits=[t_ws3])
    t_ws5 = S.add("vector", lambda e: e.tensor_scalar(
        out=s11[0:1, 9:10], in0=s11[0:1, 7:8], scalar1=0.5, scalar2=None, op0=A.mult),
        waits=[t_ws3])
    t_sw2b_mm = S.add("tensor", lambda e: e.matmul(
        pb1[:, 1408:1409], ones_r[:, :], s11[0:1, 9:10], start=True, stop=True),
        waits=[t_ws5, t_ones, t_h2[7]])
    t_sw2b = S.add("vector", lambda e: e.tensor_copy(out=sgc("sw2"), in_=pb1[:, 1408:1409]),
                   waits=[t_sw2b_mm])
    t_amc = S.add("vector", lambda e: e.tensor_reduce(
        out=amxc[:, :], in_=amx8[:, :], axis=AX.X, op=A.max), waits=tok_amx)
    t_aml = S.add("gpsimd", lambda e: e.tensor_reduce(
        out=s11[0:1, 0:1], in_=amxc[:, :], axis=AX.C, op=A.max), waits=[t_amc])
    d_ami = S.dma("sync", b_am_i.ap()[:, :], s11[0:1, 0:1], waits=[t_aml, t_z11])
    tok_wq = []
    prev_q = []
    for c in range(NCH):
        bw = ([wred_prev[5]] if c == 1 else
              (prevB5[4] if c == 0 else []))
        d = S.dma("scalar", P2R[c % 2], t_wfT.ap()[c * 128:(c + 1) * 128, :],
                  waits=bw + (prev_q[c - 2:c - 1] if c >= 2 else []))
        tq = S.add("vector", lambda e, c=c: e.tensor_scalar(
            out=wq8[:, c * 1024:(c + 1) * 1024], in0=P2R[c % 2], scalar1=sgc("sw2"),
            scalar2=None, op0=A.mult), waits=[d, t_sw2b])
        prev_q.append(tq)
        tok_wq.append(tq)
    cc_am = S.add("gpsimd", lambda e: e.collective_compute(
        "AllReduce", A.max, replica_groups=[list(range(B))],
        ins=[b_am_i.ap().opt()], outs=[b_am_o.ap().opt()]),
        waits=[d_ami, cc_mod], sem="cc", inc=1)
    t_amg = S.dma("sync", s11[0:1, 1:2], b_am_o.ap()[:, :], waits=[cc_am])

    t_sc1 = S.add("vector", lambda e: e.tensor_scalar(
        out=s11[0:1, 1:2], in0=s11[0:1, 1:2], scalar1=1e-12, scalar2=None, op0=A.max),
        waits=[t_amg])
    t_sc2_ = S.add("vector", lambda e: e.reciprocal(out=s11[0:1, 2:3], in_=s11[0:1, 1:2]),
                   waits=[t_sc1])
    t_sc3 = S.add("vector", lambda e: e.tensor_scalar(
        out=s11[0:1, 2:3], in0=s11[0:1, 2:3], scalar1=448.0, scalar2=448.0,
        op0=A.mult, op1=A.min), waits=[t_sc2_])
    t_sc4 = S.add("vector", lambda e: e.reciprocal(out=s11[0:1, 3:4], in_=s11[0:1, 2:3]),
                  waits=[t_sc3])
    t_sc5_ = S.add("vector", lambda e: e.tensor_scalar(
        out=s11[0:1, 4:5], in0=s11[0:1, 2:3], scalar1=0.5, scalar2=None, op0=A.mult),
        waits=[t_sc3])
    t_dq1 = S.add("vector", lambda e: e.tensor_tensor(
        out=s11[0:1, 10:11], in0=s11[0:1, 3:4], in1=s11[0:1, 8:9], op=A.mult),
        waits=[t_sc4, t_ws4])
    t_dq2 = S.add("vector", lambda e: e.tensor_scalar(
        out=s11[0:1, 5:6], in0=s11[0:1, 10:11], scalar1=4.0, scalar2=None, op0=A.mult),
        waits=[t_dq1])
    t_xsb_mm = S.add("tensor", lambda e: e.matmul(
        pb0[:, 140:141], ones_r[:, :], s11[0:1, 4:5], start=True, stop=True),
        waits=[t_sc5_, t_sr2a])
    t_dqb_mm = S.add("tensor", lambda e: e.matmul(
        pb0[:, 141:142], ones_r[:, :], s11[0:1, 5:6], start=True, stop=True),
        waits=[t_dq2, t_sr2a])
    t_xsb = S.add("vector", lambda e: e.tensor_copy(out=sgc("xs2"), in_=pb0[:, 140:141]),
                  waits=[t_xsb_mm])
    t_dqb = S.add("vector", lambda e: e.tensor_copy(out=sgc("deq"), in_=pb0[:, 141:142]),
                  waits=[t_dqb_mm])

    r2bq = r2b.bitcast(FP8)
    wfSq = wfS.bitcast(FP8)
    qdst = [r2bq[:, k * 2048:(k + 1) * 2048] for k in range(4)] +            [wfSq[:, k * 2048:(k + 1) * 2048] for k in range(2)]
    tok_q8 = []
    for c in range(NCH):
        if c < 4:
            extra = tok_Q1[-1:]
        else:
            extra = [tok_wq[7]]
        if c < 6:
            tok_q8.append(S.add("vector", lambda e, c=c: e.tensor_scalar(
                out=qdst[c], in0=xch[c], scalar1=sgc("xs2"),
                scalar2=None, op0=A.mult),
                waits=[t_h2[c], t_xsb, tok_amx[c]] + extra))
        else:
            tq = S.add("vector", lambda e, c=c: e.tensor_scalar(
                out=q8tmp[:, :], in0=xch[c], scalar1=sgc("xs2"),
                scalar2=None, op0=A.mult),
                waits=[t_h2[c], t_xsb, tok_amx[c], tok_wq[6]])
            tok_q8.append(S.add("vector", lambda e, c=c: e.tensor_copy(
                out=q8[c], in_=q8tmp[:, :]), waits=[tq]))

    wq8_3d = wq8.ap().rearrange("p (k m) -> p k m", k=NCH)
    q8_3d = r1.ap().bitcast(FP8).rearrange("p (c x) -> p c x", c=NCH)
    r2b_3d = r2b.bitcast(FP8).ap().rearrange("p (c x) -> p c x", c=4)
    wfS_3d = wfS.bitcast(FP8).ap().rearrange("p (c x) -> p c x", c=2)
    rhs_pairs = [r2b_3d[:, 0:2, :], r2b_3d[:, 2:4, :],
                 wfS_3d[:, 0:2, :], q8_3d[:, 6:8, 0:2048]]
    fb_col = lambda m: vcat[:, 16 + m:17 + m]
    prev_e1 = {}
    prev_x2 = {}
    tok_out = []
    for m in range(NCH):
        pg = pb0 if m % 2 == 0 else pb1
        mm_w0 = ([prev_e1[m - 2]] if m >= 2
                 else [t_xsb, t_dqb] + ([t_sw2b] if m == 1 else []))
        mms = []
        for n in range(4):
            for kp in range(NCH // 2):
                mms.append(S.add("tensor", lambda e, m=m, n=n, kp=kp, pg=pg: e.matmul(
                    pg[:, n * 512:(n + 1) * 512],
                    wq8_3d[:, 2 * kp:2 * kp + 2, m * 128:(m + 1) * 128],
                    rhs_pairs[kp][:, :, n * 512:(n + 1) * 512],
                    start=(kp == 0), stop=(kp == NCH // 2 - 1),
                    perf_mode=mybir.MatmulPerfMode.DoubleRow),
                    waits=[tok_wq[2 * kp], tok_wq[2 * kp + 1], tok_q8[2 * kp],
                           tok_q8[2 * kp + 1]] + (mm_w0 if (kp == 0 and n == 0) else [])))
        t_e1 = S.add("scalar", lambda e, m=m, pg=pg: e.activation(
            out=e1P[:, :], in_=pg[:, :], func=AF.Identity,
            scale=sgc("deq"), bias=fb_col(m)),
            waits=mms + [t_dqb] + ([prev_x2[m - 1]] if m >= 1 else []))
        prev_e1[m] = t_e1
        t_x2 = S.add("vector", lambda e, m=m: e.scalar_tensor_tensor(
            out=x1[m], in0=e1P[:, :], scalar=deriv[:, 24 + m:25 + m],
            in1=x1[m], op0=A.mult, op1=A.add),
            waits=[t_e1, t_gb, tok_q8[m]])
        prev_x2[m] = t_x2
        tok_out.append(S.dma("sync", t_out.ap()[m * 128:(m + 1) * 128, :], x1[m],
                             waits=[t_x2]))

    if dbg:
        tok_out.append(S.dma("sync", dbg_outs["d_modT"].ap()[:, :], modT[:, :], waits=[t_modT]))
        tok_out.append(S.dma("sync", dbg_outs["d_statT"].ap()[:, :], statT[:, :], waits=[t_stT]))
        tok_out.append(S.dma("sync", dbg_outs["d_coefT"].ap()[:, :], coefT[:, :], waits=t_co))
        tok_out.append(S.dma("sync", dbg_outs["d_kv"].ap()[:, :], kvf[:, :], waits=[t_kc]))
        tok_out.append(S.dma("sync", dbg_outs["d_sg"].ap()[:, :], sg_[:, :], waits=[t_sP1, t_sP12, t_ssan, t_xsb, t_dqb, t_sw2b]))
        tok_out.append(S.dma("sync", dbg_outs["d_q8u"].ap()[:, :], q8[0].bitcast(mybir.dt.uint8),
                             waits=[tok_q8[0]] + [prev_x2[m] for m in range(NCH)]))
        tok_out.append(S.dma("sync", dbg_outs["d_wq8u"].ap()[:, :], wq8[:, 0:1024].bitcast(mybir.dt.uint8),
                             waits=[tok_wq[0]]))
        tok_out.append(S.dma("sync", dbg_outs["d_gt0"].ap()[:, :], gt0, waits=[t_clip]))
        tok_out.append(S.dma("sync", dbg_outs["d_gt1"].ap()[:, :], gt1, waits=t_gt1))
        tok_out.append(S.dma("sync", dbg_outs["d_x1"].ap()[:, :], x1[0], waits=[tok_res1[0]]))
        tok_out.append(S.dma("sync", dbg_outs["d_h2"].ap()[:, :], xch[0], waits=[t_h2[0]]))
        tok_out.append(S.dma("sync", dbg_outs["d_scal"].ap()[:, :], s11[:, :],
                             waits=[t_dq2, t_sc5_, t_ws5]))

    S.add("sync", lambda e: e.nop(), waits=tok_out, sem="dI", inc=1)

    with es:
        sems = {nm: es.enter_context(nc.semaphore(nm))
                for nm in sorted(set(list(S.cnt.keys()) + ["dI"]))}
        block = es.enter_context(nc.Block())
        S.emit(block, sems)
    return nc


def kernel(L0_attn_x, L0_attn_ln_w, L0_attn_ln_b, L0_attn_Wi, L0_attn_bi,
           L0_attn_Wf, L0_attn_bf, L0_attn_Wz, L0_attn_bz, L0_attn_Wo,
           L0_attn_bo, L0_attn_Wp, L0_attn_bp, L0_ffn_weight, L0_ffn_bias,
           L0_c, L0_wmod):
    f = lambda a: np.ascontiguousarray(np.asarray(a, dtype=np.float32))
    x = f(L0_attn_x)
    wall = np.concatenate([f(L0_attn_Wi), f(L0_attn_Wf), f(L0_attn_Wz), f(L0_attn_Wo)], 0)
    wallT = np.ascontiguousarray(
        wall.T.reshape(8, 128, 256).transpose(1, 0, 2).reshape(128, 2048))
    bvec = np.concatenate([f(L0_attn_bi), f(L0_attn_bf), f(L0_attn_bz), f(L0_attn_bo)], 0)
    bvec2 = np.ascontiguousarray(bvec.reshape(2, 128).T)
    wpT = np.ascontiguousarray(f(L0_attn_Wp).T)
    bprow = f(L0_attn_bp).reshape(1, D)
    wfT = np.ascontiguousarray(f(L0_ffn_weight).T)
    vcat = np.ascontiguousarray(np.concatenate(
        [f(L0_attn_ln_w).reshape(8, 128).T, f(L0_attn_ln_b).reshape(8, 128).T,
         f(L0_ffn_bias).reshape(8, 128).T], 1))
    cmat = f(L0_c)
    wmod = f(L0_wmod)
    eye = np.eye(128, dtype=np.float32)
    eyebf = np.eye(128, dtype=ml_dtypes.bfloat16)
    onesbf = np.ones((1, T), dtype=ml_dtypes.bfloat16)

    nc = build_program(dbg=_DBG)
    in_maps = [{
        "xT": np.ascontiguousarray(x[b].T),
        "wallT": wallT, "bvec2": bvec2, "wpT": wpT, "bprow": bprow,
        "wfT": wfT, "vcat": vcat, "cmat": cmat,
        "wmod_sl": np.ascontiguousarray(wmod[:, b * JW:(b + 1) * JW]),
        "eye": eye, "eyebf": eyebf, "onesbf": onesbf,
    } for b in range(B)]
    res = run_bass_kernel_spmd(nc, in_maps, core_ids=list(range(B)))
    out = np.stack([np.asarray(res.results[b]["outT"]).T for b in range(B)], 0)
    if _DBG:
        kernel._dbg = [res.results[b] for b in range(B)]
    return out


# version 30
# speedup vs baseline: 1.0995x; 1.0446x over previous
import contextlib
import numpy as np
import ml_dtypes
import concourse.bass as bass
import concourse.mybir as mybir
from concourse.bass_utils import run_bass_kernel_spmd
from concourse.alu_op_type import AluOpType as A
from concourse.aot_env import allow_ici_all_to_all

F32 = mybir.dt.float32
BF16 = mybir.dt.bfloat16
FP8 = mybir.dt.float8e4
AF = mybir.ActivationFunctionType
AX = mybir.AxisListType

B, T, D, NH = 8, 2048, 1024, 64
NCH = 8
TH = T // 2
JW = 6144 // B

_DBG = False

_ENGINES = ("sync", "gpsimd", "vector", "scalar", "tensor")


class Sch:

    def __init__(self):
        self.steps = {e: [] for e in _ENGINES}
        self.cnt = {}

    def add(self, eng, emit, waits=(), sem=None, inc=1):
        if sem is None:
            sem = {"sync": "dI", "gpsimd": "sG", "vector": "sV",
                   "scalar": "sS", "tensor": "sT"}[eng]
        v = self.cnt.get(sem, 0) + inc
        self.cnt[sem] = v
        self.steps[eng].append((emit, tuple(waits), sem, inc))
        return (sem, v)

    def dma(self, eng, out, in_, waits=(), sem=None):
        if sem is None:
            self._dmai = getattr(self, "_dmai", 0) + 1
            sem = "dma%d" % self._dmai
        return self.add(eng, lambda e, o=out, i=in_: e.dma_start(out=o, in_=i),
                        waits=waits, sem=sem, inc=16)

    def emit(self, block, sems):
        own = {"vector": "sV", "scalar": "sS", "gpsimd": "sG"}
        for eng in _ENGINES:
            if not self.steps[eng]:
                continue

            def section(e, eng=eng):
                seen = {}
                mysem = own.get(eng)
                cnt = 0
                for emitfn, waits, semkey, inc in self.steps[eng]:
                    if mysem is not None and semkey == mysem and cnt > 0:
                        waits = tuple(waits) + ((mysem, cnt),)
                    for sk, val in waits:
                        if seen.get(sk, 0) < val:
                            e.wait_ge(sems[sk], val)
                            seen[sk] = val
                    inst = emitfn(e)
                    if inst is not None:
                        inst.then_inc(sems[semkey], inc)
                        if semkey == mysem:
                            cnt += inc
            getattr(block, eng)(section)


def build_program(dbg=False):
    nc = bass.Bass("TRN2", target_bir_lowering=False, detect_race_conditions=False)

    def din(name, shape, dt=F32):
        return nc.dram_tensor(name, list(shape), dt, kind="ExternalInput")

    t_xT = din("xT", (D, T))
    t_wallT = din("wallT", (128, 2048))
    t_bvec2 = din("bvec2", (128, 2))
    t_wpT = din("wpT", (NH, D))
    t_bprow = din("bprow", (1, D))
    t_wfT = din("wfT", (D, D))
    t_vcat = din("vcat", (128, 24))
    t_cmat = din("cmat", (B, D))
    t_wmod = din("wmod_sl", (D, JW))
    t_eye = din("eye", (128, 128))
    t_eyebf = din("eyebf", (128, 128), BF16)
    t_onesbf = din("onesbf", (1, T), BF16)
    t_out = nc.dram_tensor("outT", [D, T], F32, kind="ExternalOutput")

    b_md_i = nc.dram_tensor("b_md_i", [B, JW], F32)
    b_md_o = nc.dram_tensor("b_md_o", [B, JW], F32)
    b_am_i = nc.dram_tensor("b_am_i", [1, 1], F32)
    b_row = nc.dram_tensor("b_row", [1, 2048], F32)
    b_row2 = nc.dram_tensor("b_row2", [1, 2048], F32)
    b_sc = nc.dram_tensor("b_sc", [1, 8], F32)
    b_am_o = nc.dram_tensor("b_am_o", [1, 1], F32)

    dbg_outs = {}
    if dbg:
        for nm, shp, dt in [("d_modT", [128, 48], F32), ("d_statT", [128, 96], F32),
                            ("d_coefT", [128, 48], BF16), ("d_gt0", [128, T], F32),
                            ("d_gt1", [128, T], F32),
                            ("d_x1", [128, T], F32), ("d_h2", [128, T], F32),
                            ("d_scal", [1, 16], F32), ("d_kv", [128, 8], F32),
                            ("d_sg", [128, 16], F32)]:
            dbg_outs[nm] = nc.dram_tensor(nm, shp, dt, kind="ExternalOutput")
        for nm, shp, dt in [("d_q8u", [128, 2048], mybir.dt.uint8),
                            ("d_wq8u", [128, 1024], mybir.dt.uint8)]:
            dbg_outs[nm] = nc.dram_tensor(nm, shp, dt, kind="ExternalOutput")

    es = contextlib.ExitStack()
    sbt = lambda name, shape, dt=F32: es.enter_context(nc.sbuf_tensor(name, list(shape), dt))
    pst = lambda name, shape, dt=F32: es.enter_context(nc.psum_tensor(name, list(shape), dt))

    r1 = sbt("r1", (128, 16384))
    r2 = sbt("r2", (128, 16384))
    r2b = sbt("r2b", (128, 2048))
    saA = es.enter_context(nc.sbuf_tensor("saA", [64, 6144], F32))
    ar2 = sbt("ar2", (128, 2048))

    wallT = [r2[:, 2048 + c * 256: 2048 + (c + 1) * 256] for c in range(NCH)]
    wl = [r2[:, 4096 + c * 256: 4096 + (c + 1) * 256] for c in range(NCH)]
    w3bf = [r2[:, 6144 + c * 128: 6144 + (c + 1) * 128].bitcast(BF16) for c in range(NCH)]
    gt0 = r2[:, 7168:9216]
    sqbfP = [r2[:, 9472:10496].bitcast(BF16), r2[:, 10496:11520].bitcast(BF16)]
    cmat_v = r2[0:B, 12800:13824]
    silu_v = r2[0:B, 14336:15360]
    silu_ch = lambda c: r2[0:B, 14336 + c * 128:14336 + (c + 1) * 128]
    wpT_v = r2[0:NH, 11776:12800]
    modsb8 = r2[0:B, 15360:15360 + JW]
    xbfP = [r2[:, 13312:14336].bitcast(BF16), r2[:, 14336:15360].bitcast(BF16)]
    x1 = [r2[:, c * 2048:(c + 1) * 2048] for c in range(NCH)]

    xch = [r1[:, c * 2048:(c + 1) * 2048] for c in range(NCH)]
    q8 = [r1[:, c * 2048: c * 2048 + 512].bitcast(FP8) for c in range(NCH)]

    gt1 = r2b[:, 0:2048]
    sqbf2P = [r2b[:, 0:1024].bitcast(BF16), r2b[:, 1024:2048].bitcast(BF16)]
    xbf2P = [ar2[:, 0:1024].bitcast(BF16), ar2[:, 1024:2048].bitcast(BF16)]

    sl_li, sl_lf, sl_z, sl_o, sl_m, sl_mp = (saA[0:64, k * 1024:(k + 1) * 1024]
                                             for k in range(6))
    statrowsA = saA[0:4, 0:2048]
    statrowsB = saA[0:2, 2048:4096]
    statrows2a = saA[0:1, 0:2048]
    statrows2b = saA[0:1, 2048:4096]
    arow = saA[0:1, 4096:6144]
    r2row = saA[0:1, 4096:6144]

    h65 = sbt("h65", (65, T), BF16)
    lhsT65 = sbt("lhsT65", (65, D), BF16)
    rhs9 = sbt("rhs9", (4, T), BF16)
    lhsT9 = sbt("lhsT9", (4, 256), BF16)
    B2rhs = sbt("B2rhs", (2, T), BF16)
    B2lhsT = sbt("B2lhsT", (2, 1024), BF16)
    k8 = sbt("k8", (8, 128), BF16)
    kvbf = sbt("kvbf", (128, 8), BF16)
    sT32 = sbt("sT32", (128, 64))
    wstatA = sbt("wstatA", (128, 32), BF16)
    wstatB = sbt("wstatB", (128, 16), BF16)
    wq8 = sbt("wq8", (128, 8192), FP8)
    e1P = sbt("e1P", (128, 2048), BF16)
    wfS = sbt("wfS", (128, 1024))
    onecbf = sbt("onecbf", (128, 1), BF16)

    vcat = sbt("vcat_s", (128, 24))
    bvec2 = sbt("bvec2_s", (128, 2))
    modT = sbt("modT", (128, 48))
    modsb48 = sbt("modsb48", (48, 128))
    deriv = sbt("deriv", (128, 64))
    Vs = sbt("Vs", (128, 40))
    statT = sbt("statT", (128, 96))
    statT2 = sbt("statT2", (128, 32))
    coefT = sbt("coefT", (128, 48), BF16)
    coefT2 = sbt("coefT2", (128, 16))
    coefT2b = sbt("coefT2b", (128, 16), BF16)
    coefTa = sbt("coefTa", (128, 16))
    rm = sbt("rm", (128, 256))
    sg_ = sbt("sg_", (128, 16))
    kvf = sbt("kvf", (128, 8))
    kvr = sbt("kvr", (128, 24))
    sc5 = sbt("sc5", (5, 1))
    amx8 = sbt("amx8", (128, 8))
    amxc = sbt("amxc", (128, 1))
    prow = sbt("prow", (1, 128))
    s11 = sbt("s11", (1, 16))
    ones_r = sbt("ones_r", (1, 128))
    ones_c = sbt("ones_c", (128, 1))
    eye = sbt("eye_s", (128, 128))
    eyebf = sbt("eyebf_s", (128, 128), BF16)

    pb0 = pst("pb0", (128, 2048))
    pb1 = pst("pb1", (128, 2048))
    pb1bf = pb1[:, 0:1024].bitcast(BF16)
    pb1bf2 = pb1[:, 1024:2048].bitcast(BF16)

    S = Sch()
    rmv = lambda k: rm[:, k * 16:(k + 1) * 16]
    MCAR, CCAR, MLAST, B2S, WAMA, WAMB, WACC = 208, 209, 210, 211, 213, 214, 215
    SGC = {"sca": 0, "sc2": 1, "spsa": 2, "ssan": 3, "ssa2": 4, "sP1": 5, "sP12": 6,
           "xs2": 7, "deq": 8, "sw2": 9}
    sgc = lambda k: sg_[:, SGC[k]:SGC[k] + 1]

    d_cmat = S.dma("sync", cmat_v, t_cmat.ap()[:, :])
    d_eye = S.dma("sync", eye[:, :], t_eye.ap()[:, :])
    d_wpT = S.dma("sync", wpT_v, t_wpT.ap()[:, :])
    d_bprow = S.dma("sync", saA[0:1, 4096:5120], t_bprow.ap()[:, :])
    wmpark = [r1[:, 10240 + c * 768: 10240 + (c + 1) * 768] for c in range(NCH)]
    d_wmpark = [S.dma("scalar", wmpark[c], t_wmod.ap()[c * 128:(c + 1) * 128, :])
                for c in range(NCH)]
    d_eyebf = S.dma("sync", eyebf[:, :], t_eyebf.ap()[:, :])
    d_vcat = S.dma("sync", vcat[:, :], t_vcat.ap()[:, :])
    d_bvec = S.dma("sync", bvec2[:, :], t_bvec2.ap()[:, :])
    d_x = [S.dma("scalar", xch[c], t_xT.ap()[c * 128:(c + 1) * 128, :]) for c in range(5)]
    d_wall1 = S.dma("scalar", r2[:, 2048:4096], t_wallT.ap()[:, :])
    d_wall = [d_wall1] * NCH

    t_ones = S.add("vector", lambda e: e.memset(ones_r[:, :], 1.0))
    t_ones2 = S.add("vector", lambda e: e.memset(ones_c[:, :], 1.0))
    t_ones3 = S.add("vector", lambda e: e.memset(onecbf[:, :], 1.0))
    t_z11 = S.add("vector", lambda e: e.memset(s11[:, :], 0.0))
    t_zsg = S.add("vector", lambda e: e.memset(sg_[:, :], 0.0))
    t_silu = S.add("scalar", lambda e: e.activation(out=silu_v, in_=cmat_v, func=AF.Silu),
                   waits=[d_cmat])
    tok_tr = [S.add("tensor", lambda e, c=c: e.transpose(
        pb0[:, 1024 + c * 8:1024 + (c + 1) * 8], silu_ch(c), eye[0:B, 0:B]),
        waits=[t_silu, d_eye]) for c in range(NCH)]
    t_sT = S.add("vector", lambda e: e.tensor_copy(out=sT32[:, :], in_=pb0[:, 1024:1088]),
                 waits=tok_tr)
    mmod = {}
    tok_mod = []
    for c in range(NCH):
        mmod[c] = [S.add("tensor", lambda e, c=c, n0=n0, n1=n1: e.matmul(
            pb0[0:B, n0:n1], sT32[:, c * 8:(c + 1) * 8], wmpark[c][:, n0:n1],
            start=(c == 0), stop=(c == NCH - 1)),
            waits=[d_wmpark[c], t_sT]) for (n0, n1) in ((0, 512), (512, JW))]
        tok_mod = tok_mod + mmod[c]
    d_x.append(S.dma("scalar", xch[5], t_xT.ap()[5 * 128:6 * 128, :],
                     waits=mmod[0] + mmod[1] + mmod[2]))
    d_x.append(S.dma("scalar", xch[6], t_xT.ap()[6 * 128:7 * 128, :],
                     waits=mmod[2] + mmod[3] + mmod[4] + mmod[5]))
    d_x.append(S.dma("scalar", xch[7], t_xT.ap()[7 * 128:8 * 128, :],
                     waits=mmod[5] + mmod[6] + mmod[7]))
    t_modev = S.add("vector", lambda e: e.tensor_copy(out=modsb8, in_=pb0[0:B, 0:JW]),
                    waits=tok_mod)
    t_l65a = S.add("vector", lambda e: e.tensor_copy(out=lhsT65[0:NH, :], in_=wpT_v),
                   waits=[d_wpT])
    t_l65b = S.add("vector", lambda e: e.tensor_copy(out=lhsT65[64:65, :],
                                                     in_=saA[0:1, 4096:5120]), waits=[d_bprow])
    d_mdi = S.dma("sync", b_md_i.ap()[:, :], modsb8, waits=[t_modev])

    def _cc_mod(e):
        with allow_ici_all_to_all():
            return e.collective_compute(
                "AllToAll", A.bypass, replica_groups=[list(range(B))],
                ins=[b_md_i.ap().opt()], outs=[b_md_o.ap().opt()])
    cc_mod = S.add("gpsimd", _cc_mod, waits=[d_mdi], sem="cc", inc=1)
    d_mdo = S.dma("sync", modsb48[:, :],
                  b_md_o.ap().rearrange("r (s e) -> (r s) e", e=128), waits=[cc_mod])
    t_modtr = S.add("tensor", lambda e: e.transpose(
        pb1[:, 0:48], modsb48[:, :], eye[0:48, 0:48]), waits=[d_mdo, d_eye])
    t_modT = S.add("vector", lambda e: e.tensor_copy(out=modT[:, :], in_=pb1[:, 0:48]),
                   waits=[t_modtr])

    t_p1 = S.add("vector", lambda e: e.tensor_scalar(
        out=deriv[:, 0:8], in0=modT[:, 8:16], scalar1=1.0, scalar2=None, op0=A.add),
        waits=[t_modT])
    t_p2 = S.add("vector", lambda e: e.tensor_scalar(
        out=deriv[:, 8:16], in0=modT[:, 32:40], scalar1=1.0, scalar2=None, op0=A.add),
        waits=[t_modT])

    def _gab(dst, src):
        t2 = S.add("vector", lambda e: e.tensor_scalar(
            out=deriv[:, dst:dst + 8], in0=modT[:, src:src + 8], scalar1=0.0, scalar2=2.0,
            op0=A.is_ge, op1=A.mult), waits=[t_modT])
        t3 = S.add("vector", lambda e: e.tensor_scalar(
            out=deriv[:, dst:dst + 8], in0=deriv[:, dst:dst + 8], scalar1=-1.0, scalar2=None,
            op0=A.add), waits=[t2])
        t4 = S.add("vector", lambda e: e.tensor_tensor(
            out=deriv[:, 56:64], in0=deriv[:, dst:dst + 8], in1=modT[:, src:src + 8],
            op=A.mult), waits=[t3])
        t5 = S.add("vector", lambda e: e.tensor_scalar(
            out=deriv[:, 56:64], in0=deriv[:, 56:64], scalar1=0.25, scalar2=None,
            op0=A.max), waits=[t4])
        return S.add("vector", lambda e: e.tensor_tensor(
            out=deriv[:, dst:dst + 8], in0=deriv[:, dst:dst + 8], in1=deriv[:, 56:64],
            op=A.mult), waits=[t5])
    t_ga = _gab(16, 16)
    t_gb = _gab(24, 40)
    t_psa = S.add("vector", lambda e: e.tensor_tensor(
        out=deriv[:, 32:40], in0=deriv[:, 0:8], in1=modT[:, 0:8], op=A.mult),
        waits=[t_p1, t_ga])
    t_c2a = S.add("vector", lambda e: e.tensor_scalar(
        out=deriv[:, 40:48], in0=modT[:, 8:16], scalar1=2.0, scalar2=None, op0=A.add),
        waits=[t_modT])
    t_c2 = S.add("vector", lambda e: e.tensor_tensor(
        out=deriv[:, 40:48], in0=deriv[:, 40:48], in1=modT[:, 8:16], op=A.mult),
        waits=[t_c2a])
    t_sa2 = S.add("vector", lambda e: e.tensor_tensor(
        out=deriv[:, 48:56], in0=modT[:, 0:8], in1=modT[:, 0:8], op=A.mult),
        waits=[t_modT])

    wsA_j = wstatA.ap().rearrange("p (c j) -> p j c", j=4)
    wsB_j = wstatB.ap().rearrange("p (c j) -> p j c", j=2)
    t_ws = [S.add("vector", lambda e: e.memset(wstatA[:, :], 1.0)),
            S.add("vector", lambda e: e.memset(wstatB[:, :], 1.0))]
    t_ws.append(S.add("vector", lambda e: e.tensor_copy(out=wsA_j[:, 1, :], in_=modT[:, 8:16]),
                      waits=[t_modT]))
    t_ws.append(S.add("vector", lambda e: e.tensor_copy(out=wsA_j[:, 2, :], in_=deriv[:, 40:48]),
                      waits=[t_c2]))
    t_ws.append(S.add("vector", lambda e: e.tensor_copy(out=wsA_j[:, 3, :], in_=deriv[:, 32:40]),
                      waits=[t_psa]))
    t_ws.append(S.add("vector", lambda e: e.tensor_copy(out=wsB_j[:, 1, :], in_=deriv[:, 40:48]),
                      waits=[t_c2]))
    vs_j = Vs.ap().rearrange("p (c j) -> p j c", j=5)
    t_vs = [S.add("vector", lambda e, j=j, srcv=srcv: e.tensor_copy(out=vs_j[:, j, :], in_=srcv),
                  waits=[wtok])
            for j, (srcv, wtok) in enumerate([(modT[:, 8:16], t_modT), (deriv[:, 40:48], t_c2),
                                              (deriv[:, 32:40], t_psa), (modT[:, 0:8], t_modT),
                                              (deriv[:, 48:56], t_sa2)])]

    xbufs = [xbfP[0], xbfP[1],
             r2[:, 7168:8192].bitcast(BF16), r2[:, 8192:9216].bitcast(BF16),
             r2b[:, 0:1024].bitcast(BF16), r2b[:, 1024:2048].bitcast(BF16),
             r2[:, 0:1024].bitcast(BF16), r2[:, 1024:2048].bitcast(BF16)]
    sqbufs = [sqbfP[0], sqbfP[1],
              ar2[:, 0:1024].bitcast(BF16), ar2[:, 1024:2048].bitcast(BF16),
              e1P[:, :]]
    t_xbf, t_sq = {}, {}
    prevB5 = {}
    for c in range(NCH):
        t_xbf[c] = S.add("vector", lambda e, c=c: e.tensor_copy(
            out=xbufs[c], in_=xch[c]), waits=[d_x[c]] + (tok_tr if c < 2 else []))
    for c in range(5):
        t_sq[c] = S.add("scalar", lambda e, c=c: e.activation(
            out=sqbufs[c % 5], in_=xch[c], func=AF.Square), waits=[d_x[c]])

    t_wl = [S.add("vector", lambda e, c=c: e.tensor_scalar(
        out=wl[c], in0=wallT[c], scalar1=vcat[:, c:c + 1], scalar2=None, op0=A.mult),
        waits=[d_wall[c], d_vcat]) for c in range(NCH)]
    t_w3 = [S.add("vector", lambda e, c=c: e.tensor_scalar(
        out=w3bf[c], in0=wl[c], scalar1=deriv[:, c:c + 1], scalar2=None, op0=A.mult),
        waits=[t_wl[c], t_p1]) for c in range(NCH)]

    kvr_j = kvr.ap().rearrange("p (c j) -> p j c", j=3)
    t_kvr = [S.add("vector", lambda e: e.memset(kvr[:, :], 1.0))]
    t_kvr.append(S.add("vector", lambda e: e.tensor_copy(out=kvr_j[:, 0, :], in_=modT[:, 0:8]),
                       waits=[t_modT]))
    t_kvr.append(S.add("vector", lambda e: e.tensor_copy(out=kvr_j[:, 1, :], in_=deriv[:, 0:8]),
                       waits=[t_p1]))

    tok_A, tok_B = [], []
    for c in range(NCH):
        if c >= 5:
            t_sq[c] = S.add("scalar", lambda e, c=c: e.activation(
                out=sqbufs[c % 5], in_=xch[c], func=AF.Square),
                waits=[d_x[c]] + prevB5[c - 5])
        mmA = [S.add("tensor", lambda e, c=c, n=n: e.matmul(
            pb0[0:4, n * 512:(n + 1) * 512], wstatA[:, c * 4:(c + 1) * 4],
            xbufs[c][:, n * 512:(n + 1) * 512], start=(c == 0), stop=(c == NCH - 1)),
            waits=[t_xbf[c]] + t_ws) for n in range(4)]
        mmB = [S.add("tensor", lambda e, c=c, n=n: e.matmul(
            pb1[0:2, n * 512:(n + 1) * 512], wstatB[:, c * 2:(c + 1) * 2],
            sqbufs[c % 5][:, n * 512:(n + 1) * 512], start=(c == 0), stop=(c == NCH - 1)),
            waits=[t_sq[c]] + t_ws) for n in range(4)]
        prevB5[c] = mmB
        tok_A += mmA
        tok_B += mmB
    t_srA = S.add("vector", lambda e: e.tensor_copy(out=statrowsA, in_=pb0[0:4, :]),
                  waits=tok_A)
    t_srB = S.add("vector", lambda e: e.tensor_copy(out=statrowsB, in_=pb1[0:2, :]),
                  waits=tok_B)
    tok_kv = []
    for mt in range(2):
        for c in range(NCH):
            tok_kv.append(S.add("tensor", lambda e, c=c, mt=mt: e.matmul(
                pb0[:, mt * 4:mt * 4 + 3], wl[c][:, mt * 128:(mt + 1) * 128],
                kvr[:, c * 3:(c + 1) * 3], start=(c == 0), stop=(c == NCH - 1)),
                waits=[t_wl[c], t_srA] + t_kvr))
        for c in range(NCH):
            tok_kv.append(S.add("tensor", lambda e, c=c, mt=mt: e.matmul(
                pb0[:, mt * 4 + 3:mt * 4 + 4], wallT[c][:, mt * 128:(mt + 1) * 128],
                vcat[:, 8 + c:9 + c], start=(c == 0), stop=(c == NCH - 1)),
                waits=[d_wall[c], d_vcat, t_srA]))
    t_kvf = S.add("vector", lambda e: e.tensor_copy(out=kvf[:, :], in_=pb0[:, 0:8]),
                  waits=tok_kv)
    t_kc = S.add("vector", lambda e: e.tensor_tensor(
        out=kvf[:, 3:4], in0=kvf[:, 3:4], in1=bvec2[:, 0:1], op=A.add),
        waits=[t_kvf, d_bvec])
    t_kc2 = S.add("vector", lambda e: e.tensor_tensor(
        out=kvf[:, 7:8], in0=kvf[:, 7:8], in1=bvec2[:, 1:2], op=A.add),
        waits=[t_kvf, d_bvec])
    t_kvb = S.add("vector", lambda e: e.tensor_copy(
        out=kvbf.ap().rearrange("p (v b) -> p v b", b=2), in_=kvf.ap().rearrange("p (b v) -> p v b", b=2)),
        waits=[t_kc, t_kc2])
    t_kvt = S.add("tensor", lambda e: e.transpose(pb1bf[0:8, 0:128], kvbf[:, :], eyebf[:, :]),
                  waits=[t_kvb, d_eyebf, t_srB])
    t_k8 = S.add("vector", lambda e: e.tensor_copy(out=k8[:, :], in_=pb1bf[0:8, 0:128]),
                 waits=[t_kvt])
    d_l9 = S.dma("sync", lhsT9.ap().rearrange("p (b c) -> p b c", b=2), k8[:, :],
                 waits=[t_k8])
    tok_ks = [S.add("tensor", lambda e, c=c: e.matmul(
        pb0[0:5, 8:9], Vs[:, c * 5:(c + 1) * 5], ones_c[:, :],
        start=(c == 0), stop=(c == NCH - 1)), waits=t_vs + [t_ones2, t_srA] + tok_kv[-1:])
        for c in range(NCH)]
    t_sc5 = S.add("vector", lambda e: e.tensor_copy(out=sc5[:, :], in_=pb0[0:5, 8:9]),
                  waits=tok_ks)
    d_sc1 = S.dma("sync", b_sc.ap()[0:1, 0:5], sc5[:, :], waits=[t_sc5])
    t_sg = S.dma("sync", sg_[:, 0:5], b_sc.ap()[0:1, 0:5].partition_broadcast(128),
                 waits=[d_sc1, t_zsg])
    t_sP1 = S.add("vector", lambda e: e.tensor_scalar(
        out=sgc("sP1"), in0=sgc("sca"), scalar1=1024.0, scalar2=None, op0=A.add),
        waits=[t_sg])
    t_sP12 = S.add("vector", lambda e: e.tensor_scalar(
        out=sgc("sP12"), in0=sgc("sc2"), scalar1=1024.0, scalar2=None, op0=A.add),
        waits=[t_sg])
    t_ssan = S.add("vector", lambda e: e.tensor_scalar(
        out=sgc("ssan"), in0=sgc("ssan"), scalar1=1.0 / 1024.0, scalar2=None, op0=A.mult),
        waits=[t_sg])

    tok_st = []
    for tt in range(16):
        tok_st.append(S.add("tensor", lambda e, tt=tt: e.transpose(
            pb1[:, 512 + tt * 6:512 + tt * 6 + 4],
            statrowsA[:, tt * 128:(tt + 1) * 128], eye[0:4, 0:4]),
            waits=[t_srA, t_srB, t_sc5]))
        tok_st.append(S.add("tensor", lambda e, tt=tt: e.transpose(
            pb1[:, 512 + tt * 6 + 4:512 + tt * 6 + 6],
            statrowsB[:, tt * 128:(tt + 1) * 128], eye[0:2, 0:2]),
            waits=[t_srA, t_srB, t_sc5]))
    t_stT = S.add("vector", lambda e: e.tensor_copy(out=statT[:, :], in_=pb1[:, 512:608]),
                  waits=tok_st)


    st_ap = statT.ap().rearrange("p (tt j) -> p j tt", j=6)
    V_S0, V_Sca, V_Sc2, V_Spsa, V_Q0, V_Qc2 = (st_ap[:, j, :] for j in range(6))

    rmt = []

    def vop(emit, waits=()):
        tok = S.add("vector", emit, waits=list(waits) + [t_stT])
        rmt.append(tok)
        return tok
    vop(lambda e: e.tensor_scalar(out=rmv(0), in0=V_S0, scalar1=1.0 / 1024, scalar2=None, op0=A.mult))
    vop(lambda e: e.tensor_tensor(out=rmv(2), in0=rmv(0), in1=rmv(0), op=A.mult))
    vop(lambda e: e.scalar_tensor_tensor(out=rmv(3), in0=V_Q0, scalar=1.0 / 1024,
                                         in1=rmv(2), op0=A.mult, op1=A.subtract))
    vop(lambda e: e.tensor_scalar(out=rmv(3), in0=rmv(3), scalar1=1e-5, scalar2=None, op0=A.add))
    vop(lambda e: e.reciprocal(out=rmv(5), in_=rmv(3)))
    t_rx = S.add("scalar", lambda e: e.activation(out=rmv(6), in_=rmv(5), func=AF.Sqrt),
                 waits=rmt[-1:])
    vop(lambda e: e.tensor_tensor(out=rmv(7), in0=V_S0, in1=V_Sca, op=A.add))
    vop(lambda e: e.scalar_tensor_tensor(out=rmv(8), in0=rmv(0), scalar=sgc("sP1"),
                                         in1=rmv(7), op0=A.mult, op1=A.subtract),
        waits=[t_sP1])
    vop(lambda e: e.tensor_tensor(out=rmv(9), in0=rmv(8), in1=rmv(6), op=A.mult),
        waits=[t_rx])
    vop(lambda e: e.tensor_scalar(out=rmv(7), in0=rmv(9), scalar1=-1.0 / 1024,
                                  scalar2=sgc("ssan"), op0=A.mult, op1=A.add),
        waits=[t_ssan])
    vop(lambda e: e.tensor_tensor(out=rmv(8), in0=V_S0, in1=V_Sc2, op=A.add))
    vop(lambda e: e.tensor_tensor(out=rmv(9), in0=V_Q0, in1=V_Qc2, op=A.add))
    vop(lambda e: e.tensor_scalar(out=rmv(10), in0=rmv(2), scalar1=sgc("sP12"),
                                  scalar2=None, op0=A.mult), waits=[t_sP12])
    vop(lambda e: e.tensor_tensor(out=rmv(11), in0=rmv(0), in1=rmv(8), op=A.mult))
    vop(lambda e: e.scalar_tensor_tensor(out=rmv(9), in0=rmv(11), scalar=-2.0,
                                         in1=rmv(9), op0=A.mult, op1=A.add))
    vop(lambda e: e.tensor_tensor(out=rmv(9), in0=rmv(9), in1=rmv(10), op=A.add))
    vop(lambda e: e.tensor_tensor(out=rmv(9), in0=rmv(9), in1=rmv(5), op=A.mult))
    vop(lambda e: e.scalar_tensor_tensor(out=rmv(10), in0=rmv(0), scalar=sgc("spsa"),
                                         in1=V_Spsa, op0=A.mult, op1=A.subtract),
        waits=[t_sg])
    vop(lambda e: e.tensor_tensor(out=rmv(10), in0=rmv(10), in1=rmv(6), op=A.mult),
        waits=[t_rx])
    vop(lambda e: e.scalar_tensor_tensor(out=rmv(9), in0=rmv(10), scalar=-2.0,
                                         in1=rmv(9), op0=A.mult, op1=A.add))
    vop(lambda e: e.tensor_scalar(out=rmv(9), in0=rmv(9), scalar1=sgc("ssa2"),
                                  scalar2=1.0 / 1024, op0=A.add, op1=A.mult), waits=[t_sg])
    vop(lambda e: e.tensor_tensor(out=rmv(10), in0=rmv(7), in1=rmv(7), op=A.mult))
    vop(lambda e: e.tensor_tensor(out=rmv(9), in0=rmv(9), in1=rmv(10), op=A.subtract))
    vop(lambda e: e.tensor_scalar(out=rmv(9), in0=rmv(9), scalar1=1e-5, scalar2=None, op0=A.add))
    vop(lambda e: e.reciprocal(out=rmv(11), in_=rmv(9)))
    t_rh = S.add("scalar", lambda e: e.activation(out=rmv(10), in_=rmv(11), func=AF.Sqrt),
                 waits=rmt[-1:])
    vop(lambda e: e.tensor_tensor(out=rmv(11), in0=rmv(6), in1=rmv(10), op=A.mult),
        waits=[t_rh])
    vop(lambda e: e.tensor_tensor(out=rmv(12), in0=rmv(11), in1=rmv(0), op=A.mult))
    vop(lambda e: e.tensor_tensor(out=rmv(4), in0=rmv(10), in1=rmv(7), op=A.mult))
    co_ap = coefT.ap().rearrange("p (tt j) -> p j tt", j=3)
    t_co = [S.add("vector", lambda e, j=j, slab=slab, sg=sg: e.tensor_scalar(
        out=co_ap[:, j, :], in0=rmv(slab), scalar1=sg, scalar2=None, op0=A.mult),
        waits=rmt[-6:] + [t_rh]) for j, slab, sg in [(0, 10, 1.0), (1, 12, -1.0), (2, 4, -1.0)]]
    t_coa = S.add("vector", lambda e: e.tensor_copy(
        out=coefTa[:, :], in_=rmv(11)), waits=rmt[-6:] + [t_rh])
    tok_ct = [S.add("tensor", lambda e, tt=tt: e.transpose(
        pb1bf[0:3, tt * 128:(tt + 1) * 128], coefT[:, tt * 3:(tt + 1) * 3], eyebf[:, :]),
        waits=t_co + [t_k8, t_stT]) for tt in range(16)]
    t_r9a = S.add("vector", lambda e: e.tensor_copy(out=rhs9[0:3, :], in_=pb1bf[0:3, 0:2048]),
                  waits=tok_ct)
    t_r9b = S.dma("sync", rhs9[3:4, :], t_onesbf.ap()[:, :])
    tok_cta = [S.add("tensor", lambda e, tt=tt: e.transpose(
        pb1[0:1, tt * 128:(tt + 1) * 128], coefTa[:, tt:tt + 1], eye[:, :]),
        waits=[t_coa, t_r9a]) for tt in range(16)]
    t_arow = S.add("vector", lambda e: e.tensor_copy(out=arow, in_=pb1[0:1, 0:2048]),
                   waits=tok_cta + [t_l65b])
    d_ar1 = S.dma("sync", b_row.ap()[:, :], arow, waits=[t_arow])
    t_asb = S.dma("sync", ar2[:, :], b_row.ap().partition_broadcast(128),
                  waits=[d_ar1] + tok_B[-1:])


    tok_G = []
    first_w = [t_kvf, t_sc5, t_srA, t_srB, t_stT, t_k8, t_r9a, t_arow]
    for c in range(NCH + 1):
        for mt in range(2):
            pg = pb0 if mt == 0 else pb1
            for n in range(4):
                if c < NCH:
                    tok_G.append(S.add("tensor", lambda e, c=c, mt=mt, n=n, pg=pg: e.matmul(
                        pg[:, n * 512:(n + 1) * 512], w3bf[c][:, mt * 128:(mt + 1) * 128],
                        xbufs[c][:, n * 512:(n + 1) * 512],
                        start=(c == 0), stop=False),
                        waits=[t_xbf[c], t_w3[c]] + (first_w if c == 0 else [])))
                else:
                    tok_G.append(S.add("tensor", lambda e, mt=mt, n=n, pg=pg: e.matmul(
                        pg[:, n * 512:(n + 1) * 512], lhsT9[:, mt * 128:(mt + 1) * 128],
                        rhs9[:, n * 512:(n + 1) * 512], start=False, stop=True),
                        waits=[d_l9, t_r9b]))
    t_gt0 = [S.add("vector", lambda e, n=n: e.tensor_tensor(
        out=gt0[:, n * 512:(n + 1) * 512], in0=pb0[:, n * 512:(n + 1) * 512],
        in1=ar2[:, n * 512:(n + 1) * 512], op=A.mult), waits=tok_G + [t_asb])
        for n in range(4)]
    t_gt1 = [S.add("vector", lambda e, n=n: e.tensor_tensor(
        out=gt1[:, n * 512:(n + 1) * 512], in0=pb1[:, n * 512:(n + 1) * 512],
        in1=ar2[:, n * 512:(n + 1) * 512], op=A.mult), waits=tok_G + [t_asb])
        for n in range(4)]
    t_clip = S.add("vector", lambda e: e.tensor_scalar(
        out=gt0[:, :], in0=gt0[:, :], scalar1=20.0, scalar2=-20.0, op0=A.min, op1=A.max),
        waits=t_gt0)

    wq8f32 = wq8.bitcast(F32)
    P1R = [wq8f32[:, 0:1024], wq8f32[:, 1024:2048], wfS[:, 0:1024]]
    e1Pf32 = e1P.bitcast(F32)
    P2R = [e1Pf32[:, 0:1024], wfS[:, 0:1024]]
    q8tmp = e1P.bitcast(FP8)[:, 0:2048]
    wred_prev = []
    wacc_tok = [None]
    d_wf1 = []

    def emit_wf1_dma(c):
        w = [] if c < 3 else [wred_prev[c - 3]]
        d_wf1.append(S.dma("scalar", P1R[c % 3], t_wfT.ap()[c * 128:(c + 1) * 128, :],
                           waits=w))

    def emit_wred(c):
        tk = S.add("vector", lambda e, c=c: e.tensor_reduce(
            out=rm[:, 208 + (c % 4):208 + (c % 4) + 1], in_=P1R[c % 3], axis=AX.X,
            op=A.max, apply_absolute_value=True), waits=[d_wf1[c]])
        if c == 0:
            wa = S.add("vector", lambda e: e.tensor_copy(
                out=rm[:, WACC:WACC + 1], in_=rm[:, 208:209]), waits=[tk])
        else:
            wa = S.add("vector", lambda e, c=c: e.tensor_tensor(
                out=rm[:, WACC:WACC + 1], in0=rm[:, WACC:WACC + 1],
                in1=rm[:, 208 + (c % 4):208 + (c % 4) + 1], op=A.max),
                waits=[tk, wacc_tok[0]])
        wacc_tok[0] = wa
        wred_prev.append(tk)
    for c in range(3):
        emit_wf1_dma(c)

    t_h1 = S.add("vector", lambda e: e.memset(h65[64:65, :], 1.0))
    s_lf = saA[0:64, 0:2048]
    s_i = saA[0:64, 2048:4096]
    s_m = saA[0:64, 4096:6144]
    li_v = gt0[0:64, :]
    z_v = gt1[0:64, :]
    o_v = ar2[0:64, :]
    d_lf = S.dma("sync", s_lf, gt0[64:128, :], waits=[t_clip])
    d_oo = S.dma("scalar", o_v, gt1[64:128, :], waits=t_gt1)
    t_zt = S.add("scalar", lambda e: e.activation(out=z_v, in_=z_v, func=AF.Tanh),
                 waits=t_gt1)
    t_ot = S.add("scalar", lambda e: e.activation(out=o_v, in_=o_v, func=AF.Tanh,
                                                  scale=0.5), waits=[d_oo])
    t_ms = S.add("vector", lambda e: e.tensor_tensor_scan(
        out=s_m, data0=s_lf, data1=li_v, initial=0.0, op0=A.add, op1=A.max),
        waits=[d_lf, t_clip])
    t_i1 = S.add("gpsimd", lambda e: e.tensor_tensor(
        out=s_i, in0=li_v, in1=s_m, op=A.subtract), waits=[t_ms])
    t_d1 = S.add("vector", lambda e: e.tensor_tensor(
        out=s_lf, in0=s_lf, in1=s_m, op=A.subtract), waits=[t_ms])
    t_f2 = S.add("vector", lambda e: e.tensor_tensor(
        out=s_lf[:, 1:T], in0=s_lf[:, 1:T], in1=s_m[:, 0:T - 1], op=A.add),
        waits=[t_d1])
    t_oa = S.add("vector", lambda e: e.tensor_scalar(
        out=o_v, in0=o_v, scalar1=0.5, scalar2=0.5, op0=A.mult, op1=A.add),
        waits=[t_ot])
    t_ex = S.add("scalar", lambda e: e.activation(
        out=saA[0:64, 0:4096], in_=saA[0:64, 0:4096], func=AF.Exp),
        waits=[t_f2, t_i1])
    t_u = S.add("vector", lambda e: e.tensor_tensor(
        out=s_i, in0=s_i, in1=z_v, op=A.mult), waits=[t_ex, t_zt])
    t_cs = S.add("vector", lambda e: e.tensor_tensor_scan(
        out=s_m, data0=s_lf, data1=s_i, initial=0.0, op0=A.mult, op1=A.add),
        waits=[t_u])
    t_tc = S.add("scalar", lambda e: e.activation(out=s_m, in_=s_m, func=AF.Tanh),
                 waits=[t_cs])
    t_h65 = [S.add("vector", lambda e: e.tensor_tensor(
        out=h65[0:64, :], in0=o_v, in1=s_m, op=A.mult), waits=[t_tc, t_oa])]

    tok_res1 = []
    prev_ev = {}
    for m in range(NCH):
        pg = pb0 if m % 2 == 0 else pb1
        psum_free = [prev_ev[m - 2]] if m >= 2 else (t_gt1 + [t_clip])
        mms = [S.add("tensor", lambda e, m=m, n=n, pg=pg: e.matmul(
            pg[:, n * 512:(n + 1) * 512], lhsT65[:, m * 128:(m + 1) * 128],
            h65[:, n * 512:(n + 1) * 512], start=True, stop=True),
            waits=t_h65 + [t_h1, t_l65a, t_l65b] + psum_free) for n in range(4)]
        region_dead = [t_kvt] if m < 2 else []
        ev = S.add("vector", lambda e, m=m, pg=pg: e.scalar_tensor_tensor(
            out=x1[m], in0=pg[:, :], scalar=deriv[:, 16 + m:17 + m], in1=xch[m],
            op0=A.mult, op1=A.add), waits=mms + [t_ga] + region_dead)
        prev_ev[m] = ev
        tok_res1.append(ev)


    t_x1bf, t_x1sq = {}, {}
    tok_S1, tok_Q1 = [], []
    prevA, prevB = [], []
    prev2A, prev2B = [], []
    for c in range(NCH):
        t_x1bf[c] = S.add("vector", lambda e, c=c: e.tensor_copy(
            out=xbf2P[c % 2], in_=x1[c]), waits=[tok_res1[c], t_asb] + prev2A)
        t_x1sq[c] = S.add("scalar", lambda e, c=c: e.activation(
            out=sqbf2P[c % 2], in_=x1[c], func=AF.Square),
            waits=[tok_res1[c]] + prev2B)
        mmA = [S.add("tensor", lambda e, c=c, n=n: e.matmul(
            pb0[0:1, n * 512:(n + 1) * 512], onecbf[:, :],
            xbf2P[c % 2][:, n * 512:(n + 1) * 512], start=(c == 0), stop=(c == NCH - 1)),
            waits=[t_x1bf[c], t_ones3] + ([tok_res1[6]] if c == 0 else []))
            for n in range(4)]
        mmB = [S.add("tensor", lambda e, c=c, n=n: e.matmul(
            pb1[0:1, n * 512:(n + 1) * 512], onecbf[:, :],
            sqbf2P[c % 2][:, n * 512:(n + 1) * 512], start=(c == 0), stop=(c == NCH - 1)),
            waits=[t_x1sq[c], t_ones3] + ([tok_res1[7]] if c == 0 else []))
            for n in range(4)]
        prev2A, prev2B = prevA, prevB
        prevA, prevB = mmA, mmB
        tok_S1 += mmA
        tok_Q1 += mmB
    t_sr2a = S.add("vector", lambda e: e.tensor_copy(out=statrows2a, in_=pb0[0:1, :]),
                   waits=tok_S1)
    t_sr2b = S.add("vector", lambda e: e.tensor_copy(out=statrows2b, in_=pb1[0:1, :]),
                   waits=tok_Q1)
    tok_st2 = []
    for tt in range(16):
        tok_st2.append(S.add("tensor", lambda e, tt=tt: e.transpose(
            pb0[:, 512 + tt * 2:512 + tt * 2 + 1],
            statrows2a[:, tt * 128:(tt + 1) * 128], eye[0:1, 0:1]),
            waits=[t_sr2a, t_sr2b]))
        tok_st2.append(S.add("tensor", lambda e, tt=tt: e.transpose(
            pb0[:, 512 + tt * 2 + 1:512 + tt * 2 + 2],
            statrows2b[:, tt * 128:(tt + 1) * 128], eye[0:1, 0:1]),
            waits=[t_sr2a, t_sr2b]))
    t_stT2 = S.add("vector", lambda e: e.tensor_copy(out=statT2[:, :], in_=pb0[:, 512:544]),
                   waits=tok_st2)
    st2_ap = statT2.ap().rearrange("p (tt j) -> p j tt", j=2)
    rm2 = []

    def vop2(emit, waits=()):
        tok = S.add("vector", emit, waits=list(waits) + [t_stT2])
        rm2.append(tok)
        return tok
    vop2(lambda e: e.tensor_scalar(out=rmv(0), in0=st2_ap[:, 0, :], scalar1=1.0 / 1024,
                                   scalar2=None, op0=A.mult))
    vop2(lambda e: e.tensor_scalar(out=rmv(1), in0=st2_ap[:, 1, :], scalar1=1.0 / 1024,
                                   scalar2=None, op0=A.mult))
    vop2(lambda e: e.tensor_tensor(out=rmv(2), in0=rmv(0), in1=rmv(0), op=A.mult))
    vop2(lambda e: e.tensor_tensor(out=rmv(3), in0=rmv(1), in1=rmv(2), op=A.subtract))
    vop2(lambda e: e.tensor_scalar(out=rmv(3), in0=rmv(3), scalar1=1e-5, scalar2=None, op0=A.add))
    vop2(lambda e: e.reciprocal(out=rmv(4), in_=rmv(3)))
    t_r2v = S.add("scalar", lambda e: e.activation(out=rmv(5), in_=rmv(4), func=AF.Sqrt),
                  waits=rm2[-1:])
    vop2(lambda e: e.tensor_tensor(out=rmv(6), in0=rmv(0), in1=rmv(5), op=A.mult),
         waits=[t_r2v])
    t_c2w = [S.add("vector", lambda e: e.tensor_copy(out=coefT2[:, :], in_=rmv(5)),
                   waits=rm2[-1:] + [t_r2v]),
             S.add("vector", lambda e: e.tensor_scalar(
                 out=coefT2b[:, :], in0=rmv(6), scalar1=-1.0, scalar2=None, op0=A.mult),
                 waits=rm2[-1:])]
    tok_c2t = [S.add("tensor", lambda e, tt=tt: e.transpose(
        pb1[0:1, tt * 128:(tt + 1) * 128], coefT2[:, tt:tt + 1], eye[:, :]),
        waits=t_c2w[:1] + [t_sr2b] + tok_Q1) for tt in range(16)]
    t_r2r = S.add("vector", lambda e: e.tensor_copy(out=r2row, in_=pb1[0:1, 0:2048]),
                  waits=tok_c2t)
    d_r2a = S.dma("sync", b_row2.ap()[:, :], r2row, waits=[t_r2r])
    t_r2sb = S.dma("sync", ar2[:, :], b_row2.ap().partition_broadcast(128),
                   waits=[d_r2a, t_asb] + tok_S1)
    tok_c2tb = [S.add("tensor", lambda e, tt=tt: e.transpose(
        pb1bf2[0:1, tt * 128:(tt + 1) * 128], coefT2b[:, tt:tt + 1],
        eyebf[:, :]), waits=t_c2w[1:2] + [t_r2r]) for tt in range(16)]
    t_b2r2 = S.add("vector", lambda e: e.tensor_copy(
        out=B2rhs[0:1, :], in_=pb1bf2[0:1, 0:2048]), waits=tok_c2tb)
    t_b2r1 = S.dma("sync", B2rhs[1:2, :], t_onesbf.ap()[:, :])
    tok_rb = [t_r2sb]
    t_b2l = []
    tok_b2t = []
    mps = rm.ap().rearrange("p n -> p n")[:, 216:232].rearrange("p (c j) -> p j c", j=2)
    t_mp1 = S.add("vector", lambda e: e.tensor_copy(out=mps[:, 0, :], in_=deriv[:, 8:16]),
                  waits=[t_p2])
    t_mp2 = S.add("vector", lambda e: e.tensor_copy(out=mps[:, 1, :], in_=modT[:, 24:32]),
                  waits=[t_modT])
    for c in range(NCH):
        tt_ = S.add("tensor", lambda e, c=c: e.transpose(
            pb1[0:2, 1152:1280], rm[:, 216 + c * 2:218 + c * 2], eye[:, :]),
            waits=[t_mp1, t_mp2, t_b2r2] + tok_rb + (tok_b2t[-1:] if tok_b2t else [t_sr2b] + tok_Q1))
        ev = S.add("vector", lambda e, c=c: e.tensor_copy(
            out=B2lhsT[:, c * 128:(c + 1) * 128], in_=pb1[0:2, 1152:1280]), waits=[tt_])
        tok_b2t.append(ev)
        t_b2l.append(ev)

    tok_amx = []
    t_h2 = {}
    prev_b2 = {}
    for c in range(NCH):
        pg = pb0 if c % 2 == 0 else pb1
        t_t1 = S.add("vector" if c % 2 == 0 else "gpsimd", lambda e, c=c: e.tensor_tensor(
            out=xch[c], in0=x1[c], in1=ar2[:, :], op=A.mult),
            waits=[t_r2sb, tok_res1[c], t_x1bf[c], t_x1sq[c]])
        mmb2 = [S.add("tensor", lambda e, c=c, n=n, pg=pg: e.matmul(
            pg[:, n * 512:(n + 1) * 512], B2lhsT[:, c * 128:(c + 1) * 128],
            B2rhs[:, n * 512:(n + 1) * 512], start=True, stop=True),
            waits=[t_b2l[c], t_b2r1, t_b2r2] +
                  ([prev_b2[c - 2]] if c >= 2 else [t_r2sb] + tok_b2t))
            for n in range(4)]
        t_h2[c] = S.add("vector", lambda e, c=c, pg=pg: e.scalar_tensor_tensor(
            out=xch[c], in0=xch[c], scalar=deriv[:, 8 + c:9 + c], in1=pg[:, :],
            op0=A.mult, op1=A.add), waits=mmb2 + [t_t1, t_p2])
        prev_b2[c] = t_h2[c]
        tok_amx.append(S.add("vector", lambda e, c=c: e.tensor_reduce(
            out=amx8[:, c:c + 1], in_=xch[c], axis=AX.X, op=A.max,
            apply_absolute_value=True), waits=[t_h2[c]]))
        emit_wred(c)
        if c + 3 < NCH:
            emit_wf1_dma(c + 3)
    t_wl_ = S.add("gpsimd", lambda e: e.tensor_reduce(
        out=s11[0:1, 6:7], in_=rm[:, WACC:WACC + 1], axis=AX.C, op=A.max),
        waits=[wacc_tok[0]])
    t_ws1 = S.add("vector", lambda e: e.tensor_scalar(
        out=s11[0:1, 6:7], in0=s11[0:1, 6:7], scalar1=1e-12, scalar2=None, op0=A.max),
        waits=[t_wl_])
    t_ws2 = S.add("vector", lambda e: e.reciprocal(out=s11[0:1, 7:8], in_=s11[0:1, 6:7]),
                  waits=[t_ws1])
    t_ws3 = S.add("vector", lambda e: e.tensor_scalar(
        out=s11[0:1, 7:8], in0=s11[0:1, 7:8], scalar1=448.0, scalar2=448.0,
        op0=A.mult, op1=A.min), waits=[t_ws2])
    t_ws4 = S.add("vector", lambda e: e.reciprocal(out=s11[0:1, 8:9], in_=s11[0:1, 7:8]),
                  waits=[t_ws3])
    t_ws5 = S.add("vector", lambda e: e.tensor_scalar(
        out=s11[0:1, 9:10], in0=s11[0:1, 7:8], scalar1=0.5, scalar2=None, op0=A.mult),
        waits=[t_ws3])
    t_sw2b_mm = S.add("tensor", lambda e: e.matmul(
        pb1[:, 1408:1409], ones_r[:, :], s11[0:1, 9:10], start=True, stop=True),
        waits=[t_ws5, t_ones, t_h2[7]])
    t_sw2b = S.add("vector", lambda e: e.tensor_copy(out=sgc("sw2"), in_=pb1[:, 1408:1409]),
                   waits=[t_sw2b_mm])
    t_aml = S.add("gpsimd", lambda e: e.tensor_reduce(
        out=s11[0:1, 0:1], in_=amx8[:, :], axis=AX.XYZWC, op=A.max), waits=tok_amx)
    d_ami = S.dma("sync", b_am_i.ap()[:, :], s11[0:1, 0:1], waits=[t_aml, t_z11])
    tok_wq = []
    prev_q = []
    for c in range(NCH):
        bw = ([wred_prev[5]] if c == 1 else
              (prevB5[4] if c == 0 else []))
        d = S.dma("scalar", P2R[c % 2], t_wfT.ap()[c * 128:(c + 1) * 128, :],
                  waits=bw + (prev_q[c - 2:c - 1] if c >= 2 else []))
        tq = S.add("vector", lambda e, c=c: e.tensor_scalar(
            out=wq8[:, c * 1024:(c + 1) * 1024], in0=P2R[c % 2], scalar1=sgc("sw2"),
            scalar2=None, op0=A.mult), waits=[d, t_sw2b])
        prev_q.append(tq)
        tok_wq.append(tq)
    cc_am = S.add("gpsimd", lambda e: e.collective_compute(
        "AllReduce", A.max, replica_groups=[list(range(B))],
        ins=[b_am_i.ap().opt()], outs=[b_am_o.ap().opt()]),
        waits=[d_ami, cc_mod], sem="cc", inc=1)
    t_amg = S.dma("sync", s11[0:1, 1:2], b_am_o.ap()[:, :], waits=[cc_am])

    t_sc1 = S.add("vector", lambda e: e.tensor_scalar(
        out=s11[0:1, 1:2], in0=s11[0:1, 1:2], scalar1=1e-12, scalar2=None, op0=A.max),
        waits=[t_amg])
    t_sc2_ = S.add("vector", lambda e: e.reciprocal(out=s11[0:1, 2:3], in_=s11[0:1, 1:2]),
                   waits=[t_sc1])
    t_sc3 = S.add("vector", lambda e: e.tensor_scalar(
        out=s11[0:1, 2:3], in0=s11[0:1, 2:3], scalar1=448.0, scalar2=448.0,
        op0=A.mult, op1=A.min), waits=[t_sc2_])
    t_sc4 = S.add("vector", lambda e: e.reciprocal(out=s11[0:1, 3:4], in_=s11[0:1, 2:3]),
                  waits=[t_sc3])
    t_sc5_ = S.add("vector", lambda e: e.tensor_scalar(
        out=s11[0:1, 4:5], in0=s11[0:1, 2:3], scalar1=0.5, scalar2=None, op0=A.mult),
        waits=[t_sc3])
    t_dq1 = S.add("vector", lambda e: e.tensor_tensor(
        out=s11[0:1, 10:11], in0=s11[0:1, 3:4], in1=s11[0:1, 8:9], op=A.mult),
        waits=[t_sc4, t_ws4])
    t_dq2 = S.add("vector", lambda e: e.tensor_scalar(
        out=s11[0:1, 5:6], in0=s11[0:1, 10:11], scalar1=4.0, scalar2=None, op0=A.mult),
        waits=[t_dq1])
    t_xsb_mm = S.add("tensor", lambda e: e.matmul(
        pb0[:, 140:141], ones_r[:, :], s11[0:1, 4:5], start=True, stop=True),
        waits=[t_sc5_, t_sr2a])
    t_dqb_mm = S.add("tensor", lambda e: e.matmul(
        pb0[:, 141:142], ones_r[:, :], s11[0:1, 5:6], start=True, stop=True),
        waits=[t_dq2, t_sr2a])
    t_xsb = S.add("vector", lambda e: e.tensor_copy(out=sgc("xs2"), in_=pb0[:, 140:141]),
                  waits=[t_xsb_mm])
    t_dqb = S.add("vector", lambda e: e.tensor_copy(out=sgc("deq"), in_=pb0[:, 141:142]),
                  waits=[t_dqb_mm])

    r2bq = r2b.bitcast(FP8)
    wfSq = wfS.bitcast(FP8)
    qdst = [r2bq[:, k * 2048:(k + 1) * 2048] for k in range(4)] +            [wfSq[:, k * 2048:(k + 1) * 2048] for k in range(2)]
    tok_q8 = []
    for c in range(NCH):
        if c < 4:
            extra = tok_Q1[-1:]
        else:
            extra = [tok_wq[7]]
        if c < 6:
            tok_q8.append(S.add("vector", lambda e, c=c: e.tensor_scalar(
                out=qdst[c], in0=xch[c], scalar1=sgc("xs2"),
                scalar2=None, op0=A.mult),
                waits=[t_h2[c], t_xsb, tok_amx[c]] + extra))
        else:
            tq = S.add("vector", lambda e, c=c: e.tensor_scalar(
                out=q8tmp[:, :], in0=xch[c], scalar1=sgc("xs2"),
                scalar2=None, op0=A.mult),
                waits=[t_h2[c], t_xsb, tok_amx[c], tok_wq[6]])
            tok_q8.append(S.add("vector", lambda e, c=c: e.tensor_copy(
                out=q8[c], in_=q8tmp[:, :]), waits=[tq]))

    wq8_3d = wq8.ap().rearrange("p (k m) -> p k m", k=NCH)
    q8_3d = r1.ap().bitcast(FP8).rearrange("p (c x) -> p c x", c=NCH)
    r2b_3d = r2b.bitcast(FP8).ap().rearrange("p (c x) -> p c x", c=4)
    wfS_3d = wfS.bitcast(FP8).ap().rearrange("p (c x) -> p c x", c=2)
    rhs_pairs = [r2b_3d[:, 0:2, :], r2b_3d[:, 2:4, :],
                 wfS_3d[:, 0:2, :], q8_3d[:, 6:8, 0:2048]]
    fb_col = lambda m: vcat[:, 16 + m:17 + m]
    prev_e1 = {}
    prev_x2 = {}
    tok_out = []
    for m in range(NCH):
        pg = pb0 if m % 2 == 0 else pb1
        mm_w0 = ([prev_e1[m - 2]] if m >= 2
                 else [t_xsb, t_dqb] + ([t_sw2b] if m == 1 else []))
        mms = []
        for n in range(4):
            for kp in range(NCH // 2):
                mms.append(S.add("tensor", lambda e, m=m, n=n, kp=kp, pg=pg: e.matmul(
                    pg[:, n * 512:(n + 1) * 512],
                    wq8_3d[:, 2 * kp:2 * kp + 2, m * 128:(m + 1) * 128],
                    rhs_pairs[kp][:, :, n * 512:(n + 1) * 512],
                    start=(kp == 0), stop=(kp == NCH // 2 - 1),
                    perf_mode=mybir.MatmulPerfMode.DoubleRow),
                    waits=[tok_wq[2 * kp], tok_wq[2 * kp + 1], tok_q8[2 * kp],
                           tok_q8[2 * kp + 1]] + (mm_w0 if (kp == 0 and n == 0) else [])))
        t_e1 = S.add("scalar", lambda e, m=m, pg=pg: e.activation(
            out=e1P[:, :], in_=pg[:, :], func=AF.Identity,
            scale=sgc("deq"), bias=fb_col(m)),
            waits=mms + [t_dqb] + ([prev_x2[m - 1]] if m >= 1 else []))
        prev_e1[m] = t_e1
        t_x2 = S.add("vector", lambda e, m=m: e.scalar_tensor_tensor(
            out=x1[m], in0=e1P[:, :], scalar=deriv[:, 24 + m:25 + m],
            in1=x1[m], op0=A.mult, op1=A.add),
            waits=[t_e1, t_gb, tok_q8[m]])
        prev_x2[m] = t_x2
        tok_out.append(S.dma("sync", t_out.ap()[m * 128:(m + 1) * 128, :], x1[m],
                             waits=[t_x2]))

    if dbg:
        tok_out.append(S.dma("sync", dbg_outs["d_modT"].ap()[:, :], modT[:, :], waits=[t_modT]))
        tok_out.append(S.dma("sync", dbg_outs["d_statT"].ap()[:, :], statT[:, :], waits=[t_stT]))
        tok_out.append(S.dma("sync", dbg_outs["d_coefT"].ap()[:, :], coefT[:, :], waits=t_co))
        tok_out.append(S.dma("sync", dbg_outs["d_kv"].ap()[:, :], kvf[:, :], waits=[t_kc]))
        tok_out.append(S.dma("sync", dbg_outs["d_sg"].ap()[:, :], sg_[:, :], waits=[t_sP1, t_sP12, t_ssan, t_xsb, t_dqb, t_sw2b]))
        tok_out.append(S.dma("sync", dbg_outs["d_q8u"].ap()[:, :], q8[0].bitcast(mybir.dt.uint8),
                             waits=[tok_q8[0]] + [prev_x2[m] for m in range(NCH)]))
        tok_out.append(S.dma("sync", dbg_outs["d_wq8u"].ap()[:, :], wq8[:, 0:1024].bitcast(mybir.dt.uint8),
                             waits=[tok_wq[0]]))
        tok_out.append(S.dma("sync", dbg_outs["d_gt0"].ap()[:, :], gt0, waits=[t_clip]))
        tok_out.append(S.dma("sync", dbg_outs["d_gt1"].ap()[:, :], gt1, waits=t_gt1))
        tok_out.append(S.dma("sync", dbg_outs["d_x1"].ap()[:, :], x1[0], waits=[tok_res1[0]]))
        tok_out.append(S.dma("sync", dbg_outs["d_h2"].ap()[:, :], xch[0], waits=[t_h2[0]]))
        tok_out.append(S.dma("sync", dbg_outs["d_scal"].ap()[:, :], s11[:, :],
                             waits=[t_dq2, t_sc5_, t_ws5]))

    S.add("sync", lambda e: e.nop(), waits=tok_out, sem="dI", inc=1)

    with es:
        sems = {nm: es.enter_context(nc.semaphore(nm))
                for nm in sorted(set(list(S.cnt.keys()) + ["dI"]))}
        block = es.enter_context(nc.Block())
        S.emit(block, sems)
    return nc


def kernel(L0_attn_x, L0_attn_ln_w, L0_attn_ln_b, L0_attn_Wi, L0_attn_bi,
           L0_attn_Wf, L0_attn_bf, L0_attn_Wz, L0_attn_bz, L0_attn_Wo,
           L0_attn_bo, L0_attn_Wp, L0_attn_bp, L0_ffn_weight, L0_ffn_bias,
           L0_c, L0_wmod):
    f = lambda a: np.ascontiguousarray(np.asarray(a, dtype=np.float32))
    x = f(L0_attn_x)
    wall = np.concatenate([f(L0_attn_Wi), f(L0_attn_Wf), f(L0_attn_Wz), f(L0_attn_Wo)], 0)
    wallT = np.ascontiguousarray(
        wall.T.reshape(8, 128, 256).transpose(1, 0, 2).reshape(128, 2048))
    bvec = np.concatenate([f(L0_attn_bi), f(L0_attn_bf), f(L0_attn_bz), f(L0_attn_bo)], 0)
    bvec2 = np.ascontiguousarray(bvec.reshape(2, 128).T)
    wpT = np.ascontiguousarray(f(L0_attn_Wp).T)
    bprow = f(L0_attn_bp).reshape(1, D)
    wfT = np.ascontiguousarray(f(L0_ffn_weight).T)
    vcat = np.ascontiguousarray(np.concatenate(
        [f(L0_attn_ln_w).reshape(8, 128).T, f(L0_attn_ln_b).reshape(8, 128).T,
         f(L0_ffn_bias).reshape(8, 128).T], 1))
    cmat = f(L0_c)
    wmod = f(L0_wmod)
    eye = np.eye(128, dtype=np.float32)
    eyebf = np.eye(128, dtype=ml_dtypes.bfloat16)
    onesbf = np.ones((1, T), dtype=ml_dtypes.bfloat16)

    nc = build_program(dbg=_DBG)
    in_maps = [{
        "xT": np.ascontiguousarray(x[b].T),
        "wallT": wallT, "bvec2": bvec2, "wpT": wpT, "bprow": bprow,
        "wfT": wfT, "vcat": vcat, "cmat": cmat,
        "wmod_sl": np.ascontiguousarray(wmod[:, b * JW:(b + 1) * JW]),
        "eye": eye, "eyebf": eyebf, "onesbf": onesbf,
    } for b in range(B)]
    res = run_bass_kernel_spmd(nc, in_maps, core_ids=list(range(B)))
    out = np.stack([np.asarray(res.results[b]["outT"]).T for b in range(B)], 0)
    if _DBG:
        kernel._dbg = [res.results[b] for b in range(B)]
    return out
